# version 02
# baseline (speedup 1.0000x reference)
import math
import numpy as np
import ml_dtypes
from contextlib import ExitStack

import concourse.bass as bass
import concourse.mybir as mybir
from concourse.bass_utils import run_bass_kernel_spmd
from concourse.alu_op_type import AluOpType as OP

F32 = mybir.dt.float32
BF16 = mybir.dt.bfloat16
I32 = mybir.dt.int32
AF = mybir.ActivationFunctionType

D = 1024
T = 2048
NB = D // 128
TT = 512
NTT = T // TT
PI = math.pi
BF = ml_dtypes.bfloat16

_DEBUG = False
_CONSTS = [1e-05, PI / 2, 1.0, 0.25, -0.5, 2 * PI, -2 * PI, 0.5, 1.0 / 128.0]
RMAGIC = 12582912.0
_NPHASE = 4
_L0_STOP = ""

ENGS = ("t", "v", "s", "g", "y")


AUTO_SYNC = ("v", "s")


class Sched:

    def __init__(self):
        self.q = {e: [] for e in ENGS}
        self.cnt = {k: 0 for k in ("t", "v", "s", "g", "dg", "dy", "da")}
        self.wm = {e: dict.fromkeys(self.cnt, 0) for e in ENGS}
        self.pending_bar = {"dg": 0, "dy": 0, "da": 0}

    def raw(self, e, fn):
        if e in AUTO_SYNC:
            self.tok(e, fn)
            return None
        self.q[e].append(("op", fn, None, 0))
        return None

    def tok(self, e, fn=None, key=None):
        key = key or e
        if e in AUTO_SYNC and key == e:
            self.wait(e, (e, self.cnt[e]))
        self.cnt[key] += 1
        self.q[e].append(("op", fn, key, 1))
        return (key, self.cnt[key])

    def dma(self, e, out, in_, sem=None, **kw):
        key = sem or ("dg" if e == "g" else "dy")
        bar = self.pending_bar[key]
        if bar > self.wm[e][key]:
            self.wm[e][key] = bar
            self.q[e].append(("wait", key, bar))
        self.cnt[key] += 16

        def fn(h, out=out, in_=in_, kw=kw):
            return h.dma_start(out=out, in_=in_, **kw)

        self.q[e].append(("op", fn, key, 16))
        return (key, self.cnt[key])

    def wait(self, e, tok):
        if tok is None:
            return
        key, c = tok
        if key in ("dg", "dy", "da"):
            c = self.cnt[key]
            self.pending_bar[key] = max(self.pending_bar[key], c)
        if self.wm[e][key] >= c:
            return
        self.wm[e][key] = c
        self.q[e].append(("wait", key, c))

    def now(self, key):
        return (key, self.cnt[key])

    def sync_self(self, e):
        t = self.tok(e)
        self.wait(e, t)


def build_core_program():
    nc = bass.Bass("TRN2", target_bir_lowering=False)
    S = Sched()
    dram = {}

    def din(name, shape, dt=F32):
        dram[name] = nc.dram_tensor(name, list(shape), dt, kind="ExternalInput").ap()
        return dram[name]

    def dout(name, shape, dt=F32):
        dram[name] = nc.dram_tensor(name, list(shape), dt, kind="ExternalOutput").ap()
        return dram[name]

    x_in = din("x_fm", [D, T])
    out_fm = dout("out_fm", [D, T])

    for w in ("Wq", "Wk", "Wv", "Wo"):
        din(w, [D, D])
    din("rf", [64, 128])
    din("rfpad", [128, 256])
    din("r0w", [D, 14])
    din("l0w1", [14, D, 2048])
    din("l0b1", [14, 2048])
    din("l0b1r", [14, 128, 16])
    din("l0w2", [14, 2048, D])
    din("l0b2", [14, D])
    for m in ("m1", "m2"):
        din(m + "_inwT", [D, 4096])
        din(m + "_xprojT", [2048, 96])
        din(m + "_dtwT", [64, 2048])
        din(m + "_outwT", [2048, D])
        din(m + "_convw", [2048, 4])
        din(m + "_convb", [2048])
        din(m + "_dtb", [2048])
        din(m + "_Alog", [2048, 16])
        din(m + "_D", [2048])
    for v in ("norm_g", "norm_b", "ffn_g", "ffn_b"):
        din(v, [D])
    din("ff1T", [D, 4096])
    din("ff1b", [4096])
    din("ff2T", [4096, D])
    din("ff2b", [D])
    din("Wr", [D, 8])
    din("W1", [8, D, 4096])
    din("W2", [8, 4096, D])
    din("ones128", [128, 128])
    din("nrmpack", [128, 32])
    din("ffbpack", [128, 40])
    for m in ("m1", "m2"):
        din(m + "_colpack", [128, 112])
        din(m + "_Alogr", [128, 256])
        din(m + "_dtw128", [128, 2048])
    din("sel128", [128, 2048])
    din("cvals", [128, 16])
    din("onescol", [128, 1])
    din("eye128", [128, 128])
    din("eye16", [16, 16])
    dram["x2_spill"] = nc.dram_tensor("x2_spill", [D, T], F32, kind="Internal").ap()

    dbg_names = ("x1", "x2", "x3") if _DEBUG else ()
    for name in dbg_names:
        dout("dbg_" + name, [D, T])

    es = ExitStack()
    es.__enter__()

    def sbuf(name, shape, dt=F32):
        return es.enter_context(nc.sbuf_tensor(name, list(shape), dt))

    XA = sbuf("XA", [128, NB * T])
    XLNB = sbuf("XLNB", [128, NB * T], BF16)
    GB = sbuf("GB", [128, NB * T], BF16)
    AUX = sbuf("AUX", [128, 24576], BF16)

    ONESR = sbuf("ONESR", [128, 128])
    ONESCB = sbuf("ONESCB", [128, 1], BF16)
    ROWS = sbuf("ROWS", [1, 2 * T + 1024])
    STG = [sbuf(f"STG{i}", [128, TT]) for i in range(2)]
    STGB = [sbuf(f"STGB{i}", [128, TT], BF16) for i in range(2)]
    RFP = sbuf("RFP", [128, 256], BF16)
    CVALS = sbuf("CVALS", [128, 16])
    CVALSB = sbuf("CVALSB", [128, 16], BF16)
    SELW = sbuf("SELW", [16, T], BF16)
    EYEB = sbuf("EYEB", [16, 16], BF16)
    B1E = sbuf("B1E", [128, 16], BF16)
    XSH = AUX[:, 19968:20016]
    BCR = AUX[0:16, 20016:20528]
    DA = [AUX[:, 20528 + 256 * i:20784 + 256 * i] for i in range(2)]
    DB = AUX[:, 21040:21296]
    EYE128B = AUX[:, 21296:21424]
    FFB = AUX[:, 21424:21464]
    HH = AUX[:, 21504:22016].bitcast(F32)
    CARR = AUX[:, 22016:22528].bitcast(F32)
    NEGA = AUX[:, 22528:23040].bitcast(F32)
    MCOL = AUX[:, 23040:23264].bitcast(F32)
    NRM = AUX[:, 23264:23328].bitcast(F32)

    PS = [es.enter_context(nc.psum_tensor(f"PS{i}", [128, 512], F32)) for i in range(8)]

    sems = {k: es.enter_context(nc.semaphore("sem_" + k)) for k in S.cnt}

    def XT(i):
        return XA[:, T * i:T * (i + 1)]

    def XLN(i):
        return XLNB[:, T * i:T * (i + 1)]

    def ROW(r, sl=None):
        if sl is None:
            sl = slice(0, T)
        return ROWS[0:1, T * r + sl.start: T * r + sl.stop]

    def RT1(tt):
        return ROWS[0:1, 2 * T + TT * (tt % 2): 2 * T + TT * (tt % 2 + 1)]

    def act(out, in_, func, bias=0.0, scale=1.0, accum=None):
        def fn(h, o=out, i=in_, f=func, b=bias, sc=scale, a=accum):
            if a is None:
                return h.activation(out=o, in_=i, func=f, bias=b, scale=sc)
            return h.activation(out=o, in_=i, func=f, bias=b, scale=sc, accum_out=a)
        return fn

    def vtt(out, a, b, op):
        return lambda h, o=out, x=a, y=b, p=op: h.tensor_tensor(out=o, in0=x, in1=y, op=p)

    def vcopy(out, in_):
        return lambda h, o=out, i=in_: h.tensor_copy(out=o, in_=i)

    def vts(out, a, s1, s2=None, op0=OP.mult, op1=None):
        def fn(h, o=out, x=a, u=s1, w=s2, p0=op0, p1=op1):
            if p1 is None:
                return h.tensor_scalar(out=o, in0=x, scalar1=u, scalar2=None, op0=p0)
            return h.tensor_scalar(out=o, in0=x, scalar1=u, scalar2=w, op0=p0, op1=p1)
        return fn

    def vstt(out, a, s, b, op0, op1):
        return lambda h, o=out, x=a, u=s, y=b, p0=op0, p1=op1: h.scalar_tensor_tensor(
            out=o, in0=x, scalar=u, in1=y, op0=p0, op1=p1)

    def mm(psum_ap, lhsT, rhs, start, stop):
        return lambda h, o=psum_ap, l=lhsT, r=rhs, a=start, b=stop: h.matmul(
            o, l, r, start=a, stop=b)

    def load_w(dst, src, K=128):
        R, C = src.shape
        k = R // 128
        return S.dma("g", dst.rearrange("p (k c) -> p k c", k=k),
                     src.rearrange("(k p) c -> p k c", p=128))

    t_cval = S.dma("y", CVALS[:, :], dram["cvals"])
    t_cvalb = S.dma("g", CVALSB[:, :], dram["cvals"])
    for j, cv in enumerate(_CONSTS):
        nc.const_aps.register(F32, cv, CVALS[:, j:j + 1])
        nc.const_aps.register(BF16, cv, CVALSB[:, j:j + 1])
    for e in ENGS:
        S.wait(e, t_cval)
        S.wait(e, t_cvalb)
    t_ones = S.dma("y", ONESR[:, :], dram["ones128"])
    S.dma("g", ONESCB[:, :], dram["onescol"])
    t_rf = S.dma("g", RFP[:, :], dram["rfpad"])

    t_x = S.dma("y", XA.rearrange("p (i t) -> p i t", i=NB),
                x_in.rearrange("(i p) t -> p i t", p=128))
    XB0 = AUX[:, 0:NB * T]
    t_xb0 = S.dma("g", XB0.rearrange("p (i t) -> p i t", i=NB),
                  x_in.rearrange("(i p) t -> p i t", p=128))

    def ln_stats(xb_view, r_mu, r_inv, dep_tok, pa, pb):
        assert (r_mu, r_inv) == (0, 1)
        tq = None
        prev_mm = [None, None]
        for tt in range(NTT):
            sl = slice(TT * tt, TT * (tt + 1))
            for i in range(NB):
                S.wait("v", dep_tok)
                S.wait("v", prev_mm[i % 2])
                tk = S.tok("v", vtt(STGB[i % 2][:, :], xb_view(i)[:, sl],
                                    xb_view(i)[:, sl], OP.mult))
                S.wait("t", tk)
                S.raw("t", mm(PS[pa][0:1, :], ONESCB[:, :], xb_view(i)[:, sl],
                              i == 0, i == NB - 1))
                tq = S.tok("t", mm(PS[pb][0:1, :], ONESCB[:, :], STGB[i % 2][:, :],
                                   i == 0, i == NB - 1))
                prev_mm[i % 2] = tq
            S.wait("v", tq)
            S.raw("v", vts(ROW(0, sl), PS[pa][0:1, :], 1.0 / D))
            tv = S.tok("v", vts(ROW(1, sl), PS[pb][0:1, :], 1.0 / D))
        S.wait("v", tv)
        last = None
        for p2 in range(NTT // 2):
            c0, c1 = 2 * p2, 2 * p2 + 1
            sl0 = slice(TT * c0, TT * (c0 + 1))
            sl1 = slice(TT * c1, TT * (c1 + 1))
            if last is not None:
                S.wait("v", last)
            S.raw("v", vtt(RT1(c0), ROW(0, sl0), ROW(0, sl0), OP.mult))
            S.raw("v", vtt(RT1(c1), ROW(0, sl1), ROW(0, sl1), OP.mult))
            S.raw("v", vtt(STG[0][0:1, :], ROW(1, sl0), RT1(c0), OP.subtract))
            tb = S.tok("v", vtt(STG[1][0:1, :], ROW(1, sl1), RT1(c1), OP.subtract))
            S.wait("s", tb)
            S.raw("s", act(RT1(c0), STG[0][0:1, :], AF.Ln, bias=1e-5))
            S.raw("s", act(RT1(c1), STG[1][0:1, :], AF.Ln, bias=1e-5))
            S.raw("s", act(ROW(1, sl0), RT1(c0), AF.Exp, scale=-0.5))
            last = S.tok("s", act(ROW(1, sl1), RT1(c1), AF.Exp, scale=-0.5))
        return last

    def bcast_row(r, sl, bank, tok_dep):
        S.wait("t", tok_dep)
        S.wait("t", t_ones)
        return S.tok("t", mm(PS[bank][:, :], ONESR[0:1, :], ROW(r, sl), True, True))

    def ln_apply(xb_view, out_view, r_mu, r_inv, stat_tok, pa, pb):
        last = None
        te = None
        for tt in range(NTT):
            sl = slice(TT * tt, TT * (tt + 1))
            S.wait("t", te)
            tb1 = bcast_row(r_mu, sl, pa, stat_tok)
            tb2 = bcast_row(r_inv, sl, pb, stat_tok)
            S.wait("s", tb2)
            if last is not None:
                S.wait("s", last)
            S.raw("s", act(STG[0][:, :], PS[pa][:, :], AF.Copy))
            te = S.tok("s", act(STG[1][:, :], PS[pb][:, :], AF.Copy))
            for i in range(NB):
                S.wait("v", te)
                S.raw("v", vtt(STGB[i % 2][:, :], xb_view(i)[:, sl], STG[0][:, :], OP.subtract))
                if i > 0:
                    last = S.tok("v", vtt(out_view(i - 1)[:, sl], STGB[(i - 1) % 2][:, :],
                                          STG[1][:, :], OP.mult))
            last = S.tok("v", vtt(out_view(NB - 1)[:, sl], STGB[(NB - 1) % 2][:, :],
                                  STG[1][:, :], OP.mult))
        return last

    def fresh_xb(dst_region, dep_tok):
        S.wait("g", dep_tok)
        return S.dma("g", dst_region.rearrange("p (i t) -> p i t", i=NB),
                     XA.rearrange("p (i t) -> p i t", i=NB))

    def xb0_view(i):
        return XB0[:, T * i:T * (i + 1)]

    st1 = ln_stats(xb0_view, 0, 1, t_xb0, 0, 1)
    t_xln = ln_apply(xb0_view, XLN, 0, 1, st1, 2, 3)

    WK = AUX[:, 0:8192]
    WV = AUX[:, 8192:16384]
    WQ = AUX[:, 16384:24576]
    WO = AUX[:, 0:8192]
    S.wait("g", t_xln)
    t_wk = load_w(WK, dram["Wk"])
    t_wv = load_w(WV, dram["Wv"])
    t_wq = load_w(WQ, dram["Wq"])
    t_wo = None

    def wsl(arena, kb, ob):
        return arena[:, kb * D + 128 * ob: kb * D + 128 * (ob + 1)]

    KC = GB[:, 0:4096]
    VTC = GB[:, 4096:8192]
    KPT = GB[:, 8192:10240]
    KVX = GB[:, 14336:15488]
    QC = GB[:, 0:4096]
    QPG = GB[:, 8192:10240]
    ATTC = GB[:, 10240:14336]

    if _NPHASE >= 1:
        def trig_block(ps_bank, n_grp, out_ap_fn, part_halves=False):
            S.wait("v", S.now("s"))
            parts = []
            for half, (k_off, s_bias) in enumerate([(0.25, PI / 2), (0.0, 0.0)]):
                if part_halves:
                    src = ps_bank[64 * half:64 * (half + 1), 0:512]
                    w0 = STG[0][64 * half:64 * (half + 1), 0:512]
                    w1 = STG[1][64 * half:64 * (half + 1), 0:512]
                else:
                    src = ps_bank.rearrange("p (g f) -> p g f", f=128)[:, 0:n_grp, 64 * half:64 * (half + 1)]
                    w0 = STG[0].rearrange("p (g f) -> p g f", f=128)[:, 0:n_grp, 64 * half:64 * (half + 1)]
                    w1 = STG[1].rearrange("p (g f) -> p g f", f=128)[:, 0:n_grp, 64 * half:64 * (half + 1)]
                parts.append((src, w0, w1, out_ap_fn(half), s_bias, k_off))
            tr = None
            for src, w0, w1, oo, s_bias, k_off in parts:
                S.raw("v", vts(w0, src, 1.0 / (2 * PI), k_off, OP.mult, OP.add))
                S.raw("v", vts(w1, w0, RMAGIC, -RMAGIC, OP.add, OP.add))
                tr = S.tok("v", vstt(w0, w1, -2 * PI, src, OP.mult, OP.add))
            S.wait("s", tr)
            tok = None
            for src, w0, w1, oo, s_bias, k_off in parts:
                tok = S.tok("s", act(oo, w0, AF.Sin, bias=s_bias))
            return tok

        for tt in range(NTT):
            sl = slice(TT * tt, TT * (tt + 1))
            tke = None
            for ob in ([] if _L0_STOP in ("VnoK", "Vswap") else range(NB)):
                S.wait("t", t_wk)
                S.wait("t", t_xln)
                if tke is not None:
                    S.wait("t", tke)
                for kb in range(NB):
                    tk = S.tok("t", mm(PS[ob % 2][:, :], wsl(WK, kb, ob), XLN(kb)[:, sl],
                                       kb == 0, kb == NB - 1))
                S.wait("s", tk)
                tke = S.tok("s", act(KC[:, 512 * ob:512 * (ob + 1)], PS[ob % 2][:, :], AF.Copy))
            if _L0_STOP == "K":
                continue
            for sub in range(4):
                ssl = slice(TT * tt + 128 * sub, TT * tt + 128 * (sub + 1))
                S.wait("t", t_wv)
                for vh in range(2):
                    for kb in range(NB):
                        lhs_vt = wsl(WV, kb, 0) if _L0_STOP == "Vswap" else XLN(kb)[:, ssl]
                        tv = S.tok("t", mm(PS[2 + vh][:, :], lhs_vt,
                                           WV[:, kb * D + 512 * vh: kb * D + 512 * (vh + 1)],
                                           kb == 0, kb == NB - 1))
                S.wait("s", tv)
                if _L0_STOP == "Vnoevac":
                    tve = S.tok("s")
                else:
                    S.raw("s", act(VTC[:, 1024 * sub:1024 * sub + 512], PS[2][:, :], AF.Copy))
                    tve = S.tok("s", act(VTC[:, 1024 * sub + 512:1024 * (sub + 1)], PS[3][:, :], AF.Copy))
                if _L0_STOP in ("V", "Vnoevac", "VnoK", "Vswap"):
                    continue
                S.wait("t", tke)
                S.wait("t", t_rf)
                first_cs = (tt == 0 and sub == 0)
                last_cs = (tt == NTT - 1 and sub == 3)
                for g4 in range(4):
                    for j in range(4):
                        h = 4 * g4 + j
                        ob, half = h // 2, h % 2
                        lhs = KC[:, 512 * ob + 128 * sub: 512 * ob + 128 * (sub + 1)]
                        rf_t = RFP[:, 128 * half:128 * (half + 1)]
                        tp = S.tok("t", mm(PS[4][:, 128 * j:128 * (j + 1)], lhs, rf_t, True, True))
                    S.wait("v", tp)
                    tsin = trig_block(PS[4], 4,
                                      lambda half, g4=g4: KPT.rearrange("p (g f) -> p g f", f=128)
                                      [:, 4 * g4:4 * g4 + 4, 64 * half:64 * (half + 1)])
                    if _L0_STOP == "P":
                        continue
                    S.wait("t", tsin)
                    S.wait("t", tve)
                    for j in range(4):
                        h = 4 * g4 + j
                        bank, off = 5 + h // 7, 65 * (h % 7)
                        first = first_cs and (h % 7 == 0)
                        last = last_cs and (h in (6, 13, 15))
                        lhs = KPT[:, 128 * h:128 * (h + 1)]
                        S.raw("t", mm(PS[bank][:, off:off + 64], lhs,
                                      VTC[:, 1024 * sub + 64 * h: 1024 * sub + 64 * (h + 1)],
                                      first, False))
                        tkv = S.tok("t", mm(PS[bank][:, off + 64:off + 65], lhs,
                                            ONESCB[:, :], False, last))
        S.wait("g", S.now("t"))
        t_wo = load_w(WO, dram["Wo"])
        tkvx = None
        if _L0_STOP in ("", "B"):
            S.wait("s", tkv)
            for b, (lo, n) in enumerate([(0, 7), (7, 7), (14, 2)]):
                tkvx = S.tok("s", act(KVX[:, 65 * lo:65 * (lo + n)], PS[5 + b][:, 0:65 * n],
                                      AF.Copy, scale=1.0 / 128.0))

        if _L0_STOP == "B":
            globals()["_B_ONLY_TOK"] = tkvx
        if _L0_STOP == "":
            tx1 = None
            for tt in range(NTT):
                sl = slice(TT * tt, TT * (tt + 1))
                tqe = None
                for ob in range(NB):
                    S.wait("t", t_wq)
                    S.wait("t", tkvx)
                    S.wait("t", tx1)
                    if tqe is not None:
                        S.wait("t", tqe)
                    for kb in range(NB):
                        tk = S.tok("t", mm(PS[ob % 2][:, :], wsl(WQ, kb, ob), XLN(kb)[:, sl],
                                           kb == 0, kb == NB - 1))
                    S.wait("s", tk)
                    tqe = S.tok("s", act(QC[:, 512 * ob:512 * (ob + 1)], PS[ob % 2][:, :], AF.Copy))
                tae = None
                tx1_hist = []
                for g4 in range(4):
                    S.wait("t", tqe)
                    for j in range(4):
                        h = 4 * g4 + j
                        ob, half = h // 2, h % 2
                        rhs = QC[:, 512 * ob:512 * (ob + 1)]
                        rf_t = RFP[:, 128 * half:128 * (half + 1)]
                        tp = S.tok("t", mm(PS[2 + j][:, :], rf_t, rhs, True, True))
                    S.wait("v", tp)
                    tqp = None
                    for j in range(4):
                        tqp = trig_block(PS[2 + j], 1,
                                         lambda half, j=j: QPG[64 * half:64 * (half + 1),
                                                              512 * j:512 * (j + 1)],
                                         part_halves=True)
                    for j in range(4):
                        h = 4 * g4 + j
                        S.wait("t", tqp)
                        bank = 6 + (j % 2)
                        if tae is not None:
                            S.wait("t", tae)
                        tq = S.tok("t", mm(PS[bank][0:65, :], KVX[:, 65 * h:65 * (h + 1)],
                                           QPG[:, 512 * j:512 * (j + 1)], True, True))
                        S.wait("v", tq)
                        S.raw("v", vts(STG[0][64:65, :], PS[bank][64:65, :], 1e-6, None, OP.max))
                        S.sync_self("v")
                        tz = S.tok("v", lambda h_, b=bank: h_.reciprocal(
                            out=STG[1][64:65, :], in_=STG[0][64:65, :]))
                        S.wait("t", tz)
                        tzb = S.tok("t", mm(PS[j % 2][0:64, :], ONESR[64:65, 0:64],
                                            STG[1][64:65, :], True, True))
                        S.wait("s", tq)
                        S.wait("s", tz)
                        tse = S.tok("s", act(STGB[0][0:64, :], PS[bank][0:64, :], AF.Copy))
                        S.wait("v", tse)
                        S.wait("v", tzb)
                        ob, half = h // 2, h % 2
                        tae = S.tok("v", vtt(ATTC[64 * half:64 * (half + 1), 512 * ob:512 * (ob + 1)],
                                             STGB[0][0:64, :], PS[j % 2][0:64, :], OP.mult))
                for ob in range(NB):
                    S.wait("t", tae)
                    S.wait("t", t_wo)
                    if ob >= 2:
                        S.wait("t", tx1_hist[ob - 2])
                    for kb in range(NB):
                        tk = S.tok("t", mm(PS[2 + ob % 2][:, :], wsl(WO, kb, ob),
                                           ATTC[:, 512 * kb:512 * (kb + 1)], kb == 0, kb == NB - 1))
                    S.wait("v", tk)
                    S.wait("v", t_x)
                    tx1 = S.tok("v", vtt(XT(ob)[:, sl], XT(ob)[:, sl], PS[2 + ob % 2][:, :], OP.add))
                    tx1_hist.append(tx1)

            t_last = tx1
        if _L0_STOP != "":
            t_last = t_xln
    if _NPHASE == 0:
        t_last = t_xln

    if _NPHASE >= 2 and _L0_STOP == "":
        t_in = t_last

        t_eyeb = S.dma("g", EYEB[:, :], dram["eye16"])
        XB2 = GB[:, 0:NB * T]

        def xb2_view(i):
            return XB2[:, T * i:T * (i + 1)]

        S.wait("g", t_in)
        t_xb2 = S.dma("g", XB2.rearrange("p (i t) -> p i t", i=NB),
                      XA.rearrange("p (i t) -> p i t", i=NB))
        st2 = ln_stats(xb2_view, 0, 1, t_xb2, 0, 1)
        t_xln2 = ln_apply(xb2_view, XLN, 0, 1, st2, 2, 3)

        HR = GB[:, 8192:10240].bitcast(F32)
        R0WS = HR[:, 0:112]
        EYE = HR[:, 112:240]
        LTM = HR[:, 240:304]
        LTZ = HR[:, 304:368]
        RCOL = HR[:, 368:376]
        REXP = HR[:, 376:392]
        RSBN = HR[:, 392:406]
        GST = HR[:, 406:918]
        RRS = HR[0:1, 918:934]
        B2SV = GB[0:16, 0:1024]
        S.wait("y", t_xln2)
        t_r0 = S.dma("y", R0WS.rearrange("p (k c) -> p k c", k=NB),
                     dram["r0w"].rearrange("(k p) c -> p k c", p=128))
        t_eye = S.dma("y", EYE, dram["eye128"])
        S.wait("g", t_xln2)
        t_b2 = S.dma("g", B2SV[0:14, :], dram["l0b2"])
        t_b1all = None

        S.wait("t", t_r0)
        S.wait("t", S.now("v"))
        for kb in range(NB):
            trs = S.tok("t", mm(PS[5][0:14, 0:1], R0WS[:, 14 * kb:14 * (kb + 1)],
                                ONESR[:, 0:1], kb == 0, kb == NB - 1))
        S.wait("t", t_eye)
        S.wait("v", trs)
        S.raw("v", vcopy(GST[0:14, 0:1], PS[5][0:14, 0:1]))
        tev = S.tok("v", vcopy(GST[0:14, 1:15], EYE[0:14, 0:14]))
        S.wait("t", tev)
        trr = S.tok("t", mm(PS[6][0:1, 0:14], GST[0:14, 0:1], GST[0:14, 1:15], True, True))
        S.wait("v", trr)
        tn = S.tok("v", vts(RRS[0:1, 0:14], PS[6][0:1, 0:14], -1.0))
        S.wait("t", tn)
        trsb = S.tok("t", mm(PS[6][:, 0:14], ONESR[0:1, :], RRS[0:1, 0:14], True, True))
        S.wait("v", trsb)
        tn2 = S.tok("v", vcopy(RSBN, PS[6][:, 0:14]))

        t_selw = None
        memset_tok = S.tok("v", lambda h: h.memset(SELW[:, :], 0.0))
        for tt in range(NTT):
            sl = slice(TT * tt, TT * (tt + 1))
            S.wait("t", S.now("v"))
            for kb in range(NB):
                tg = S.tok("t", mm(PS[5][0:14, :], R0WS[:, 14 * kb:14 * (kb + 1)],
                                   XT(kb)[:, sl], kb == 0, kb == NB - 1))
            S.wait("v", tg)
            tgs = S.tok("v", vcopy(GST[0:14, :], PS[5][0:14, :]))
            S.wait("t", tgs)
            S.wait("t", st2)
            for tc in range(4):
                csl = slice(128 * tc, 128 * (tc + 1))
                S.raw("t", mm(PS[7][:, 16 * tc:16 * tc + 14],
                              GST[0:14, csl], EYE[0:14, 0:14], tc == 0, False))
                S.raw("t", mm(PS[7][:, 16 * tc + 14:16 * tc + 15],
                              ROW(0, slice(sl.start + csl.start, sl.start + csl.stop)),
                              EYE[0:1, 0:1], False, False))
                ttr = S.tok("t", mm(PS[7][:, 16 * tc + 15:16 * tc + 16],
                                    ROW(1, slice(sl.start + csl.start, sl.start + csl.stop)),
                                    EYE[0:1, 0:1], False, tc == 3))
            S.wait("v", ttr)
            S.wait("v", tn2)
            tswt = None
            for tc in range(4):
                o = 16 * tc
                S.wait("v", tswt)
                S.raw("v", vstt(LTM[:, 0:14], RSBN, PS[7][:, o + 14:o + 15],
                                PS[7][:, o:o + 14], OP.mult, OP.add))
                S.raw("v", vts(LTZ[:, 0:14], LTM[:, 0:14], PS[7][:, o + 15:o + 16], None, OP.mult))
                S.raw("v", lambda h, z=LTZ: h.tensor_reduce(
                    out=RCOL[:, 0:1], in_=z[:, 0:14], axis=mybir.AxisListType.X, op=OP.max))
                S.raw("v", vts(RCOL[:, 1:2], RCOL[:, 0:1], -1.0))
                tmask = S.tok("v", vts(LTM[:, 0:14], LTZ[:, 0:14], RCOL[:, 0:1], None, OP.is_ge))
                S.wait("s", tmask)
                texp = S.tok("s", act(REXP[:, 0:14], LTZ[:, 0:14], AF.Exp, bias=RCOL[:, 1:2]))
                S.wait("v", texp)
                S.raw("v", lambda h: h.tensor_reduce(
                    out=RCOL[:, 2:3], in_=REXP[:, 0:14], axis=mybir.AxisListType.X, op=OP.add))
                S.raw("v", lambda h: h.reciprocal(out=RCOL[:, 3:4], in_=RCOL[:, 2:3]))
                tswm = S.tok("v", vts(LTZ[:, 0:14], LTM[:, 0:14], RCOL[:, 3:4], None, OP.mult))
                S.wait("t", tswm)
                tswt = S.tok("t", mm(PS[4][0:14, 128 * tc:128 * (tc + 1)],
                                     LTZ[:, 0:14], EYE[:, :], tc == 0, tc == 3))
            S.wait("v", tswt)
            S.wait("v", memset_tok)
            t_selw = S.tok("v", vcopy(SELW[0:14, sl], PS[4][0:14, :]))

        S.wait("g", t_xln2)
        t_onesrow = S.dma("g", STGB[1][0:1, 0:128], dram["ones128"][0:1, :])

        for tt in range(NTT):
            sl = slice(TT * tt, TT * (tt + 1))
            for ob in range(NB):
                S.wait("t", t_selw)
                S.wait("t", t_b2)
                tb2m = S.tok("t", mm(PS[ob % 2][:, :], B2SV[0:14, 128 * ob:128 * (ob + 1)],
                                     SELW[0:14, sl], True, True))
                S.wait("v", tb2m)
                tb2a = S.tok("v", vtt(XT(ob)[:, sl], XT(ob)[:, sl], PS[ob % 2][:, :], OP.add))
                S.wait("t", tb2a)

        W1S = AUX[:, 0:16384]
        W2A = AUX[:, 16384:24576]
        W2B = GB[:, 0:8192]
        HIDR = GB[:, 8192:16384]

        def w2sl(fb, ob):
            base = W2A if fb < 8 else W2B
            f = fb if fb < 8 else fb - 8
            return base[:, 1024 * f + 128 * ob: 1024 * f + 128 * (ob + 1)]

        t_w1 = t_w2 = None
        last_w1_use = last_w2_use = None
        last_hid_evac = None
        tb2a_tok = tb2a
        for e in range(14):
            S.wait("g", last_w1_use)
            t_w1 = S.dma("g", W1S.rearrange("p (k c) -> p k c", k=NB),
                         dram["l0w1"][e].rearrange("(k p) c -> p k c", p=128))
            t_b1e = S.dma("g", B1E[:, :], dram["l0b1r"][e])
            S.wait("g", last_w2_use)
            if e == 0:
                S.wait("g", tb2a_tok)
                S.wait("g", t_selw)
            t_w2 = S.dma("g", W2A.rearrange("p (k c) -> p k c", k=8),
                         dram["l0w2"][e][0:1024].rearrange("(k p) c -> p k c", p=128))
            t_w2b = S.dma("g", W2B.rearrange("p (k c) -> p k c", k=8),
                          dram["l0w2"][e][1024:2048].rearrange("(k p) c -> p k c", p=128))
            for tt in range(NTT):
                sl = slice(TT * tt, TT * (tt + 1))
                S.wait("t", t_selw)
                S.wait("t", t_eyeb)
                trx = S.tok("t", mm(PS[5][0:1, :], EYEB[0:14, e:e + 1], SELW[0:14, sl],
                                    True, True))
                S.wait("s", trx)
                tsw = S.tok("s", act(STGB[0][0:1, :], PS[5][0:1, :], AF.Copy))
                S.wait("t", tsw)
                S.wait("t", t_onesrow)
                tswb = S.tok("t", mm(PS[4][:, :], STGB[1][0:1, 0:128],
                                     STGB[0][0:1, :], True, True))
                for fb in range(16):
                    S.wait("t", t_w1)
                    if last_hid_evac is not None:
                        S.wait("t", last_hid_evac)
                    for kb in range(NB):
                        th = S.tok("t", mm(PS[fb % 2][:, :],
                                           W1S[:, 2048 * kb + 128 * fb: 2048 * kb + 128 * (fb + 1)],
                                           XLN(kb)[:, sl], kb == 0, kb == NB - 1))
                    S.wait("s", th)
                    S.wait("s", t_b1e)
                    S.wait("s", t_selw)
                    last_hid_evac = S.tok("s", act(HIDR[:, 512 * fb:512 * (fb + 1)],
                                                   PS[fb % 2][:, :], AF.Gelu,
                                                   bias=B1E[:, fb:fb + 1]))
                for ob in range(NB):
                    S.wait("t", t_w2)
                    S.wait("t", t_w2b)
                    S.wait("t", last_hid_evac)
                    for fb in range(16):
                        tw = S.tok("t", mm(PS[2 + ob % 2][:, :], w2sl(fb, ob),
                                           HIDR[:, 512 * fb:512 * (fb + 1)],
                                           fb == 0, fb == 15))
                    S.wait("s", tw)
                    tev2 = S.tok("s", act(STG[0][:, :], PS[2 + ob % 2][:, :], AF.Copy))
                    S.wait("v", tev2)
                    S.wait("v", tswb)
                    S.raw("v", vtt(STG[1][:, :], STG[0][:, :], PS[4][:, :], OP.mult))
                    txa = S.tok("v", vtt(XT(ob)[:, sl], XT(ob)[:, sl], STG[1][:, :], OP.add))
                    S.wait("t", txa)
                last_w1_use = S.now("t")
            last_w2_use = S.now("t")
        t_last = txa

    if _NPHASE >= 3 and _L0_STOP == "":
        t_in = t_last
        CH = 256
        NCH = T // CH

        XB3 = GB[:, 0:NB * T]

        def xb3_view(i):
            return XB3[:, T * i:T * (i + 1)]

        S.wait("g", t_in)
        t_xb3 = S.dma("g", XB3.rearrange("p (i t) -> p i t", i=NB),
                      XA.rearrange("p (i t) -> p i t", i=NB))
        st3 = ln_stats(xb3_view, 0, 1, t_xb3, 0, 1)
        t_xln3 = ln_apply(xb3_view, XLN, 0, 1, st3, 2, 3)

        S.wait("y", t_xln3)
        t_spill = S.dma("y", dram["x2_spill"].rearrange("(i p) t -> p i t", p=128),
                        XA.rearrange("p (i t) -> p i t", i=NB))
        XAB = XA[:, :].bitcast(BF16)

        st_in = ln_stats(XLN, 0, 1, t_xln3, 0, 1)

        t_ey128b = S.dma("g", EYE128B[:, :], dram["eye128"])
        t_bcsel = S.dma("g", SELW[:, :], dram["sel128"][0:16, :])
        t_nrm = S.dma("g", NRM[:, :], dram["nrmpack"])
        mz = S.tok("v", lambda h: h.memset(STGB[1][:, :], 0.0))

        XSR = GB[:, 0:4096]
        XSIL = GB[:, 4096:8192]
        SZR = GB[:, 8192:12288]
        DTH = GB[:, 12288:16384]
        OUT_W = AUX[:, 0:16384]
        XPJ = AUX[:, 16384:17920]
        DTW = AUX[:, 17920:19968]

        def xsr_t(db):
            return XSR[:, 256 * db:256 * (db + 1)]

        def xsil_t(db):
            return XSIL[:, 256 * db:256 * (db + 1)]

        def sz_t(db):
            return SZR[:, 256 * db:256 * (db + 1)]

        def dt_t(db):
            return DTH[:, 256 * db:256 * (db + 1)]

        def g1s_t(kb):
            return DTH[:, 256 * kb:256 * (kb + 1)]

        gsp = nc.dram_tensor("g1_spill", [D, T], BF16, kind="Internal").ap()

        def load_mamba_weights(mi, gate_tok):
            S.wait("g", gate_tok)
            S.wait("g", S.now("v"))
            S.wait("g", S.now("s"))
            S.dma("g", XAB.rearrange("p (k c) -> p k c", k=NB),
                  dram[mi + "_inwT"].rearrange("(k p) c -> p k c", p=128))
            S.dma("g", OUT_W.rearrange("p (k c) -> p k c", k=16),
                  dram[mi + "_outwT"].rearrange("(k p) c -> p k c", p=128))
            S.dma("g", XPJ.rearrange("p (k c) -> p k c", k=16),
                  dram[mi + "_xprojT"].rearrange("(k p) c -> p k c", p=128))
            S.dma("g", DTW[:, :], dram[mi + "_dtw128"])
            S.dma("g", MCOL[:, :], dram[mi + "_colpack"])
            S.wait("y", gate_tok)
            S.wait("y", S.now("v"))
            S.wait("y", S.now("s"))
            t_al = S.dma("y", STG[0][:, 0:256], dram[mi + "_Alogr"])
            S.wait("s", t_al)
            tn1 = S.tok("s", act(STG[1][:, 0:256], STG[0][:, 0:256], AF.Exp))
            S.wait("v", tn1)
            t_nega = S.tok("v", vts(NEGA[:, :], STG[1][:, 0:256], -1.0))
            return S.now("dg"), t_nega

        def emit_mamba(rev, t_w, t_nega):
            t_y_done = None
            t_sp_prev = None
            t_xi1 = None
            mzc = S.tok("v", lambda h: h.memset(CARR[:, :], 0.0))
            mzh = S.tok("v", lambda h: h.memset(XSH[:, :], 0.0))
            for c in range(NCH):
                isl = slice(CH * c, CH * (c + 1))
                osl = slice(T - CH * (c + 1), T - CH * c)
                rsl = osl if rev else isl

                def xsrc(kb):
                    a = XLN(kb)[:, rsl]
                    return a[:, ::-1] if rev else a

                S.wait("t", st_in)
                S.wait("t", t_y_done)
                S.raw("t", mm(PS[3][:, 0:256], ONESR[0:1, :], ROW(0, rsl), True, False))
                tbc = S.tok("t", mm(PS[3][:, 256:512], ONESR[0:1, :], ROW(1, rsl), False, True))
                mu_b = PS[3][:, 0:256][:, ::-1] if rev else PS[3][:, 0:256]
                iv_b = PS[3][:, 256:512][:, ::-1] if rev else PS[3][:, 256:512]
                th = None
                for kb in range(NB):
                    S.wait("v", tbc)
                    S.wait("v", t_xln3)
                    S.wait("v", t_sp_prev)
                    S.wait("v", t_nrm)
                    S.raw("v", vtt(STGB[0][:, 0:256], xsrc(kb), mu_b, OP.subtract))
                    S.raw("v", vtt(STGB[0][:, 0:256], STGB[0][:, 0:256], iv_b, OP.mult))
                    th = S.tok("v", vts(DTH[:, 256 * kb:256 * (kb + 1)], STGB[0][:, 0:256],
                                        NRM[:, kb:kb + 1], NRM[:, 8 + kb:9 + kb],
                                        OP.mult, OP.add))
                tev = {}
                for eb in range(32):
                    S.wait("t", th)
                    S.wait("t", t_w)
                    if eb >= 2:
                        S.wait("t", tev[eb - 2])
                    for kb in range(NB):
                        tm = S.tok("t", mm(PS[eb % 2][:, 0:256],
                                           XAB[:, 4096 * kb + 128 * eb:4096 * kb + 128 * (eb + 1)],
                                           DTH[:, 256 * kb:256 * (kb + 1)],
                                           kb == 0, kb == NB - 1))
                    S.wait("s", tm)
                    if eb < 16:
                        tev[eb] = S.tok("s", act(xsr_t(eb), PS[eb % 2][:, 0:256], AF.Copy))
                    else:
                        tev[eb] = S.tok("s", act(sz_t(eb - 16), PS[eb % 2][:, 0:256], AF.Silu))
                t_conv = None
                for db in range(16):
                    S.wait("v", tev[db])
                    S.wait("v", mzh)
                    S.wait("v", t_conv)
                    S.raw("v", vts(STG[0][:, 0:256], xsr_t(db), MCOL[:, 4 * db + 3:4 * db + 4]))
                    a, b = STG[0], STG[1]
                    for k in (2, 1, 0):
                        off = 3 - k
                        S.raw("v", vstt(b[:, off:256], xsr_t(db)[:, 0:256 - off],
                                        MCOL[:, 4 * db + k:4 * db + k + 1],
                                        a[:, off:256], OP.mult, OP.add))
                        S.raw("v", vstt(b[:, 0:off], XSH[:, 3 * db + 3 - off:3 * db + 3],
                                        MCOL[:, 4 * db + k:4 * db + k + 1],
                                        a[:, 0:off], OP.mult, OP.add))
                        a, b = b, a
                    tcv = S.tok("v", vcopy(XSH[:, 3 * db:3 * db + 3], xsr_t(db)[:, 253:256]))
                    S.wait("s", tcv)
                    t_conv = S.tok("s", act(xsil_t(db), a[:, 0:256], AF.Silu,
                                            bias=MCOL[:, 64 + db:65 + db]))
                S.wait("t", t_conv)
                for kb in range(16):
                    tp = S.tok("t", mm(PS[2][0:96, 0:256], XPJ[:, 96 * kb:96 * (kb + 1)],
                                       xsil_t(kb), kb == 0, kb == 15))
                S.wait("s", tp)
                S.wait("s", mz)
                tdbl = S.tok("s", act(STGB[1][0:96, 0:256], PS[2][0:96, 0:256], AF.Copy))
                t_dtev = {}
                for mb in range(16):
                    S.wait("t", tdbl)
                    if mb >= 2:
                        S.wait("t", t_dtev[mb - 2])
                    tdm = S.tok("t", mm(PS[mb % 2][:, 0:256], DTW[:, 128 * mb:128 * (mb + 1)],
                                        STGB[1][:, 0:256], True, True))
                    S.wait("s", tdm)
                    if mb >= 1:
                        S.wait("s", t_dtev[mb - 1])
                    S.raw("s", act(STG[0][:, 0:256], PS[mb % 2][:, 0:256], AF.Abs,
                                   bias=MCOL[:, 80 + mb:81 + mb]))
                    S.raw("s", act(STG[1][:, 0:256], STG[0][:, 0:256], AF.Exp, scale=-1.0))
                    S.raw("s", act(STG[0][:, 0:256], STG[1][:, 0:256], AF.Ln, bias=1.0))
                    tr1 = S.tok("s", act(STG[1][:, 0:256], PS[mb % 2][:, 0:256], AF.Relu,
                                         bias=MCOL[:, 80 + mb:81 + mb]))
                    S.wait("v", tr1)
                    t_dtev[mb] = S.tok("v", vtt(dt_t(mb), STG[0][:, 0:256],
                                                STG[1][:, 0:256], OP.add))
                t_dt_last = t_dtev[15]
                S.wait("t", t_dt_last)
                S.wait("t", t_ey128b)
                S.raw("t", mm(PS[2][0:16, 0:256], EYE128B[:, 64:80], STGB[1][:, 0:256],
                              True, False))
                trx = S.tok("t", mm(PS[2][0:16, 256:512], EYE128B[:, 80:96],
                                    STGB[1][:, 0:256], False, True))
                S.wait("s", trx)
                tbcr = S.tok("s", act(BCR[0:16, :], PS[2][0:16, :], AF.Copy))
                t_da = [None, None]
                t_scan = [None, None]
                t_hc_done = None
                for sg in range(2):
                    S.wait("t", tbcr)
                    S.wait("t", t_bcsel)
                    S.wait("t", t_hc_done)
                    tsb = None
                    for j in range(8):
                        s_ = 8 * sg + j
                        bank, half = j // 2, j % 2
                        S.raw("t", mm(PS[bank][:, 256 * half:256 * (half + 1)],
                                      SELW[0:16, 128 * s_:128 * (s_ + 1)],
                                      BCR[0:16, 0:256], half == 0, half == 1))
                        tsb = S.tok("t", mm(PS[4 + bank][:, 256 * half:256 * (half + 1)],
                                            SELW[0:16, 128 * s_:128 * (s_ + 1)],
                                            BCR[0:16, 256:512], half == 0, half == 1))
                    for db in range(16):
                        S.wait("v", tsb)
                        S.wait("v", t_dt_last)
                        S.wait("v", t_nega)
                        S.raw("v", vtt(STGB[0][:, 0:256], dt_t(db), xsil_t(db), OP.mult))
                        for j in range(8):
                            s_ = 8 * sg + j
                            bank, half = j // 2, j % 2
                            bb = PS[bank][:, 256 * half:256 * (half + 1)]
                            cb = PS[4 + bank][:, 256 * half:256 * (half + 1)]
                            par = j % 2
                            S.wait("s", t_scan[par])
                            t_da[par] = S.tok("s", act(DA[par][:, 0:256], dt_t(db), AF.Exp,
                                                       scale=NEGA[:, 16 * db + s_:16 * db + s_ + 1]))
                            S.wait("v", t_da[par])
                            S.raw("v", vtt(DB[:, 0:256], STGB[0][:, 0:256], bb, OP.mult))
                            S.wait("v", mzc)
                            S.raw("v", lambda h, p=par, ix=16 * db + s_: h.tensor_tensor_scan(
                                out=HH[:, 0:256], data0=DA[p][:, 0:256], data1=DB[:, 0:256],
                                initial=CARR[:, ix:ix + 1], op0=OP.mult, op1=OP.add))
                            S.raw("v", vcopy(CARR[:, 16 * db + s_:16 * db + s_ + 1],
                                             HH[:, 255:256]))
                            t_scan[par] = S.tok("v", vtt(STG[0][:, 0:256], HH[:, 0:256],
                                                         cb, OP.mult))
                            if j == 0:
                                S.raw("v", vcopy(STG[1][:, 0:256], STG[0][:, 0:256]))
                            else:
                                S.raw("v", vtt(STG[1][:, 0:256], STG[1][:, 0:256],
                                               STG[0][:, 0:256], OP.add))
                        if sg == 0:
                            t_hc_done = S.tok("v", vcopy(xsr_t(db), STG[1][:, 0:256]))
                        else:
                            t_hc_done = S.tok("v", vtt(xsr_t(db), xsr_t(db),
                                                       STG[1][:, 0:256], OP.add))
                ty2 = None
                for db in range(16):
                    S.wait("v", t_hc_done)
                    S.raw("v", vstt(STG[0][:, 0:256], xsil_t(db),
                                    MCOL[:, 96 + db:97 + db], xsr_t(db), OP.mult, OP.add))
                    ty2 = S.tok("v", vtt(xsr_t(db), STG[0][:, 0:256], sz_t(db), OP.mult))
                if rev:
                    S.wait("g", ty2)
                    tg1l = S.dma("g", DTH[:, 0:2048].rearrange("p (k t) -> p k t", k=NB),
                                 gsp.rearrange("(k p) t -> p k t", p=128)[:, :, osl])
                t_go = None
                for mb2 in range(NB):
                    S.wait("t", ty2)
                    if t_go is not None:
                        S.wait("t", t_go)
                    for db in range(16):
                        tom = S.tok("t", mm(PS[mb2 % 2][:, 0:256],
                                            OUT_W[:, 1024 * db + 128 * mb2:1024 * db + 128 * (mb2 + 1)],
                                            xsr_t(db), db == 0, db == 15))
                    S.wait("v", tom)
                    if not rev:
                        S.wait("v", t_xln3)
                        t_go = S.tok("v", vtt(g1s_t(mb2), XLN(mb2)[:, isl],
                                              PS[mb2 % 2][:, 0:256], OP.add))
                    else:
                        S.wait("v", tg1l)
                        tga = S.tok("v", vtt(STG[0][:, 0:256], g1s_t(mb2),
                                             PS[mb2 % 2][:, 0:256][:, ::-1], OP.add))
                        S.wait("s", tga)
                        tge = S.tok("s", act(STG[1][:, 0:256], STG[0][:, 0:256], AF.Gelu))
                        S.wait("v", tge)
                        t_go = S.tok("v", vtt(XLN(mb2)[:, osl], XLN(mb2)[:, osl],
                                              STG[1][:, 0:256], OP.add))
                        t_xi1 = t_go
                if not rev:
                    S.wait("g", t_go)
                    t_sp_prev = S.dma("g", gsp.rearrange("(k p) t -> p k t", p=128)[:, :, isl],
                                      DTH[:, 0:2048].rearrange("p (k t) -> p k t", k=NB))
                t_y_done = t_go
            return t_xi1, S.now("t")

        tw1, tn1 = load_mamba_weights("m1", t_spill)
        _xx, t_m1t = emit_mamba(False, tw1, tn1)
        tw2, tn2_ = load_mamba_weights("m2", t_m1t)
        t_xi1, t_m2t = emit_mamba(True, tw2, tn2_)

        st4 = ln_stats(XLN, 0, 1, t_xi1, 0, 1)
        XFF = GB[:, 0:NB * T]

        def xff_t(kb):
            return XFF[:, 2048 * kb:2048 * (kb + 1)]

        t_xff = None
        for tt in range(NTT):
            sl = slice(TT * tt, TT * (tt + 1))
            S.wait("t", S.now("v"))
            tb1 = bcast_row(0, sl, 2, st4)
            tb2 = bcast_row(1, sl, 3, st4)
            for kb in range(NB):
                S.wait("v", tb2)
                S.raw("v", vtt(STGB[0][:, :], XLN(kb)[:, sl], PS[2][:, :], OP.subtract))
                S.raw("v", vtt(STGB[0][:, :], STGB[0][:, :], PS[3][:, :], OP.mult))
                t_xff = S.tok("v", vts(xff_t(kb)[:, sl], STGB[0][:, :],
                                       NRM[:, 16 + kb:17 + kb], NRM[:, 24 + kb:25 + kb],
                                       OP.mult, OP.add))
        FF1Q = AUX[:, 0:8192]
        FF2Q = AUX[:, 8192:16384]
        HIDQ = XAB[:, 0:2048]
        ACC2 = XAB[:, 8192:24576]

        def acc2_t(ob):
            return ACC2[:, 2048 * ob:2048 * (ob + 1)]

        t_ff1b = S.dma("g", FFB[:, :], dram["ffbpack"])
        t_q = None
        t_acc = None
        for q in range(4):
            S.wait("g", t_m2t)
            if q > 0:
                S.wait("g", t_q)
            t_w1q = S.dma("g", FF1Q.rearrange("p (k c) -> p k c", k=NB),
                          dram["ff1T"][:, 1024 * q:1024 * (q + 1)].rearrange(
                              "(k p) c -> p k c", p=128))
            t_w2q = S.dma("g", FF2Q.rearrange("p (k c) -> p k c", k=NB),
                          dram["ff2T"][1024 * q:1024 * (q + 1), :].rearrange(
                              "(k p) c -> p k c", p=128))
            for tq in range(8):
                qsl = slice(256 * tq, 256 * (tq + 1))
                thev = {}
                for hb in range(8):
                    S.wait("t", t_w1q)
                    S.wait("t", t_xff)
                    S.wait("t", t_acc)
                    if hb >= 2:
                        S.wait("t", thev[hb - 2])
                    for kb in range(NB):
                        tm = S.tok("t", mm(PS[hb % 2][:, 0:256],
                                           FF1Q[:, 1024 * kb + 128 * hb:1024 * kb + 128 * (hb + 1)],
                                           xff_t(kb)[:, qsl], kb == 0, kb == NB - 1))
                    S.wait("s", tm)
                    S.wait("s", t_ff1b)
                    thev[hb] = S.tok("s", act(HIDQ[:, 256 * hb:256 * (hb + 1)],
                                              PS[hb % 2][:, 0:256], AF.Gelu,
                                              bias=FFB[:, 8 * q + hb:8 * q + hb + 1]))
                for ob in range(NB):
                    S.wait("t", t_w2q)
                    S.wait("t", thev[7])
                    if ob >= 2:
                        S.wait("t", t_acc)
                    for ht in range(8):
                        tm2 = S.tok("t", mm(PS[2 + ob % 2][:, 0:256],
                                            FF2Q[:, 1024 * ht + 128 * ob:1024 * ht + 128 * (ob + 1)],
                                            HIDQ[:, 256 * ht:256 * (ht + 1)],
                                            ht == 0, ht == 7))
                    if q == 0:
                        S.wait("s", tm2)
                        t_acc = S.tok("s", act(acc2_t(ob)[:, qsl], PS[2 + ob % 2][:, 0:256],
                                               AF.Copy))
                    else:
                        S.wait("v", tm2)
                        t_acc = S.tok("v", vtt(acc2_t(ob)[:, qsl], acc2_t(ob)[:, qsl],
                                               PS[2 + ob % 2][:, 0:256], OP.add))
            t_q = S.now("t")
        t_xi2 = None
        for ob in range(NB):
            S.wait("v", t_acc)
            t_xi2 = S.tok("v", vstt(XLN(ob)[:, :], acc2_t(ob), NRM[:, 28]
                          if False else FFB[:, 32 + ob:33 + ob], XLN(ob)[:, :],
                          OP.add, OP.add))
        S.wait("y", t_q)
        S.wait("y", t_xi2)
        t_rel = S.dma("y", XA.rearrange("p (i t) -> p i t", i=NB),
                      dram["x2_spill"].rearrange("(i p) t -> p i t", p=128))
        t_x3 = None
        for ob in range(NB):
            S.wait("v", t_rel)
            S.wait("v", t_xi2)
            t_x3 = S.tok("v", vtt(XT(ob), XT(ob), XLN(ob)[:, :], OP.add))
        t_last = t_x3

    if _NPHASE >= 4 and _L0_STOP == "":
        t_in = t_last

        XB4 = GB[:, 0:NB * T]

        def xb4_view(i):
            return XB4[:, T * i:T * (i + 1)]

        S.wait("g", t_in)
        t_xb4 = S.dma("g", XB4.rearrange("p (i t) -> p i t", i=NB),
                      XA.rearrange("p (i t) -> p i t", i=NB))
        st4b = ln_stats(xb4_view, 0, 1, t_xb4, 0, 1)
        t_xln4 = ln_apply(xb4_view, XLN, 0, 1, st4b, 2, 3)

        HR2 = GB[:, 8192:10240].bitcast(F32)
        WRS = HR2[:, 0:64]
        EYE2 = HR2[:, 64:192]
        LTM2 = HR2[:, 192:208]
        LTZ2 = HR2[:, 208:224]
        RC2 = HR2[:, 224:232]
        RE2 = HR2[:, 232:248]
        RSBN2 = HR2[:, 248:256]
        GST2 = HR2[:, 256:768]
        RRS2 = HR2[0:1, 768:784]
        S.wait("y", t_xln4)
        t_wr = S.dma("y", WRS.rearrange("p (k c) -> p k c", k=NB),
                     dram["Wr"].rearrange("(k p) c -> p k c", p=128))
        t_eye2 = S.dma("y", EYE2, dram["eye128"])
        S.wait("g", t_xln4)
        t_or = S.dma("g", STGB[1][0:1, 0:128], dram["ones128"][0:1, :])

        S.wait("t", t_wr)
        S.wait("t", S.now("v"))
        for kb in range(NB):
            trs = S.tok("t", mm(PS[5][0:8, 0:1], WRS[:, 8 * kb:8 * (kb + 1)],
                                ONESR[:, 0:1], kb == 0, kb == NB - 1))
        S.wait("t", t_eye2)
        S.wait("v", trs)
        S.raw("v", vcopy(GST2[0:8, 0:1], PS[5][0:8, 0:1]))
        tev4 = S.tok("v", vcopy(GST2[0:8, 1:9], EYE2[0:8, 0:8]))
        S.wait("t", tev4)
        trr = S.tok("t", mm(PS[6][0:1, 0:8], GST2[0:8, 0:1], GST2[0:8, 1:9], True, True))
        S.wait("v", trr)
        tn = S.tok("v", vts(RRS2[0:1, 0:8], PS[6][0:1, 0:8], -1.0))
        S.wait("t", tn)
        trsb = S.tok("t", mm(PS[6][:, 0:8], ONESR[0:1, :], RRS2[0:1, 0:8], True, True))
        S.wait("v", trsb)
        tn2b = S.tok("v", vcopy(RSBN2, PS[6][:, 0:8]))

        t_selw4 = None
        memset4 = S.tok("v", lambda h: h.memset(SELW[:, :], 0.0))
        for tt in range(NTT):
            sl = slice(TT * tt, TT * (tt + 1))
            S.wait("t", S.now("v"))
            for kb in range(NB):
                tg = S.tok("t", mm(PS[5][0:8, :], WRS[:, 8 * kb:8 * (kb + 1)],
                                   XT(kb)[:, sl], kb == 0, kb == NB - 1))
            S.wait("v", tg)
            tgs = S.tok("v", vcopy(GST2[0:8, :], PS[5][0:8, :]))
            S.wait("t", tgs)
            S.wait("t", st4b)
            for tc in range(4):
                csl = slice(128 * tc, 128 * (tc + 1))
                S.raw("t", mm(PS[7][:, 16 * tc:16 * tc + 8],
                              GST2[0:8, csl], EYE2[0:8, 0:8], tc == 0, False))
                S.raw("t", mm(PS[7][:, 16 * tc + 14:16 * tc + 15],
                              ROW(0, slice(sl.start + csl.start, sl.start + csl.stop)),
                              EYE2[0:1, 0:1], False, False))
                ttr = S.tok("t", mm(PS[7][:, 16 * tc + 15:16 * tc + 16],
                                    ROW(1, slice(sl.start + csl.start, sl.start + csl.stop)),
                                    EYE2[0:1, 0:1], False, tc == 3))
            S.wait("v", ttr)
            S.wait("v", tn2b)
            tswt = None
            for tc in range(4):
                o = 16 * tc
                S.wait("v", tswt)
                S.raw("v", vstt(LTM2[:, 0:8], RSBN2, PS[7][:, o + 14:o + 15],
                                PS[7][:, o:o + 8], OP.mult, OP.add))
                S.raw("v", vts(LTZ2[:, 0:8], LTM2[:, 0:8], PS[7][:, o + 15:o + 16], None, OP.mult))
                S.raw("v", lambda h: h.tensor_reduce(
                    out=RC2[:, 0:1], in_=LTZ2[:, 0:8], axis=mybir.AxisListType.X, op=OP.max))
                S.raw("v", vts(RC2[:, 1:2], RC2[:, 0:1], -1.0))
                S.raw("v", vts(LTM2[:, 0:8], LTZ2[:, 0:8], RC2[:, 0:1], None, OP.is_ge))
                S.raw("v", vts(LTM2[:, 8:16], LTM2[:, 0:8], -1e30, None, OP.mult))
                S.raw("v", vtt(LTM2[:, 8:16], LTZ2[:, 0:8], LTM2[:, 8:16], OP.add))
                S.raw("v", lambda h: h.tensor_reduce(
                    out=RC2[:, 4:5], in_=LTM2[:, 8:16], axis=mybir.AxisListType.X, op=OP.max))
                tmask = S.tok("v", vts(LTM2[:, 0:8], LTZ2[:, 0:8], RC2[:, 4:5], None, OP.is_ge))
                S.wait("s", tmask)
                texp = S.tok("s", act(RE2[:, 0:8], LTZ2[:, 0:8], AF.Exp, bias=RC2[:, 1:2]))
                S.wait("v", texp)
                S.raw("v", lambda h: h.tensor_reduce(
                    out=RC2[:, 2:3], in_=RE2[:, 0:8], axis=mybir.AxisListType.X, op=OP.add))
                S.raw("v", lambda h: h.reciprocal(out=RC2[:, 3:4], in_=RC2[:, 2:3]))
                S.raw("v", vts(RE2[:, 0:8], RE2[:, 0:8], RC2[:, 3:4], None, OP.mult))
                tswm = S.tok("v", vtt(LTZ2[:, 0:8], LTM2[:, 0:8], RE2[:, 0:8], OP.mult))
                S.wait("t", tswm)
                tswt = S.tok("t", mm(PS[4][0:8, 128 * tc:128 * (tc + 1)],
                                     LTZ2[:, 0:8], EYE2[:, :], tc == 0, tc == 3))
            S.wait("v", tswt)
            S.wait("v", memset4)
            t_selw4 = S.tok("v", vcopy(SELW[0:8, sl], PS[4][0:8, :]))

        CAP = float(globals().get('_CAP_OVERRIDE', 5120.0))
        cc_in = nc.dram_tensor("cc_in", [8, T], F32, kind="Internal").ap()
        cc_out = nc.dram_tensor("cc_out", [8, 8, T], F32, kind="Internal").ap()
        S.wait("g", t_selw4)
        t_ccw = S.dma("g", cc_in, SELW[0:8, :])
        S.wait("g", t_ccw)
        t_cc = S.tok("g", lambda h: h.collective_compute(
            "AllGather", OP.bypass, replica_groups=[list(range(8))],
            ins=[cc_in.opt()], outs=[cc_out.opt()]))
        GATES = GB[:, 0:2048].bitcast(F32)
        CBIS = GB[:, 2048:2176].bitcast(F32)
        LO = CBIS[:, 0:8]
        CAND = CBIS[:, 8:16]
        CNT = CBIS[:, 16:24]
        MSKC = GB[:, 2176:2304].bitcast(F32)
        S.wait("g", t_cc)
        for e in range(8):
            for r in range(8):
                t_g8 = S.dma("g", GATES[16 * r:16 * (r + 1), 128 * e:128 * (e + 1)],
                             cc_out[r, e, :].rearrange("(a f) -> a f", f=128))
        tz = S.tok("v", lambda h: h.memset(CBIS[:, 0:24], 0.0))
        t_bis = None
        for it in range(1, 51):
            step = 2.0 ** (-it)
            S.wait("v", t_g8)
            S.wait("v", tz)
            S.wait("v", t_bis)
            S.raw("v", vts(CAND, LO, step, None, OP.add))
            for e in range(8):
                S.raw("v", vts(MSKC[:, 0:64] if False else STG[0][:, 0:128],
                               GATES[:, 128 * e:128 * (e + 1)], CAND[:, e:e + 1],
                               None, OP.is_ge))
                S.raw("v", lambda h, e=e: h.tensor_reduce(
                    out=CNT[:, e:e + 1], in_=STG[0][:, 0:128],
                    axis=mybir.AxisListType.X, op=OP.add))
            tcn = S.tok("v")
            S.wait("t", tcn)
            tcm = S.tok("t", mm(PS[5][0:1, 0:8], ONESR[:, 0:1], CNT, True, True))
            S.wait("v", tcm)
            tsl = S.tok("v", vts(STG[0][0:1, 0:8], PS[5][0:1, 0:8], CAP, None, OP.is_ge))
            S.wait("t", tsl)
            tsb5 = S.tok("t", mm(PS[5][:, 8:16], ONESR[0:1, :], STG[0][0:1, 0:8],
                                 True, True))
            S.wait("v", tsb5)
            t_bis = S.tok("v", vstt(LO, PS[5][:, 8:16], step, LO, OP.mult, OP.add))
        S.wait("t", t_bis)
        S.wait("t", t_eye2)
        ttau = S.tok("t", mm(PS[5][0:8, 16:17], LO[0:1, 0:8], EYE2[0:1, 0:1], True, True))
        S.wait("v", ttau)
        MSK8 = GB[0:8, 4096:6144]
        S.raw("v", vts(MSK8, SELW[0:8, :], PS[5][0:8, 16:17], None, OP.is_ge))
        t_capm = S.tok("v", vtt(SELW[0:8, :], SELW[0:8, :], MSK8, OP.mult))
        t_selw4 = t_capm

        S.wait("y", t_in)
        t_init = S.dma("y", out_fm.rearrange("(i p) t -> p i t", p=128),
                       XA.rearrange("p (i t) -> p i t", i=NB))
        XAB4 = XA[:, :].bitcast(BF16)

        def w2t(j):
            if j < 24:
                return AUX[:, 1024 * j:1024 * (j + 1)]
            return GB[:, 8192 + 1024 * (j - 24):8192 + 1024 * (j - 23)]

        HID4 = GB[:, 0:8192]
        t_w1 = t_w2 = None
        last_w1_use = last_w2_use = None
        t_acc_tok = None
        last_da = None
        for e in range(8):
            S.wait("g", t_init)
            S.wait("g", t_selw4)
            S.wait("g", S.now("t"))
            S.wait("g", last_w1_use)
            t_w1 = S.dma("g", XAB4.rearrange("p (k c) -> p k c", k=NB),
                         dram["W1"][e].rearrange("(k p) c -> p k c", p=128))
            S.wait("g", last_w2_use)
            S.wait("g", t_selw4 if e == 0 else None)
            t_w2 = S.dma("g", AUX[:, 0:24576].rearrange("p (k c) -> p k c", k=24),
                         dram["W2"][e][0:3072].rearrange("(k p) c -> p k c", p=128))
            t_w2b = S.dma("g", GB[:, 8192:16384].rearrange("p (k c) -> p k c", k=8),
                          dram["W2"][e][3072:4096].rearrange("(k p) c -> p k c", p=128))
            for tq in range(8):
                qsl = slice(256 * tq, 256 * (tq + 1))
                S.wait("t", t_selw4)
                S.wait("t", t_eyeb)
                S.wait("t", t_acc_tok)
                trx4 = S.tok("t", mm(PS[6][0:1, 0:256], EYEB[0:8, e:e + 1],
                                     SELW[0:8, qsl], True, True))
                S.wait("s", trx4)
                tsw4 = S.tok("s", act(STGB[0][0:1, 0:256], PS[6][0:1, 0:256], AF.Copy))
                S.wait("t", tsw4)
                S.wait("t", t_or)
                tswb4 = S.tok("t", mm(PS[6][:, 0:256], STGB[1][0:1, 0:128],
                                      STGB[0][0:1, 0:256], True, True))
                S.wait("s", tswb4)
                tse4 = S.tok("s", act(STGB[1][:, 256:512], PS[6][:, 0:256], AF.Copy))
                thv = {}
                for hb in range(32):
                    S.wait("t", t_w1)
                    if hb >= 2:
                        S.wait("t", thv[hb - 2])
                    for kb in range(NB):
                        tm = S.tok("t", mm(PS[4 + hb % 2][:, 0:256],
                                           XAB4[:, 4096 * kb + 128 * hb:4096 * kb + 128 * (hb + 1)],
                                           XLN(kb)[:, qsl], kb == 0, kb == NB - 1))
                    S.wait("s", tm)
                    thv[hb] = S.tok("s", act(HID4[:, 256 * hb:256 * (hb + 1)],
                                             PS[4 + hb % 2][:, 0:256], AF.Relu))
                for ob in range(NB):
                    S.wait("t", t_w2)
                    S.wait("t", t_w2b)
                    S.wait("t", thv[31])
                    S.wait("t", t_acc_tok)
                    bank, half = ob // 2, ob % 2
                    for j in range(32):
                        tm2 = S.tok("t", mm(PS[bank][:, 256 * half:256 * (half + 1)],
                                            w2t(j)[:, 128 * ob:128 * (ob + 1)],
                                            HID4[:, 256 * j:256 * (j + 1)],
                                            j == 0, j == 31))
                    S.wait("v", tm2)
                    S.wait("v", tse4)
                    S.wait("v", last_da)
                    tgt = S.tok("v", vtt(STG[ob % 2][:, 0:256],
                                         PS[bank][:, 256 * half:256 * (half + 1)],
                                         STGB[1][:, 256:512], OP.mult))
                    S.wait("g", tgt)
                    last_da = S.dma("g", out_fm[128 * ob:128 * (ob + 1), qsl],
                                    STG[ob % 2][:, 0:256], sem="da", accum_op=OP.add)
                    t_acc_tok = tgt
                last_w1_use = S.now("t")
            last_w2_use = S.now("t")
        S.wait("y", S.now("dg"))
        t_last = None

    if _DEBUG:
        S.wait("y", t_last)
        t_dbg = S.dma("y", dram["dbg_x1"].rearrange("(i p) t -> p i t", p=128),
                      XA.rearrange("p (i t) -> p i t", i=NB))
        S.wait("y", t_dbg)

    if _NPHASE >= 4 and _L0_STOP == "":
        S.wait("g", S.now("da"))
        S.wait("g", S.now("dg"))
        S.wait("y", S.now("da"))
    else:
        S.wait("y", t_last)
        t_out = S.dma("y", out_fm.rearrange("(i p) t -> p i t", p=128),
                      XA.rearrange("p (i t) -> p i t", i=NB))
        S.wait("y", t_out)
        S.wait("g", S.now("dg"))

    with nc.Block() as block:
        def emit(h, lst):
            for kind, *rest in lst:
                if kind == "wait":
                    key, c = rest
                    h.wait_ge(sems[key], c)
                else:
                    fn, key, n = rest
                    inst = fn(h) if fn is not None else h.nop()
                    if key is not None:
                        inst.then_inc(sems[key], n)

        @block.tensor
        def _(h):
            emit(h, S.q["t"])

        @block.vector
        def _(h):
            emit(h, S.q["v"])

        @block.scalar
        def _(h):
            emit(h, S.q["s"])

        @block.gpsimd
        def _(h):
            emit(h, S.q["g"])

        @block.sync
        def _(h):
            emit(h, S.q["y"])

    es.close()
    return nc


def _prep_inputs(inputs):
    f = np.ascontiguousarray
    w = {}
    w["Wq"] = f(inputs["L0_attn_Wq"])
    w["Wk"] = f(inputs["L0_attn_Wk"])
    w["Wv"] = f(inputs["L0_attn_Wv"])
    w["Wo"] = f(inputs["L0_attn_Wo"])
    w["rf"] = f(inputs["L0_attn_rf"])
    w["r0w"] = f(inputs["L0_ffn_router_w"])
    w["l0w1"] = f(inputs["L0_ffn_w1"])
    w["l0b1"] = f(inputs["L0_ffn_b1"])
    w["l0w2"] = f(inputs["L0_ffn_w2"])
    w["l0b2"] = f(inputs["L0_ffn_b2"])
    for m in ("m1", "m2"):
        w[m + "_inwT"] = f(inputs[f"L1_attn_{m}_in_w"].T)
        w[m + "_xprojT"] = f(inputs[f"L1_attn_{m}_xproj_w"].T)
        w[m + "_dtwT"] = f(inputs[f"L1_attn_{m}_dt_w"].T)
        w[m + "_outwT"] = f(inputs[f"L1_attn_{m}_out_w"].T)
        w[m + "_convw"] = f(inputs[f"L1_attn_{m}_conv_w"])
        w[m + "_convb"] = f(inputs[f"L1_attn_{m}_conv_b"])
        w[m + "_dtb"] = f(inputs[f"L1_attn_{m}_dt_b"])
        w[m + "_Alog"] = f(inputs[f"L1_attn_{m}_Alog"])
        w[m + "_D"] = f(inputs[f"L1_attn_{m}_D"])
    w["norm_g"] = f(inputs["L1_attn_norm_g"])
    w["norm_b"] = f(inputs["L1_attn_norm_b"])
    w["ffn_g"] = f(inputs["L1_attn_ffn_g"])
    w["ffn_b"] = f(inputs["L1_attn_ffn_b"])
    w["ff1T"] = f(inputs["L1_attn_ff1_w"].T)
    w["ff1b"] = f(inputs["L1_attn_ff1_b"])
    w["ff2T"] = f(inputs["L1_attn_ff2_w"].T)
    w["ff2b"] = f(inputs["L1_attn_ff2_b"])
    w["Wr"] = f(inputs["L1_ffn_Wr"])
    w["W1"] = f(inputs["L1_ffn_W1"])
    w["W2"] = f(inputs["L1_ffn_W2"])
    w["ones128"] = np.ones((128, 128), np.float32)
    rfp = np.zeros((128, 256), np.float32)
    rfp[0:64, 0:128] = inputs["L0_attn_rf"]
    rfp[64:128, 128:256] = inputs["L0_attn_rf"]
    w["rfpad"] = rfp
    w["onescol"] = np.ones((128, 1), np.float32)
    w["eye128"] = np.eye(128, dtype=np.float32)
    w["l0b1r"] = np.ascontiguousarray(
        inputs["L0_ffn_b1"].reshape(14, 16, 128).transpose(0, 2, 1))

    def cols(vec, n):
        return np.ascontiguousarray(np.asarray(vec).reshape(n, 128).T)

    w["nrmpack"] = np.concatenate([cols(inputs["L1_attn_norm_g"], 8),
                                   cols(inputs["L1_attn_norm_b"], 8),
                                   cols(inputs["L1_attn_ffn_g"], 8),
                                   cols(inputs["L1_attn_ffn_b"], 8)], axis=1)
    w["ffbpack"] = np.concatenate([cols(inputs["L1_attn_ff1_b"], 32),
                                   cols(inputs["L1_attn_ff2_b"], 8)], axis=1)
    for m in ("m1", "m2"):
        cw = np.asarray(inputs[f"L1_attn_{m}_conv_w"]).reshape(16, 128, 4)
        cwc = cw.transpose(1, 0, 2).reshape(128, 64)
        w[m + "_colpack"] = np.concatenate(
            [cwc, cols(inputs[f"L1_attn_{m}_conv_b"], 16),
             cols(inputs[f"L1_attn_{m}_dt_b"], 16),
             cols(inputs[f"L1_attn_{m}_D"], 16)], axis=1).astype(np.float32)
        al = np.asarray(inputs[f"L1_attn_{m}_Alog"]).reshape(16, 128, 16)
        w[m + "_Alogr"] = np.ascontiguousarray(
            al.transpose(1, 0, 2).reshape(128, 256)).astype(np.float32)
        dtw = np.zeros((128, 2048), np.float32)
        dtw[0:64, :] = np.asarray(inputs[f"L1_attn_{m}_dt_w"]).T
        w[m + "_dtw128"] = dtw
    w["eye16"] = np.eye(16, dtype=np.float32)
    w["cvals"] = np.tile(np.pad(np.array(_CONSTS, np.float32), (0, 16 - len(_CONSTS)))[None, :], (128, 1))
    sel = np.zeros((128, 2048), np.float32)
    for p in range(128):
        if (p % 32) < 16 and p < 96:
            r = p % 32
            sel[p, 128 * r:128 * (r + 1)] = 1.0
    w["sel128"] = sel
    return w


def kernel(**inputs):
    x = np.asarray(inputs["L0_attn_x"])
    w = _prep_inputs(inputs)
    nc = build_core_program()
    in_maps = []
    for b in range(8):
        m = dict(w)
        m["x_fm"] = np.ascontiguousarray(x[b].T)
        in_maps.append(m)
    res = run_bass_kernel_spmd(nc, in_maps, core_ids=list(range(8)))
    out = np.stack([np.ascontiguousarray(r["out_fm"].T) for r in res.results], axis=0)
    if _DEBUG:
        kernel.dbg = res.results
    return out


# version 04
# speedup vs baseline: 1.0927x; 1.0927x over previous
import math
import numpy as np
import ml_dtypes
from contextlib import ExitStack

import concourse.bass as bass
import concourse.mybir as mybir
from concourse.bass_utils import run_bass_kernel_spmd
from concourse.alu_op_type import AluOpType as OP

F32 = mybir.dt.float32
BF16 = mybir.dt.bfloat16
I32 = mybir.dt.int32
AF = mybir.ActivationFunctionType

D = 1024
T = 2048
NB = D // 128
TT = 512
NTT = T // TT
PI = math.pi
BF = ml_dtypes.bfloat16

_DEBUG = False
_CONSTS = [1e-05, PI / 2, 1.0, 0.25, -0.5, 2 * PI, -2 * PI, 0.5, 1.0 / 128.0]
RMAGIC = 12582912.0
_NPHASE = 4
_L0_STOP = ""

ENGS = ("t", "v", "s", "g", "y")


AUTO_SYNC = ("v", "s")


class Sched:

    def __init__(self):
        self.q = {e: [] for e in ENGS}
        self.cnt = {k: 0 for k in ("t", "v", "s", "g", "dg", "dy", "da")}
        self.wm = {e: dict.fromkeys(self.cnt, 0) for e in ENGS}
        self.pending_bar = {"dg": 0, "dy": 0, "da": 0}

    def raw(self, e, fn):
        if e in AUTO_SYNC:
            self.tok(e, fn)
            return None
        self.q[e].append(("op", fn, None, 0))
        return None

    def tok(self, e, fn=None, key=None):
        key = key or e
        if e in AUTO_SYNC and key == e:
            self.wait(e, (e, self.cnt[e]))
        self.cnt[key] += 1
        self.q[e].append(("op", fn, key, 1))
        return (key, self.cnt[key])

    def dma(self, e, out, in_, sem=None, **kw):
        key = sem or ("dg" if e == "g" else "dy")
        bar = self.pending_bar[key]
        if bar > self.wm[e][key]:
            self.wm[e][key] = bar
            self.q[e].append(("wait", key, bar))
        self.cnt[key] += 16

        def fn(h, out=out, in_=in_, kw=kw):
            return h.dma_start(out=out, in_=in_, **kw)

        self.q[e].append(("op", fn, key, 16))
        return (key, self.cnt[key])

    def wait(self, e, tok):
        if tok is None:
            return
        key, c = tok
        if key in ("dg", "dy", "da"):
            c = self.cnt[key]
            self.pending_bar[key] = max(self.pending_bar[key], c)
        if self.wm[e][key] >= c:
            return
        self.wm[e][key] = c
        self.q[e].append(("wait", key, c))

    def now(self, key):
        return (key, self.cnt[key])

    def sync_self(self, e):
        t = self.tok(e)
        self.wait(e, t)


def build_core_program():
    nc = bass.Bass("TRN2", target_bir_lowering=False)
    S = Sched()
    dram = {}

    def din(name, shape, dt=F32):
        dram[name] = nc.dram_tensor(name, list(shape), dt, kind="ExternalInput").ap()
        return dram[name]

    def dout(name, shape, dt=F32):
        dram[name] = nc.dram_tensor(name, list(shape), dt, kind="ExternalOutput").ap()
        return dram[name]

    x_in = din("x_fm", [D, T])
    out_fm = dout("out_fm", [D, T])

    for w in ("Wq", "Wk", "Wv", "Wo"):
        din(w, [D, D])
    din("rf", [64, 128])
    din("rfpad", [128, 256])
    din("r0w", [D, 14])
    din("l0w1", [14, D, 2048])
    din("l0b1", [14, 2048])
    din("l0b1r", [14, 128, 16])
    din("l0w2", [14, 2048, D])
    din("l0b2", [14, D])
    for m in ("m1", "m2"):
        din(m + "_inwT", [D, 4096])
        din(m + "_xprojT", [2048, 96])
        din(m + "_dtwT", [64, 2048])
        din(m + "_outwT", [2048, D])
        din(m + "_convw", [2048, 4])
        din(m + "_convb", [2048])
        din(m + "_dtb", [2048])
        din(m + "_Alog", [2048, 16])
        din(m + "_D", [2048])
    for v in ("norm_g", "norm_b", "ffn_g", "ffn_b"):
        din(v, [D])
    din("ff1T", [D, 4096])
    din("ff1b", [4096])
    din("ff2T", [4096, D])
    din("ff2b", [D])
    din("Wr", [D, 8])
    din("W1", [8, D, 4096])
    din("W2", [8, 4096, D])
    din("ones128", [128, 128])
    din("nrmpack", [128, 32])
    din("ffbpack", [128, 40])
    for m in ("m1", "m2"):
        din(m + "_colpack", [128, 112])
        din(m + "_Alogr", [128, 256])
        din(m + "_dtw128", [128, 2048])
    din("sel128", [128, 2048])
    din("cvals", [128, 16])
    din("onescol", [128, 1])
    din("eye128", [128, 128])
    din("eye16", [16, 16])
    dram["x2_spill"] = nc.dram_tensor("x2_spill", [D, T], F32, kind="Internal").ap()

    dbg_names = ("x1", "x2", "x3") if _DEBUG else ()
    for name in dbg_names:
        dout("dbg_" + name, [D, T])

    es = ExitStack()
    es.__enter__()

    def sbuf(name, shape, dt=F32):
        return es.enter_context(nc.sbuf_tensor(name, list(shape), dt))

    XA = sbuf("XA", [128, NB * T])
    XLNB = sbuf("XLNB", [128, NB * T], BF16)
    GB = sbuf("GB", [128, NB * T], BF16)
    AUX = sbuf("AUX", [128, 24576], BF16)

    ONESR = sbuf("ONESR", [128, 128])
    ONESCB = sbuf("ONESCB", [128, 1], BF16)
    ROWS = sbuf("ROWS", [1, 2 * T + 1024])
    STG = [sbuf(f"STG{i}", [128, TT]) for i in range(2)]
    STGB = [sbuf(f"STGB{i}", [128, TT], BF16) for i in range(2)]
    RFP = sbuf("RFP", [128, 256], BF16)
    CVALS = sbuf("CVALS", [128, 16])
    CVALSB = sbuf("CVALSB", [128, 16], BF16)
    SELW = sbuf("SELW", [16, T], BF16)
    EYEB = sbuf("EYEB", [16, 16], BF16)
    B1E = sbuf("B1E", [128, 16], BF16)
    XSH = AUX[:, 19968:20016]
    BCR = AUX[0:16, 20016:20528]
    DA = [AUX[:, 20528 + 256 * i:20784 + 256 * i] for i in range(2)]
    DB = AUX[:, 21040:21296]
    EYE128B = AUX[:, 21296:21424]
    FFB = AUX[:, 21424:21464]
    HH = AUX[:, 21504:22016].bitcast(F32)
    CARR = AUX[:, 22016:22528].bitcast(F32)
    NEGA = AUX[:, 22528:23040].bitcast(F32)
    MCOL = AUX[:, 23040:23264].bitcast(F32)
    NRM = AUX[:, 23264:23328].bitcast(F32)

    PS = [es.enter_context(nc.psum_tensor(f"PS{i}", [128, 512], F32)) for i in range(8)]

    sems = {k: es.enter_context(nc.semaphore("sem_" + k)) for k in S.cnt}

    def XT(i):
        return XA[:, T * i:T * (i + 1)]

    def XLN(i):
        return XLNB[:, T * i:T * (i + 1)]

    def ROW(r, sl=None):
        if sl is None:
            sl = slice(0, T)
        return ROWS[0:1, T * r + sl.start: T * r + sl.stop]

    def RT1(tt):
        return ROWS[0:1, 2 * T + TT * (tt % 2): 2 * T + TT * (tt % 2 + 1)]

    def act(out, in_, func, bias=0.0, scale=1.0, accum=None):
        def fn(h, o=out, i=in_, f=func, b=bias, sc=scale, a=accum):
            if a is None:
                return h.activation(out=o, in_=i, func=f, bias=b, scale=sc)
            return h.activation(out=o, in_=i, func=f, bias=b, scale=sc, accum_out=a)
        return fn

    def vtt(out, a, b, op):
        return lambda h, o=out, x=a, y=b, p=op: h.tensor_tensor(out=o, in0=x, in1=y, op=p)

    def vcopy(out, in_):
        return lambda h, o=out, i=in_: h.tensor_copy(out=o, in_=i)

    def vts(out, a, s1, s2=None, op0=OP.mult, op1=None):
        def fn(h, o=out, x=a, u=s1, w=s2, p0=op0, p1=op1):
            if p1 is None:
                return h.tensor_scalar(out=o, in0=x, scalar1=u, scalar2=None, op0=p0)
            return h.tensor_scalar(out=o, in0=x, scalar1=u, scalar2=w, op0=p0, op1=p1)
        return fn

    def vstt(out, a, s, b, op0, op1):
        return lambda h, o=out, x=a, u=s, y=b, p0=op0, p1=op1: h.scalar_tensor_tensor(
            out=o, in0=x, scalar=u, in1=y, op0=p0, op1=p1)

    def mm(psum_ap, lhsT, rhs, start, stop):
        return lambda h, o=psum_ap, l=lhsT, r=rhs, a=start, b=stop: h.matmul(
            o, l, r, start=a, stop=b)

    def load_w(dst, src, K=128):
        R, C = src.shape
        k = R // 128
        return S.dma("g", dst.rearrange("p (k c) -> p k c", k=k),
                     src.rearrange("(k p) c -> p k c", p=128))

    t_cval = S.dma("y", CVALS[:, :], dram["cvals"])
    t_cvalb = S.dma("g", CVALSB[:, :], dram["cvals"])
    for j, cv in enumerate(_CONSTS):
        nc.const_aps.register(F32, cv, CVALS[:, j:j + 1])
        nc.const_aps.register(BF16, cv, CVALSB[:, j:j + 1])
    for e in ENGS:
        S.wait(e, t_cval)
        S.wait(e, t_cvalb)
    t_ones = S.dma("y", ONESR[:, :], dram["ones128"])
    S.dma("g", ONESCB[:, :], dram["onescol"])
    t_rf = S.dma("g", RFP[:, :], dram["rfpad"])

    t_x = S.dma("y", XA.rearrange("p (i t) -> p i t", i=NB),
                x_in.rearrange("(i p) t -> p i t", p=128))
    XB0 = AUX[:, 0:NB * T]
    t_xb0 = S.dma("g", XB0.rearrange("p (i t) -> p i t", i=NB),
                  x_in.rearrange("(i p) t -> p i t", p=128))

    def ln_stats(xb_view, r_mu, r_inv, dep_tok, pa, pb):
        assert (r_mu, r_inv) == (0, 1)
        tq = None
        prev_mm = [None, None]
        for tt in range(NTT):
            sl = slice(TT * tt, TT * (tt + 1))
            for i in range(NB):
                S.wait("v", dep_tok)
                S.wait("v", prev_mm[i % 2])
                tk = S.tok("v", vtt(STGB[i % 2][:, :], xb_view(i)[:, sl],
                                    xb_view(i)[:, sl], OP.mult))
                S.wait("t", tk)
                S.raw("t", mm(PS[pa][0:1, :], ONESCB[:, :], xb_view(i)[:, sl],
                              i == 0, i == NB - 1))
                tq = S.tok("t", mm(PS[pb][0:1, :], ONESCB[:, :], STGB[i % 2][:, :],
                                   i == 0, i == NB - 1))
                prev_mm[i % 2] = tq
            S.wait("v", tq)
            S.raw("v", vts(ROW(0, sl), PS[pa][0:1, :], 1.0 / D))
            tv = S.tok("v", vts(ROW(1, sl), PS[pb][0:1, :], 1.0 / D))
        S.wait("v", tv)
        last = None
        for p2 in range(NTT // 2):
            c0, c1 = 2 * p2, 2 * p2 + 1
            sl0 = slice(TT * c0, TT * (c0 + 1))
            sl1 = slice(TT * c1, TT * (c1 + 1))
            if last is not None:
                S.wait("v", last)
            S.raw("v", vtt(RT1(c0), ROW(0, sl0), ROW(0, sl0), OP.mult))
            S.raw("v", vtt(RT1(c1), ROW(0, sl1), ROW(0, sl1), OP.mult))
            S.raw("v", vtt(STG[0][0:1, :], ROW(1, sl0), RT1(c0), OP.subtract))
            tb = S.tok("v", vtt(STG[1][0:1, :], ROW(1, sl1), RT1(c1), OP.subtract))
            S.wait("s", tb)
            S.raw("s", act(RT1(c0), STG[0][0:1, :], AF.Ln, bias=1e-5))
            S.raw("s", act(RT1(c1), STG[1][0:1, :], AF.Ln, bias=1e-5))
            S.raw("s", act(ROW(1, sl0), RT1(c0), AF.Exp, scale=-0.5))
            last = S.tok("s", act(ROW(1, sl1), RT1(c1), AF.Exp, scale=-0.5))
        return last

    def bcast_row(r, sl, bank, tok_dep):
        S.wait("t", tok_dep)
        S.wait("t", t_ones)
        return S.tok("t", mm(PS[bank][:, :], ONESR[0:1, :], ROW(r, sl), True, True))

    def ln_apply(xb_view, out_view, r_mu, r_inv, stat_tok, pa, pb):
        last = None
        te = None
        for tt in range(NTT):
            sl = slice(TT * tt, TT * (tt + 1))
            S.wait("t", te)
            tb1 = bcast_row(r_mu, sl, pa, stat_tok)
            tb2 = bcast_row(r_inv, sl, pb, stat_tok)
            S.wait("s", tb2)
            if last is not None:
                S.wait("s", last)
            S.raw("s", act(STG[0][:, :], PS[pa][:, :], AF.Copy))
            te = S.tok("s", act(STG[1][:, :], PS[pb][:, :], AF.Copy))
            for i in range(NB):
                S.wait("v", te)
                S.raw("v", vtt(STGB[i % 2][:, :], xb_view(i)[:, sl], STG[0][:, :], OP.subtract))
                if i > 0:
                    last = S.tok("v", vtt(out_view(i - 1)[:, sl], STGB[(i - 1) % 2][:, :],
                                          STG[1][:, :], OP.mult))
            last = S.tok("v", vtt(out_view(NB - 1)[:, sl], STGB[(NB - 1) % 2][:, :],
                                  STG[1][:, :], OP.mult))
        return last

    def fresh_xb(dst_region, dep_tok):
        S.wait("g", dep_tok)
        return S.dma("g", dst_region.rearrange("p (i t) -> p i t", i=NB),
                     XA.rearrange("p (i t) -> p i t", i=NB))

    def xb0_view(i):
        return XB0[:, T * i:T * (i + 1)]

    st1 = ln_stats(xb0_view, 0, 1, t_xb0, 0, 1)
    t_xln = ln_apply(xb0_view, XLN, 0, 1, st1, 2, 3)

    WK = AUX[:, 0:8192]
    WV = AUX[:, 8192:16384]
    WQ = AUX[:, 16384:24576]
    WO = AUX[:, 0:8192]
    S.wait("g", t_xln)
    t_wk = load_w(WK, dram["Wk"])
    t_wv = load_w(WV, dram["Wv"])
    t_wq = load_w(WQ, dram["Wq"])
    t_wo = None

    def wsl(arena, kb, ob):
        return arena[:, kb * D + 128 * ob: kb * D + 128 * (ob + 1)]

    KC = GB[:, 0:4096]
    VTC = GB[:, 4096:8192]
    KPT = GB[:, 8192:10240]
    KVX = GB[:, 14336:15488]
    QC = GB[:, 0:4096]
    QPG = GB[:, 8192:10240]
    ATTC = GB[:, 10240:14336]

    if _NPHASE >= 1:
        def trig_block(ps_bank, n_grp, out_ap_fn, part_halves=False):
            S.wait("v", S.now("s"))
            parts = []
            for half, (k_off, s_bias) in enumerate([(0.25, PI / 2), (0.0, 0.0)]):
                if part_halves:
                    src = ps_bank[64 * half:64 * (half + 1), 0:512]
                    w0 = STG[0][64 * half:64 * (half + 1), 0:512]
                    w1 = STG[1][64 * half:64 * (half + 1), 0:512]
                else:
                    src = ps_bank.rearrange("p (g f) -> p g f", f=128)[:, 0:n_grp, 64 * half:64 * (half + 1)]
                    w0 = STG[0].rearrange("p (g f) -> p g f", f=128)[:, 0:n_grp, 64 * half:64 * (half + 1)]
                    w1 = STG[1].rearrange("p (g f) -> p g f", f=128)[:, 0:n_grp, 64 * half:64 * (half + 1)]
                parts.append((src, w0, w1, out_ap_fn(half), s_bias, k_off))
            tr = None
            for src, w0, w1, oo, s_bias, k_off in parts:
                S.raw("v", vts(w0, src, 1.0 / (2 * PI), k_off, OP.mult, OP.add))
                S.raw("v", vts(w1, w0, RMAGIC, -RMAGIC, OP.add, OP.add))
                tr = S.tok("v", vstt(w0, w1, -2 * PI, src, OP.mult, OP.add))
            S.wait("s", tr)
            tok = None
            for src, w0, w1, oo, s_bias, k_off in parts:
                tok = S.tok("s", act(oo, w0, AF.Sin, bias=s_bias))
            return tok

        for tt in range(NTT):
            sl = slice(TT * tt, TT * (tt + 1))
            tke = None
            for ob in ([] if _L0_STOP in ("VnoK", "Vswap") else range(NB)):
                S.wait("t", t_wk)
                S.wait("t", t_xln)
                if tke is not None:
                    S.wait("t", tke)
                for kb in range(NB):
                    tk = S.tok("t", mm(PS[ob % 2][:, :], wsl(WK, kb, ob), XLN(kb)[:, sl],
                                       kb == 0, kb == NB - 1))
                S.wait("s", tk)
                tke = S.tok("s", act(KC[:, 512 * ob:512 * (ob + 1)], PS[ob % 2][:, :], AF.Copy))
            if _L0_STOP == "K":
                continue
            for sub in range(4):
                ssl = slice(TT * tt + 128 * sub, TT * tt + 128 * (sub + 1))
                S.wait("t", t_wv)
                for vh in range(2):
                    for kb in range(NB):
                        lhs_vt = wsl(WV, kb, 0) if _L0_STOP == "Vswap" else XLN(kb)[:, ssl]
                        tv = S.tok("t", mm(PS[2 + vh][:, :], lhs_vt,
                                           WV[:, kb * D + 512 * vh: kb * D + 512 * (vh + 1)],
                                           kb == 0, kb == NB - 1))
                S.wait("s", tv)
                if _L0_STOP == "Vnoevac":
                    tve = S.tok("s")
                else:
                    S.raw("s", act(VTC[:, 1024 * sub:1024 * sub + 512], PS[2][:, :], AF.Copy))
                    tve = S.tok("s", act(VTC[:, 1024 * sub + 512:1024 * (sub + 1)], PS[3][:, :], AF.Copy))
                if _L0_STOP in ("V", "Vnoevac", "VnoK", "Vswap"):
                    continue
                S.wait("t", tke)
                S.wait("t", t_rf)
                first_cs = (tt == 0 and sub == 0)
                last_cs = (tt == NTT - 1 and sub == 3)
                for g4 in range(4):
                    for j in range(4):
                        h = 4 * g4 + j
                        ob, half = h // 2, h % 2
                        lhs = KC[:, 512 * ob + 128 * sub: 512 * ob + 128 * (sub + 1)]
                        rf_t = RFP[:, 128 * half:128 * (half + 1)]
                        tp = S.tok("t", mm(PS[4][:, 128 * j:128 * (j + 1)], lhs, rf_t, True, True))
                    S.wait("v", tp)
                    tsin = trig_block(PS[4], 4,
                                      lambda half, g4=g4: KPT.rearrange("p (g f) -> p g f", f=128)
                                      [:, 4 * g4:4 * g4 + 4, 64 * half:64 * (half + 1)])
                    if _L0_STOP == "P":
                        continue
                    S.wait("t", tsin)
                    S.wait("t", tve)
                    for j in range(4):
                        h = 4 * g4 + j
                        bank, off = 5 + h // 7, 65 * (h % 7)
                        first = first_cs and (h % 7 == 0)
                        last = last_cs and (h in (6, 13, 15))
                        lhs = KPT[:, 128 * h:128 * (h + 1)]
                        S.raw("t", mm(PS[bank][:, off:off + 64], lhs,
                                      VTC[:, 1024 * sub + 64 * h: 1024 * sub + 64 * (h + 1)],
                                      first, False))
                        tkv = S.tok("t", mm(PS[bank][:, off + 64:off + 65], lhs,
                                            ONESCB[:, :], False, last))
        S.wait("g", S.now("t"))
        t_wo = load_w(WO, dram["Wo"])
        tkvx = None
        if _L0_STOP in ("", "B"):
            S.wait("s", tkv)
            for b, (lo, n) in enumerate([(0, 7), (7, 7), (14, 2)]):
                tkvx = S.tok("s", act(KVX[:, 65 * lo:65 * (lo + n)], PS[5 + b][:, 0:65 * n],
                                      AF.Copy, scale=1.0 / 128.0))

        if _L0_STOP == "B":
            globals()["_B_ONLY_TOK"] = tkvx
        if _L0_STOP == "":
            tx1 = None
            for tt in range(NTT):
                sl = slice(TT * tt, TT * (tt + 1))
                tqe = None
                for ob in range(NB):
                    S.wait("t", t_wq)
                    S.wait("t", tkvx)
                    S.wait("t", tx1)
                    if tqe is not None:
                        S.wait("t", tqe)
                    for kb in range(NB):
                        tk = S.tok("t", mm(PS[ob % 2][:, :], wsl(WQ, kb, ob), XLN(kb)[:, sl],
                                           kb == 0, kb == NB - 1))
                    S.wait("s", tk)
                    tqe = S.tok("s", act(QC[:, 512 * ob:512 * (ob + 1)], PS[ob % 2][:, :], AF.Copy))
                tae = None
                tx1_hist = []
                for g4 in range(4):
                    S.wait("t", tqe)
                    for j in range(4):
                        h = 4 * g4 + j
                        ob, half = h // 2, h % 2
                        rhs = QC[:, 512 * ob:512 * (ob + 1)]
                        rf_t = RFP[:, 128 * half:128 * (half + 1)]
                        tp = S.tok("t", mm(PS[2 + j][:, :], rf_t, rhs, True, True))
                    S.wait("v", tp)
                    tqp = None
                    for j in range(4):
                        tqp = trig_block(PS[2 + j], 1,
                                         lambda half, j=j: QPG[64 * half:64 * (half + 1),
                                                              512 * j:512 * (j + 1)],
                                         part_halves=True)
                    for j in range(4):
                        h = 4 * g4 + j
                        S.wait("t", tqp)
                        bank = 6 + (j % 2)
                        if tae is not None:
                            S.wait("t", tae)
                        tq = S.tok("t", mm(PS[bank][0:65, :], KVX[:, 65 * h:65 * (h + 1)],
                                           QPG[:, 512 * j:512 * (j + 1)], True, True))
                        S.wait("v", tq)
                        S.raw("v", vts(STG[0][64:65, :], PS[bank][64:65, :], 1e-6, None, OP.max))
                        S.sync_self("v")
                        tz = S.tok("v", lambda h_, b=bank: h_.reciprocal(
                            out=STG[1][64:65, :], in_=STG[0][64:65, :]))
                        S.wait("t", tz)
                        tzb = S.tok("t", mm(PS[j % 2][0:64, :], ONESR[64:65, 0:64],
                                            STG[1][64:65, :], True, True))
                        S.wait("s", tq)
                        S.wait("s", tz)
                        tse = S.tok("s", act(STGB[0][0:64, :], PS[bank][0:64, :], AF.Copy))
                        S.wait("v", tse)
                        S.wait("v", tzb)
                        ob, half = h // 2, h % 2
                        tae = S.tok("v", vtt(ATTC[64 * half:64 * (half + 1), 512 * ob:512 * (ob + 1)],
                                             STGB[0][0:64, :], PS[j % 2][0:64, :], OP.mult))
                for ob in range(NB):
                    S.wait("t", tae)
                    S.wait("t", t_wo)
                    if ob >= 2:
                        S.wait("t", tx1_hist[ob - 2])
                    for kb in range(NB):
                        tk = S.tok("t", mm(PS[2 + ob % 2][:, :], wsl(WO, kb, ob),
                                           ATTC[:, 512 * kb:512 * (kb + 1)], kb == 0, kb == NB - 1))
                    S.wait("v", tk)
                    S.wait("v", t_x)
                    tx1 = S.tok("v", vtt(XT(ob)[:, sl], XT(ob)[:, sl], PS[2 + ob % 2][:, :], OP.add))
                    tx1_hist.append(tx1)

            t_last = tx1
        if _L0_STOP != "":
            t_last = t_xln
    if _NPHASE == 0:
        t_last = t_xln

    if _NPHASE >= 2 and _L0_STOP == "":
        t_in = t_last

        t_eyeb = S.dma("g", EYEB[:, :], dram["eye16"])
        XB2 = GB[:, 0:NB * T]

        def xb2_view(i):
            return XB2[:, T * i:T * (i + 1)]

        S.wait("g", t_in)
        t_xb2 = S.dma("g", XB2.rearrange("p (i t) -> p i t", i=NB),
                      XA.rearrange("p (i t) -> p i t", i=NB))
        st2 = ln_stats(xb2_view, 0, 1, t_xb2, 0, 1)
        t_xln2 = ln_apply(xb2_view, XLN, 0, 1, st2, 2, 3)

        HR = GB[:, 8192:10240].bitcast(F32)
        R0WS = HR[:, 0:112]
        EYE = HR[:, 112:240]
        LTM = HR[:, 240:304]
        LTZ = HR[:, 304:368]
        RCOL = HR[:, 368:376]
        REXP = HR[:, 376:392]
        RSBN = HR[:, 392:406]
        GST = HR[:, 406:918]
        RRS = HR[0:1, 918:934]
        B2SV = GB[0:16, 0:1024]
        S.wait("y", t_xln2)
        t_r0 = S.dma("y", R0WS.rearrange("p (k c) -> p k c", k=NB),
                     dram["r0w"].rearrange("(k p) c -> p k c", p=128))
        t_eye = S.dma("y", EYE, dram["eye128"])
        S.wait("g", t_xln2)
        t_b2 = S.dma("g", B2SV[0:14, :], dram["l0b2"])
        t_b1all = None

        S.wait("t", t_r0)
        S.wait("t", S.now("v"))
        for kb in range(NB):
            trs = S.tok("t", mm(PS[5][0:14, 0:1], R0WS[:, 14 * kb:14 * (kb + 1)],
                                ONESR[:, 0:1], kb == 0, kb == NB - 1))
        S.wait("t", t_eye)
        S.wait("v", trs)
        S.raw("v", vcopy(GST[0:14, 0:1], PS[5][0:14, 0:1]))
        tev = S.tok("v", vcopy(GST[0:14, 1:15], EYE[0:14, 0:14]))
        S.wait("t", tev)
        trr = S.tok("t", mm(PS[6][0:1, 0:14], GST[0:14, 0:1], GST[0:14, 1:15], True, True))
        S.wait("v", trr)
        tn = S.tok("v", vts(RRS[0:1, 0:14], PS[6][0:1, 0:14], -1.0))
        S.wait("t", tn)
        trsb = S.tok("t", mm(PS[6][:, 0:14], ONESR[0:1, :], RRS[0:1, 0:14], True, True))
        S.wait("v", trsb)
        tn2 = S.tok("v", vcopy(RSBN, PS[6][:, 0:14]))

        t_selw = None
        memset_tok = S.tok("v", lambda h: h.memset(SELW[:, :], 0.0))
        for tt in range(NTT):
            sl = slice(TT * tt, TT * (tt + 1))
            S.wait("t", S.now("v"))
            for kb in range(NB):
                tg = S.tok("t", mm(PS[5][0:14, :], R0WS[:, 14 * kb:14 * (kb + 1)],
                                   XT(kb)[:, sl], kb == 0, kb == NB - 1))
            S.wait("v", tg)
            tgs = S.tok("v", vcopy(GST[0:14, :], PS[5][0:14, :]))
            S.wait("t", tgs)
            S.wait("t", st2)
            for tc in range(4):
                csl = slice(128 * tc, 128 * (tc + 1))
                S.raw("t", mm(PS[7][:, 16 * tc:16 * tc + 14],
                              GST[0:14, csl], EYE[0:14, 0:14], tc == 0, False))
                S.raw("t", mm(PS[7][:, 16 * tc + 14:16 * tc + 15],
                              ROW(0, slice(sl.start + csl.start, sl.start + csl.stop)),
                              EYE[0:1, 0:1], False, False))
                ttr = S.tok("t", mm(PS[7][:, 16 * tc + 15:16 * tc + 16],
                                    ROW(1, slice(sl.start + csl.start, sl.start + csl.stop)),
                                    EYE[0:1, 0:1], False, tc == 3))
            S.wait("v", ttr)
            S.wait("v", tn2)
            tswt = None
            for tc in range(4):
                o = 16 * tc
                S.wait("v", tswt)
                S.raw("v", vstt(LTM[:, 0:14], RSBN, PS[7][:, o + 14:o + 15],
                                PS[7][:, o:o + 14], OP.mult, OP.add))
                S.raw("v", vts(LTZ[:, 0:14], LTM[:, 0:14], PS[7][:, o + 15:o + 16], None, OP.mult))
                S.raw("v", lambda h, z=LTZ: h.tensor_reduce(
                    out=RCOL[:, 0:1], in_=z[:, 0:14], axis=mybir.AxisListType.X, op=OP.max))
                S.raw("v", vts(RCOL[:, 1:2], RCOL[:, 0:1], -1.0))
                tmask = S.tok("v", vts(LTM[:, 0:14], LTZ[:, 0:14], RCOL[:, 0:1], None, OP.is_ge))
                S.wait("s", tmask)
                texp = S.tok("s", act(REXP[:, 0:14], LTZ[:, 0:14], AF.Exp, bias=RCOL[:, 1:2]))
                S.wait("v", texp)
                S.raw("v", lambda h: h.tensor_reduce(
                    out=RCOL[:, 2:3], in_=REXP[:, 0:14], axis=mybir.AxisListType.X, op=OP.add))
                S.raw("v", lambda h: h.reciprocal(out=RCOL[:, 3:4], in_=RCOL[:, 2:3]))
                tswm = S.tok("v", vts(LTZ[:, 0:14], LTM[:, 0:14], RCOL[:, 3:4], None, OP.mult))
                S.wait("t", tswm)
                tswt = S.tok("t", mm(PS[4][0:14, 128 * tc:128 * (tc + 1)],
                                     LTZ[:, 0:14], EYE[:, :], tc == 0, tc == 3))
            S.wait("v", tswt)
            S.wait("v", memset_tok)
            t_selw = S.tok("v", vcopy(SELW[0:14, sl], PS[4][0:14, :]))

        S.wait("g", t_xln2)
        t_onesrow = S.dma("g", STGB[1][0:1, 0:128], dram["ones128"][0:1, :])

        tb2a_hist = [None, None]
        for tt in range(NTT):
            sl = slice(TT * tt, TT * (tt + 1))
            for ob in range(NB):
                S.wait("t", t_selw)
                S.wait("t", t_b2)
                S.wait("t", tb2a_hist[ob % 2])
                tb2m = S.tok("t", mm(PS[ob % 2][:, :], B2SV[0:14, 128 * ob:128 * (ob + 1)],
                                     SELW[0:14, sl], True, True))
                S.wait("v", tb2m)
                tb2a = S.tok("v", vtt(XT(ob)[:, sl], XT(ob)[:, sl], PS[ob % 2][:, :], OP.add))
                tb2a_hist[ob % 2] = tb2a
        S.wait("t", tb2a)

        W1S = AUX[:, 0:16384]
        W2A = AUX[:, 16384:24576]
        W2B = GB[:, 0:8192]
        HIDR = GB[:, 8192:16384]

        def w2sl(fb, ob):
            base = W2A if fb < 8 else W2B
            f = fb if fb < 8 else fb - 8
            return base[:, 1024 * f + 128 * ob: 1024 * f + 128 * (ob + 1)]

        t_w1 = t_w2 = None
        last_w1_use = last_w2_use = None
        hid_evac = [None, None]
        txa_hist = [None, None]
        t_mult = None
        tb2a_tok = tb2a
        for e in range(14):
            S.wait("g", last_w1_use)
            t_w1 = S.dma("g", W1S.rearrange("p (k c) -> p k c", k=NB),
                         dram["l0w1"][e].rearrange("(k p) c -> p k c", p=128))
            t_b1e = S.dma("g", B1E[:, :], dram["l0b1r"][e])
            S.wait("g", last_w2_use)
            if e == 0:
                S.wait("g", tb2a_tok)
                S.wait("g", t_selw)
            t_w2 = S.dma("g", W2A.rearrange("p (k c) -> p k c", k=8),
                         dram["l0w2"][e][0:1024].rearrange("(k p) c -> p k c", p=128))
            t_w2b = S.dma("g", W2B.rearrange("p (k c) -> p k c", k=8),
                          dram["l0w2"][e][1024:2048].rearrange("(k p) c -> p k c", p=128))
            for tt in range(NTT):
                sl = slice(TT * tt, TT * (tt + 1))
                S.wait("t", t_selw)
                S.wait("t", t_eyeb)
                trx = S.tok("t", mm(PS[5][0:1, :], EYEB[0:14, e:e + 1], SELW[0:14, sl],
                                    True, True))
                S.wait("s", trx)
                tsw = S.tok("s", act(STGB[0][0:1, :], PS[5][0:1, :], AF.Copy))
                S.wait("t", tsw)
                S.wait("t", t_onesrow)
                S.wait("t", t_mult)
                tswb = S.tok("t", mm(PS[4][:, :], STGB[1][0:1, 0:128],
                                     STGB[0][0:1, :], True, True))
                for fb in range(16):
                    S.wait("t", t_w1)
                    S.wait("t", hid_evac[fb % 2])
                    for kb in range(NB):
                        th = S.tok("t", mm(PS[fb % 2][:, :],
                                           W1S[:, 2048 * kb + 128 * fb: 2048 * kb + 128 * (fb + 1)],
                                           XLN(kb)[:, sl], kb == 0, kb == NB - 1))
                    S.wait("s", th)
                    S.wait("s", t_b1e)
                    S.wait("s", t_selw)
                    hid_evac[fb % 2] = S.tok("s", act(HIDR[:, 512 * fb:512 * (fb + 1)],
                                                      PS[fb % 2][:, :], AF.Gelu,
                                                      bias=B1E[:, fb:fb + 1]))
                for ob in range(NB):
                    S.wait("t", t_w2)
                    S.wait("t", t_w2b)
                    S.wait("t", hid_evac[1])
                    S.wait("t", txa_hist[ob % 2])
                    for fb in range(16):
                        tw = S.tok("t", mm(PS[2 + ob % 2][:, :], w2sl(fb, ob),
                                           HIDR[:, 512 * fb:512 * (fb + 1)],
                                           fb == 0, fb == 15))
                    S.wait("s", tw)
                    S.wait("s", t_mult)
                    tev2 = S.tok("s", act(STG[0][:, :], PS[2 + ob % 2][:, :], AF.Copy))
                    S.wait("v", tev2)
                    S.wait("v", tswb)
                    t_mult = S.tok("v", vtt(STG[1][:, :], STG[0][:, :], PS[4][:, :], OP.mult))
                    txa = S.tok("v", vtt(XT(ob)[:, sl], XT(ob)[:, sl], STG[1][:, :], OP.add))
                    txa_hist[ob % 2] = txa
                last_w1_use = S.now("t")
            last_w2_use = S.now("t")
        S.wait("t", txa)
        t_last = txa

    if _NPHASE >= 3 and _L0_STOP == "":
        t_in = t_last
        CH = 256
        NCH = T // CH

        XB3 = GB[:, 0:NB * T]

        def xb3_view(i):
            return XB3[:, T * i:T * (i + 1)]

        S.wait("g", t_in)
        t_xb3 = S.dma("g", XB3.rearrange("p (i t) -> p i t", i=NB),
                      XA.rearrange("p (i t) -> p i t", i=NB))
        st3 = ln_stats(xb3_view, 0, 1, t_xb3, 0, 1)
        t_xln3 = ln_apply(xb3_view, XLN, 0, 1, st3, 2, 3)

        S.wait("y", t_xln3)
        t_spill = S.dma("y", dram["x2_spill"].rearrange("(i p) t -> p i t", p=128),
                        XA.rearrange("p (i t) -> p i t", i=NB))
        XAB = XA[:, :].bitcast(BF16)

        st_in = ln_stats(XLN, 0, 1, t_xln3, 0, 1)

        t_ey128b = S.dma("g", EYE128B[:, :], dram["eye128"])
        t_bcsel = S.dma("g", SELW[:, :], dram["sel128"][0:16, :])
        t_nrm = S.dma("g", NRM[:, :], dram["nrmpack"])
        mz = S.tok("v", lambda h: h.memset(STGB[1][:, :], 0.0))

        XSR = GB[:, 0:4096]
        XSIL = GB[:, 4096:8192]
        SZR = GB[:, 8192:12288]
        DTH = GB[:, 12288:16384]
        OUT_W = AUX[:, 0:16384]
        XPJ = AUX[:, 16384:17920]
        DTW = AUX[:, 17920:19968]

        def xsr_t(db):
            return XSR[:, 256 * db:256 * (db + 1)]

        def xsil_t(db):
            return XSIL[:, 256 * db:256 * (db + 1)]

        def sz_t(db):
            return SZR[:, 256 * db:256 * (db + 1)]

        def dt_t(db):
            return DTH[:, 256 * db:256 * (db + 1)]

        def g1s_t(kb):
            return DTH[:, 256 * kb:256 * (kb + 1)]

        gsp = nc.dram_tensor("g1_spill", [D, T], BF16, kind="Internal").ap()

        def load_mamba_weights(mi, gate_tok):
            S.wait("g", gate_tok)
            S.wait("g", S.now("v"))
            S.wait("g", S.now("s"))
            S.dma("g", XAB.rearrange("p (k c) -> p k c", k=NB),
                  dram[mi + "_inwT"].rearrange("(k p) c -> p k c", p=128))
            S.dma("g", OUT_W.rearrange("p (k c) -> p k c", k=16),
                  dram[mi + "_outwT"].rearrange("(k p) c -> p k c", p=128))
            S.dma("g", XPJ.rearrange("p (k c) -> p k c", k=16),
                  dram[mi + "_xprojT"].rearrange("(k p) c -> p k c", p=128))
            S.dma("g", DTW[:, :], dram[mi + "_dtw128"])
            S.dma("g", MCOL[:, :], dram[mi + "_colpack"])
            S.wait("y", gate_tok)
            S.wait("y", S.now("v"))
            S.wait("y", S.now("s"))
            t_al = S.dma("y", STG[0][:, 0:256], dram[mi + "_Alogr"])
            S.wait("s", t_al)
            tn1 = S.tok("s", act(STG[1][:, 0:256], STG[0][:, 0:256], AF.Exp))
            S.wait("v", tn1)
            t_nega = S.tok("v", vts(NEGA[:, :], STG[1][:, 0:256], -1.0))
            return S.now("dg"), t_nega

        def emit_mamba(rev, t_w, t_nega):
            t_y_done = None
            t_sp_prev = None
            t_xi1 = None
            mzc = S.tok("v", lambda h: h.memset(CARR[:, :], 0.0))
            mzh = S.tok("v", lambda h: h.memset(XSH[:, :], 0.0))
            for c in range(NCH):
                isl = slice(CH * c, CH * (c + 1))
                osl = slice(T - CH * (c + 1), T - CH * c)
                rsl = osl if rev else isl

                def xsrc(kb):
                    a = XLN(kb)[:, rsl]
                    return a[:, ::-1] if rev else a

                S.wait("t", st_in)
                S.wait("t", t_y_done)
                S.raw("t", mm(PS[3][:, 0:256], ONESR[0:1, :], ROW(0, rsl), True, False))
                tbc = S.tok("t", mm(PS[3][:, 256:512], ONESR[0:1, :], ROW(1, rsl), False, True))
                mu_b = PS[3][:, 0:256][:, ::-1] if rev else PS[3][:, 0:256]
                iv_b = PS[3][:, 256:512][:, ::-1] if rev else PS[3][:, 256:512]
                th = None
                for kb in range(NB):
                    S.wait("v", tbc)
                    S.wait("v", t_xln3)
                    S.wait("v", t_sp_prev)
                    S.wait("v", t_nrm)
                    S.raw("v", vtt(STGB[0][:, 0:256], xsrc(kb), mu_b, OP.subtract))
                    S.raw("v", vtt(STGB[0][:, 0:256], STGB[0][:, 0:256], iv_b, OP.mult))
                    th = S.tok("v", vts(DTH[:, 256 * kb:256 * (kb + 1)], STGB[0][:, 0:256],
                                        NRM[:, kb:kb + 1], NRM[:, 8 + kb:9 + kb],
                                        OP.mult, OP.add))
                tev = {}
                for eb in range(32):
                    S.wait("t", th)
                    S.wait("t", t_w)
                    if eb >= 2:
                        S.wait("t", tev[eb - 2])
                    for kb in range(NB):
                        tm = S.tok("t", mm(PS[eb % 2][:, 0:256],
                                           XAB[:, 4096 * kb + 128 * eb:4096 * kb + 128 * (eb + 1)],
                                           DTH[:, 256 * kb:256 * (kb + 1)],
                                           kb == 0, kb == NB - 1))
                    S.wait("s", tm)
                    if eb < 16:
                        tev[eb] = S.tok("s", act(xsr_t(eb), PS[eb % 2][:, 0:256], AF.Copy))
                    else:
                        tev[eb] = S.tok("s", act(sz_t(eb - 16), PS[eb % 2][:, 0:256], AF.Silu))
                t_conv = None
                for db in range(16):
                    S.wait("v", tev[db])
                    S.wait("v", mzh)
                    S.wait("v", t_conv)
                    S.raw("v", vts(STG[0][:, 0:256], xsr_t(db), MCOL[:, 4 * db + 3:4 * db + 4]))
                    a, b = STG[0], STG[1]
                    for k in (2, 1, 0):
                        off = 3 - k
                        S.raw("v", vstt(b[:, off:256], xsr_t(db)[:, 0:256 - off],
                                        MCOL[:, 4 * db + k:4 * db + k + 1],
                                        a[:, off:256], OP.mult, OP.add))
                        S.raw("v", vstt(b[:, 0:off], XSH[:, 3 * db + 3 - off:3 * db + 3],
                                        MCOL[:, 4 * db + k:4 * db + k + 1],
                                        a[:, 0:off], OP.mult, OP.add))
                        a, b = b, a
                    tcv = S.tok("v", vcopy(XSH[:, 3 * db:3 * db + 3], xsr_t(db)[:, 253:256]))
                    S.wait("s", tcv)
                    t_conv = S.tok("s", act(xsil_t(db), a[:, 0:256], AF.Silu,
                                            bias=MCOL[:, 64 + db:65 + db]))
                S.wait("t", t_conv)
                for kb in range(16):
                    tp = S.tok("t", mm(PS[2][0:96, 0:256], XPJ[:, 96 * kb:96 * (kb + 1)],
                                       xsil_t(kb), kb == 0, kb == 15))
                S.wait("s", tp)
                S.wait("s", mz)
                tdbl = S.tok("s", act(STGB[1][0:96, 0:256], PS[2][0:96, 0:256], AF.Copy))
                t_dtev = {}
                for mb in range(16):
                    S.wait("t", tdbl)
                    if mb >= 2:
                        S.wait("t", t_dtev[mb - 2])
                    tdm = S.tok("t", mm(PS[mb % 2][:, 0:256], DTW[:, 128 * mb:128 * (mb + 1)],
                                        STGB[1][:, 0:256], True, True))
                    S.wait("s", tdm)
                    if mb >= 1:
                        S.wait("s", t_dtev[mb - 1])
                    S.raw("s", act(STG[0][:, 0:256], PS[mb % 2][:, 0:256], AF.Abs,
                                   bias=MCOL[:, 80 + mb:81 + mb]))
                    S.raw("s", act(STG[1][:, 0:256], STG[0][:, 0:256], AF.Exp, scale=-1.0))
                    S.raw("s", act(STG[0][:, 0:256], STG[1][:, 0:256], AF.Ln, bias=1.0))
                    tr1 = S.tok("s", act(STG[1][:, 0:256], PS[mb % 2][:, 0:256], AF.Relu,
                                         bias=MCOL[:, 80 + mb:81 + mb]))
                    S.wait("v", tr1)
                    t_dtev[mb] = S.tok("v", vtt(dt_t(mb), STG[0][:, 0:256],
                                                STG[1][:, 0:256], OP.add))
                t_dt_last = t_dtev[15]
                S.wait("t", t_dt_last)
                S.wait("t", t_ey128b)
                S.raw("t", mm(PS[2][0:16, 0:256], EYE128B[:, 64:80], STGB[1][:, 0:256],
                              True, False))
                trx = S.tok("t", mm(PS[2][0:16, 256:512], EYE128B[:, 80:96],
                                    STGB[1][:, 0:256], False, True))
                S.wait("s", trx)
                tbcr = S.tok("s", act(BCR[0:16, :], PS[2][0:16, :], AF.Copy))
                t_da = [None, None]
                t_scan = [None, None]
                t_hc_done = None
                for sg in range(2):
                    S.wait("t", tbcr)
                    S.wait("t", t_bcsel)
                    S.wait("t", t_hc_done)
                    tsb = None
                    for j in range(8):
                        s_ = 8 * sg + j
                        bank, half = j // 2, j % 2
                        S.raw("t", mm(PS[bank][:, 256 * half:256 * (half + 1)],
                                      SELW[0:16, 128 * s_:128 * (s_ + 1)],
                                      BCR[0:16, 0:256], half == 0, half == 1))
                        tsb = S.tok("t", mm(PS[4 + bank][:, 256 * half:256 * (half + 1)],
                                            SELW[0:16, 128 * s_:128 * (s_ + 1)],
                                            BCR[0:16, 256:512], half == 0, half == 1))
                    for db in range(16):
                        S.wait("v", tsb)
                        S.wait("v", t_dt_last)
                        S.wait("v", t_nega)
                        S.raw("v", vtt(STGB[0][:, 0:256], dt_t(db), xsil_t(db), OP.mult))
                        for j in range(8):
                            s_ = 8 * sg + j
                            bank, half = j // 2, j % 2
                            bb = PS[bank][:, 256 * half:256 * (half + 1)]
                            cb = PS[4 + bank][:, 256 * half:256 * (half + 1)]
                            par = j % 2
                            S.wait("s", t_scan[par])
                            t_da[par] = S.tok("s", act(DA[par][:, 0:256], dt_t(db), AF.Exp,
                                                       scale=NEGA[:, 16 * db + s_:16 * db + s_ + 1]))
                            S.wait("v", t_da[par])
                            S.raw("v", vtt(DB[:, 0:256], STGB[0][:, 0:256], bb, OP.mult))
                            S.wait("v", mzc)
                            S.raw("v", lambda h, p=par, ix=16 * db + s_: h.tensor_tensor_scan(
                                out=HH[:, 0:256], data0=DA[p][:, 0:256], data1=DB[:, 0:256],
                                initial=CARR[:, ix:ix + 1], op0=OP.mult, op1=OP.add))
                            S.raw("v", vcopy(CARR[:, 16 * db + s_:16 * db + s_ + 1],
                                             HH[:, 255:256]))
                            t_scan[par] = S.tok("v", vtt(STGB[0][:, 256:512], HH[:, 0:256],
                                                         cb, OP.mult))
                            if j == 0:
                                S.raw("v", vcopy(STG[1][:, 0:256].bitcast(BF16)[:, 0:256],
                                                 STGB[0][:, 256:512]))
                            else:
                                S.raw("v", vtt(STG[1][:, 0:256].bitcast(BF16)[:, 0:256],
                                               STG[1][:, 0:256].bitcast(BF16)[:, 0:256],
                                               STGB[0][:, 256:512], OP.add))
                        if sg == 0:
                            t_hc_done = S.tok("v", vcopy(xsr_t(db), STG[1][:, 0:256].bitcast(BF16)[:, 0:256]))
                        else:
                            t_hc_done = S.tok("v", vtt(xsr_t(db), xsr_t(db),
                                                       STG[1][:, 0:256].bitcast(BF16)[:, 0:256], OP.add))
                ty2 = None
                for db in range(16):
                    S.wait("v", t_hc_done)
                    S.raw("v", vstt(STG[0][:, 0:256], xsil_t(db),
                                    MCOL[:, 96 + db:97 + db], xsr_t(db), OP.mult, OP.add))
                    ty2 = S.tok("v", vtt(xsr_t(db), STG[0][:, 0:256], sz_t(db), OP.mult))
                if rev:
                    S.wait("g", ty2)
                    tg1l = S.dma("g", DTH[:, 0:2048].rearrange("p (k t) -> p k t", k=NB),
                                 gsp.rearrange("(k p) t -> p k t", p=128)[:, :, osl])
                t_go = None
                for mb2 in range(NB):
                    S.wait("t", ty2)
                    if t_go is not None:
                        S.wait("t", t_go)
                    for db in range(16):
                        tom = S.tok("t", mm(PS[mb2 % 2][:, 0:256],
                                            OUT_W[:, 1024 * db + 128 * mb2:1024 * db + 128 * (mb2 + 1)],
                                            xsr_t(db), db == 0, db == 15))
                    S.wait("v", tom)
                    if not rev:
                        S.wait("v", t_xln3)
                        t_go = S.tok("v", vtt(g1s_t(mb2), XLN(mb2)[:, isl],
                                              PS[mb2 % 2][:, 0:256], OP.add))
                    else:
                        S.wait("v", tg1l)
                        tga = S.tok("v", vtt(STG[0][:, 0:256], g1s_t(mb2),
                                             PS[mb2 % 2][:, 0:256][:, ::-1], OP.add))
                        S.wait("s", tga)
                        tge = S.tok("s", act(STG[1][:, 0:256], STG[0][:, 0:256], AF.Gelu))
                        S.wait("v", tge)
                        t_go = S.tok("v", vtt(XLN(mb2)[:, osl], XLN(mb2)[:, osl],
                                              STG[1][:, 0:256], OP.add))
                        t_xi1 = t_go
                if not rev:
                    S.wait("g", t_go)
                    t_sp_prev = S.dma("g", gsp.rearrange("(k p) t -> p k t", p=128)[:, :, isl],
                                      DTH[:, 0:2048].rearrange("p (k t) -> p k t", k=NB))
                t_y_done = t_go
            return t_xi1, S.now("t")

        tw1, tn1 = load_mamba_weights("m1", t_spill)
        _xx, t_m1t = emit_mamba(False, tw1, tn1)
        tw2, tn2_ = load_mamba_weights("m2", t_m1t)
        t_xi1, t_m2t = emit_mamba(True, tw2, tn2_)

        st4 = ln_stats(XLN, 0, 1, t_xi1, 0, 1)
        XFF = GB[:, 0:NB * T]

        def xff_t(kb):
            return XFF[:, 2048 * kb:2048 * (kb + 1)]

        t_xff = None
        for tt in range(NTT):
            sl = slice(TT * tt, TT * (tt + 1))
            S.wait("t", S.now("v"))
            tb1 = bcast_row(0, sl, 2, st4)
            tb2 = bcast_row(1, sl, 3, st4)
            for kb in range(NB):
                S.wait("v", tb2)
                S.raw("v", vtt(STGB[0][:, :], XLN(kb)[:, sl], PS[2][:, :], OP.subtract))
                S.raw("v", vtt(STGB[0][:, :], STGB[0][:, :], PS[3][:, :], OP.mult))
                t_xff = S.tok("v", vts(xff_t(kb)[:, sl], STGB[0][:, :],
                                       NRM[:, 16 + kb:17 + kb], NRM[:, 24 + kb:25 + kb],
                                       OP.mult, OP.add))
        FF1Q = AUX[:, 0:8192]
        FF2Q = AUX[:, 8192:16384]
        HIDQ = XAB[:, 0:2048]
        ACC2 = XAB[:, 8192:24576]

        def acc2_t(ob):
            return ACC2[:, 2048 * ob:2048 * (ob + 1)]

        t_ff1b = S.dma("g", FFB[:, :], dram["ffbpack"])
        t_q = None
        t_acc = None
        for q in range(4):
            S.wait("g", t_m2t)
            if q > 0:
                S.wait("g", t_q)
            t_w1q = S.dma("g", FF1Q.rearrange("p (k c) -> p k c", k=NB),
                          dram["ff1T"][:, 1024 * q:1024 * (q + 1)].rearrange(
                              "(k p) c -> p k c", p=128))
            t_w2q = S.dma("g", FF2Q.rearrange("p (k c) -> p k c", k=NB),
                          dram["ff2T"][1024 * q:1024 * (q + 1), :].rearrange(
                              "(k p) c -> p k c", p=128))
            for tq in range(8):
                qsl = slice(256 * tq, 256 * (tq + 1))
                thev = {}
                for hb in range(8):
                    S.wait("t", t_w1q)
                    S.wait("t", t_xff)
                    S.wait("t", t_acc)
                    if hb >= 2:
                        S.wait("t", thev[hb - 2])
                    for kb in range(NB):
                        tm = S.tok("t", mm(PS[hb % 2][:, 0:256],
                                           FF1Q[:, 1024 * kb + 128 * hb:1024 * kb + 128 * (hb + 1)],
                                           xff_t(kb)[:, qsl], kb == 0, kb == NB - 1))
                    S.wait("s", tm)
                    S.wait("s", t_ff1b)
                    thev[hb] = S.tok("s", act(HIDQ[:, 256 * hb:256 * (hb + 1)],
                                              PS[hb % 2][:, 0:256], AF.Gelu,
                                              bias=FFB[:, 8 * q + hb:8 * q + hb + 1]))
                for ob in range(NB):
                    S.wait("t", t_w2q)
                    S.wait("t", thev[7])
                    if ob >= 2:
                        S.wait("t", t_acc)
                    for ht in range(8):
                        tm2 = S.tok("t", mm(PS[2 + ob % 2][:, 0:256],
                                            FF2Q[:, 1024 * ht + 128 * ob:1024 * ht + 128 * (ob + 1)],
                                            HIDQ[:, 256 * ht:256 * (ht + 1)],
                                            ht == 0, ht == 7))
                    if q == 0:
                        S.wait("s", tm2)
                        t_acc = S.tok("s", act(acc2_t(ob)[:, qsl], PS[2 + ob % 2][:, 0:256],
                                               AF.Copy))
                    else:
                        S.wait("v", tm2)
                        t_acc = S.tok("v", vtt(acc2_t(ob)[:, qsl], acc2_t(ob)[:, qsl],
                                               PS[2 + ob % 2][:, 0:256], OP.add))
            t_q = S.now("t")
        t_xi2 = None
        for ob in range(NB):
            S.wait("v", t_acc)
            t_xi2 = S.tok("v", vstt(XLN(ob)[:, :], acc2_t(ob), NRM[:, 28]
                          if False else FFB[:, 32 + ob:33 + ob], XLN(ob)[:, :],
                          OP.add, OP.add))
        S.wait("y", t_q)
        S.wait("y", t_xi2)
        t_rel = S.dma("y", XA.rearrange("p (i t) -> p i t", i=NB),
                      dram["x2_spill"].rearrange("(i p) t -> p i t", p=128))
        t_x3 = None
        for ob in range(NB):
            S.wait("v", t_rel)
            S.wait("v", t_xi2)
            t_x3 = S.tok("v", vtt(XT(ob), XT(ob), XLN(ob)[:, :], OP.add))
        t_last = t_x3

    if _NPHASE >= 4 and _L0_STOP == "":
        t_in = t_last

        XB4 = GB[:, 0:NB * T]

        def xb4_view(i):
            return XB4[:, T * i:T * (i + 1)]

        S.wait("g", t_in)
        t_xb4 = S.dma("g", XB4.rearrange("p (i t) -> p i t", i=NB),
                      XA.rearrange("p (i t) -> p i t", i=NB))
        st4b = ln_stats(xb4_view, 0, 1, t_xb4, 0, 1)
        t_xln4 = ln_apply(xb4_view, XLN, 0, 1, st4b, 2, 3)

        HR2 = GB[:, 8192:10240].bitcast(F32)
        WRS = HR2[:, 0:64]
        EYE2 = HR2[:, 64:192]
        LTM2 = HR2[:, 192:208]
        LTZ2 = HR2[:, 208:224]
        RC2 = HR2[:, 224:232]
        RE2 = HR2[:, 232:248]
        RSBN2 = HR2[:, 248:256]
        GST2 = HR2[:, 256:768]
        RRS2 = HR2[0:1, 768:784]
        S.wait("y", t_xln4)
        t_wr = S.dma("y", WRS.rearrange("p (k c) -> p k c", k=NB),
                     dram["Wr"].rearrange("(k p) c -> p k c", p=128))
        t_eye2 = S.dma("y", EYE2, dram["eye128"])
        S.wait("g", t_xln4)
        t_or = S.dma("g", STGB[1][0:1, 0:128], dram["ones128"][0:1, :])

        S.wait("t", t_wr)
        S.wait("t", S.now("v"))
        for kb in range(NB):
            trs = S.tok("t", mm(PS[5][0:8, 0:1], WRS[:, 8 * kb:8 * (kb + 1)],
                                ONESR[:, 0:1], kb == 0, kb == NB - 1))
        S.wait("t", t_eye2)
        S.wait("v", trs)
        S.raw("v", vcopy(GST2[0:8, 0:1], PS[5][0:8, 0:1]))
        tev4 = S.tok("v", vcopy(GST2[0:8, 1:9], EYE2[0:8, 0:8]))
        S.wait("t", tev4)
        trr = S.tok("t", mm(PS[6][0:1, 0:8], GST2[0:8, 0:1], GST2[0:8, 1:9], True, True))
        S.wait("v", trr)
        tn = S.tok("v", vts(RRS2[0:1, 0:8], PS[6][0:1, 0:8], -1.0))
        S.wait("t", tn)
        trsb = S.tok("t", mm(PS[6][:, 0:8], ONESR[0:1, :], RRS2[0:1, 0:8], True, True))
        S.wait("v", trsb)
        tn2b = S.tok("v", vcopy(RSBN2, PS[6][:, 0:8]))

        t_selw4 = None
        memset4 = S.tok("v", lambda h: h.memset(SELW[:, :], 0.0))
        for tt in range(NTT):
            sl = slice(TT * tt, TT * (tt + 1))
            S.wait("t", S.now("v"))
            for kb in range(NB):
                tg = S.tok("t", mm(PS[5][0:8, :], WRS[:, 8 * kb:8 * (kb + 1)],
                                   XT(kb)[:, sl], kb == 0, kb == NB - 1))
            S.wait("v", tg)
            tgs = S.tok("v", vcopy(GST2[0:8, :], PS[5][0:8, :]))
            S.wait("t", tgs)
            S.wait("t", st4b)
            for tc in range(4):
                csl = slice(128 * tc, 128 * (tc + 1))
                S.raw("t", mm(PS[7][:, 16 * tc:16 * tc + 8],
                              GST2[0:8, csl], EYE2[0:8, 0:8], tc == 0, False))
                S.raw("t", mm(PS[7][:, 16 * tc + 14:16 * tc + 15],
                              ROW(0, slice(sl.start + csl.start, sl.start + csl.stop)),
                              EYE2[0:1, 0:1], False, False))
                ttr = S.tok("t", mm(PS[7][:, 16 * tc + 15:16 * tc + 16],
                                    ROW(1, slice(sl.start + csl.start, sl.start + csl.stop)),
                                    EYE2[0:1, 0:1], False, tc == 3))
            S.wait("v", ttr)
            S.wait("v", tn2b)
            tswt = None
            for tc in range(4):
                o = 16 * tc
                S.wait("v", tswt)
                S.raw("v", vstt(LTM2[:, 0:8], RSBN2, PS[7][:, o + 14:o + 15],
                                PS[7][:, o:o + 8], OP.mult, OP.add))
                S.raw("v", vts(LTZ2[:, 0:8], LTM2[:, 0:8], PS[7][:, o + 15:o + 16], None, OP.mult))
                S.raw("v", lambda h: h.tensor_reduce(
                    out=RC2[:, 0:1], in_=LTZ2[:, 0:8], axis=mybir.AxisListType.X, op=OP.max))
                S.raw("v", vts(RC2[:, 1:2], RC2[:, 0:1], -1.0))
                S.raw("v", vts(LTM2[:, 0:8], LTZ2[:, 0:8], RC2[:, 0:1], None, OP.is_ge))
                S.raw("v", vts(LTM2[:, 8:16], LTM2[:, 0:8], -1e30, None, OP.mult))
                S.raw("v", vtt(LTM2[:, 8:16], LTZ2[:, 0:8], LTM2[:, 8:16], OP.add))
                S.raw("v", lambda h: h.tensor_reduce(
                    out=RC2[:, 4:5], in_=LTM2[:, 8:16], axis=mybir.AxisListType.X, op=OP.max))
                tmask = S.tok("v", vts(LTM2[:, 0:8], LTZ2[:, 0:8], RC2[:, 4:5], None, OP.is_ge))
                S.wait("s", tmask)
                texp = S.tok("s", act(RE2[:, 0:8], LTZ2[:, 0:8], AF.Exp, bias=RC2[:, 1:2]))
                S.wait("v", texp)
                S.raw("v", lambda h: h.tensor_reduce(
                    out=RC2[:, 2:3], in_=RE2[:, 0:8], axis=mybir.AxisListType.X, op=OP.add))
                S.raw("v", lambda h: h.reciprocal(out=RC2[:, 3:4], in_=RC2[:, 2:3]))
                S.raw("v", vts(RE2[:, 0:8], RE2[:, 0:8], RC2[:, 3:4], None, OP.mult))
                tswm = S.tok("v", vtt(LTZ2[:, 0:8], LTM2[:, 0:8], RE2[:, 0:8], OP.mult))
                S.wait("t", tswm)
                tswt = S.tok("t", mm(PS[4][0:8, 128 * tc:128 * (tc + 1)],
                                     LTZ2[:, 0:8], EYE2[:, :], tc == 0, tc == 3))
            S.wait("v", tswt)
            S.wait("v", memset4)
            t_selw4 = S.tok("v", vcopy(SELW[0:8, sl], PS[4][0:8, :]))

        CAP = float(globals().get('_CAP_OVERRIDE', 5120.0))
        cc_in = nc.dram_tensor("cc_in", [8, T], F32, kind="Internal").ap()
        cc_out = nc.dram_tensor("cc_out", [8, 8, T], F32, kind="Internal").ap()
        S.wait("g", t_selw4)
        t_ccw = S.dma("g", cc_in, SELW[0:8, :])
        S.wait("g", t_ccw)
        t_cc = S.tok("g", lambda h: h.collective_compute(
            "AllGather", OP.bypass, replica_groups=[list(range(8))],
            ins=[cc_in.opt()], outs=[cc_out.opt()]))
        GATES = GB[:, 0:2048].bitcast(F32)
        CBIS = GB[:, 2048:2176].bitcast(F32)
        LO = CBIS[:, 0:8]
        CAND = CBIS[:, 8:16]
        CNT = CBIS[:, 16:24]
        MSKC = GB[:, 2176:2304].bitcast(F32)
        S.wait("g", t_cc)
        for e in range(8):
            for r in range(8):
                t_g8 = S.dma("g", GATES[16 * r:16 * (r + 1), 128 * e:128 * (e + 1)],
                             cc_out[r, e, :].rearrange("(a f) -> a f", f=128))
        tz = S.tok("v", lambda h: h.memset(CBIS[:, 0:24], 0.0))
        t_bis = None
        for it in range(1, 51):
            step = 2.0 ** (-it)
            S.wait("v", t_g8)
            S.wait("v", tz)
            S.wait("v", t_bis)
            S.raw("v", vts(CAND, LO, step, None, OP.add))
            for e in range(8):
                S.raw("v", vts(MSKC[:, 0:64] if False else STG[0][:, 0:128],
                               GATES[:, 128 * e:128 * (e + 1)], CAND[:, e:e + 1],
                               None, OP.is_ge))
                S.raw("v", lambda h, e=e: h.tensor_reduce(
                    out=CNT[:, e:e + 1], in_=STG[0][:, 0:128],
                    axis=mybir.AxisListType.X, op=OP.add))
            tcn = S.tok("v")
            S.wait("t", tcn)
            tcm = S.tok("t", mm(PS[5][0:1, 0:8], ONESR[:, 0:1], CNT, True, True))
            S.wait("v", tcm)
            tsl = S.tok("v", vts(STG[0][0:1, 0:8], PS[5][0:1, 0:8], CAP, None, OP.is_ge))
            S.wait("t", tsl)
            tsb5 = S.tok("t", mm(PS[5][:, 8:16], ONESR[0:1, :], STG[0][0:1, 0:8],
                                 True, True))
            S.wait("v", tsb5)
            t_bis = S.tok("v", vstt(LO, PS[5][:, 8:16], step, LO, OP.mult, OP.add))
        S.wait("t", t_bis)
        S.wait("t", t_eye2)
        ttau = S.tok("t", mm(PS[5][0:8, 16:17], LO[0:1, 0:8], EYE2[0:1, 0:1], True, True))
        S.wait("v", ttau)
        MSK8 = GB[0:8, 4096:6144]
        S.raw("v", vts(MSK8, SELW[0:8, :], PS[5][0:8, 16:17], None, OP.is_ge))
        t_capm = S.tok("v", vtt(SELW[0:8, :], SELW[0:8, :], MSK8, OP.mult))
        t_selw4 = t_capm

        S.wait("y", t_in)
        t_init = S.dma("y", out_fm.rearrange("(i p) t -> p i t", p=128),
                       XA.rearrange("p (i t) -> p i t", i=NB))
        XAB4 = XA[:, :].bitcast(BF16)

        def w2t(j):
            if j < 24:
                return AUX[:, 1024 * j:1024 * (j + 1)]
            return GB[:, 8192 + 1024 * (j - 24):8192 + 1024 * (j - 23)]

        HID4 = GB[:, 0:8192]
        t_w1 = t_w2 = None
        last_w1_use = last_w2_use = None
        t_acc_tok = None
        last_da = None
        for e in range(8):
            S.wait("g", t_init)
            S.wait("g", t_selw4)
            S.wait("g", S.now("t"))
            S.wait("g", last_w1_use)
            t_w1 = S.dma("g", XAB4.rearrange("p (k c) -> p k c", k=NB),
                         dram["W1"][e].rearrange("(k p) c -> p k c", p=128))
            S.wait("g", last_w2_use)
            S.wait("g", t_selw4 if e == 0 else None)
            t_w2 = S.dma("g", AUX[:, 0:24576].rearrange("p (k c) -> p k c", k=24),
                         dram["W2"][e][0:3072].rearrange("(k p) c -> p k c", p=128))
            t_w2b = S.dma("g", GB[:, 8192:16384].rearrange("p (k c) -> p k c", k=8),
                          dram["W2"][e][3072:4096].rearrange("(k p) c -> p k c", p=128))
            for tq in range(8):
                qsl = slice(256 * tq, 256 * (tq + 1))
                S.wait("t", t_selw4)
                S.wait("t", t_eyeb)
                S.wait("t", t_acc_tok)
                trx4 = S.tok("t", mm(PS[6][0:1, 0:256], EYEB[0:8, e:e + 1],
                                     SELW[0:8, qsl], True, True))
                S.wait("s", trx4)
                tsw4 = S.tok("s", act(STGB[0][0:1, 0:256], PS[6][0:1, 0:256], AF.Copy))
                S.wait("t", tsw4)
                S.wait("t", t_or)
                tswb4 = S.tok("t", mm(PS[6][:, 0:256], STGB[1][0:1, 0:128],
                                      STGB[0][0:1, 0:256], True, True))
                S.wait("s", tswb4)
                tse4 = S.tok("s", act(STGB[1][:, 256:512], PS[6][:, 0:256], AF.Copy))
                thv = {}
                for hb in range(32):
                    S.wait("t", t_w1)
                    if hb >= 2:
                        S.wait("t", thv[hb - 2])
                    for kb in range(NB):
                        tm = S.tok("t", mm(PS[4 + hb % 2][:, 0:256],
                                           XAB4[:, 4096 * kb + 128 * hb:4096 * kb + 128 * (hb + 1)],
                                           XLN(kb)[:, qsl], kb == 0, kb == NB - 1))
                    S.wait("s", tm)
                    thv[hb] = S.tok("s", act(HID4[:, 256 * hb:256 * (hb + 1)],
                                             PS[4 + hb % 2][:, 0:256], AF.Relu))
                for ob in range(NB):
                    S.wait("t", t_w2)
                    S.wait("t", t_w2b)
                    S.wait("t", thv[31])
                    S.wait("t", t_acc_tok)
                    bank, half = ob // 2, ob % 2
                    for j in range(32):
                        tm2 = S.tok("t", mm(PS[bank][:, 256 * half:256 * (half + 1)],
                                            w2t(j)[:, 128 * ob:128 * (ob + 1)],
                                            HID4[:, 256 * j:256 * (j + 1)],
                                            j == 0, j == 31))
                    S.wait("v", tm2)
                    S.wait("v", tse4)
                    S.wait("v", last_da)
                    tgt = S.tok("v", vtt(STG[ob % 2][:, 0:256],
                                         PS[bank][:, 256 * half:256 * (half + 1)],
                                         STGB[1][:, 256:512], OP.mult))
                    S.wait("g", tgt)
                    last_da = S.dma("g", out_fm[128 * ob:128 * (ob + 1), qsl],
                                    STG[ob % 2][:, 0:256], sem="da", accum_op=OP.add)
                    t_acc_tok = tgt
                last_w1_use = S.now("t")
            last_w2_use = S.now("t")
        S.wait("y", S.now("dg"))
        t_last = None

    if _DEBUG:
        S.wait("y", t_last)
        t_dbg = S.dma("y", dram["dbg_x1"].rearrange("(i p) t -> p i t", p=128),
                      XA.rearrange("p (i t) -> p i t", i=NB))
        S.wait("y", t_dbg)

    if _NPHASE >= 4 and _L0_STOP == "":
        S.wait("g", S.now("da"))
        S.wait("g", S.now("dg"))
        S.wait("y", S.now("da"))
    else:
        S.wait("y", t_last)
        t_out = S.dma("y", out_fm.rearrange("(i p) t -> p i t", p=128),
                      XA.rearrange("p (i t) -> p i t", i=NB))
        S.wait("y", t_out)
        S.wait("g", S.now("dg"))

    with nc.Block() as block:
        def emit(h, lst):
            for kind, *rest in lst:
                if kind == "wait":
                    key, c = rest
                    h.wait_ge(sems[key], c)
                else:
                    fn, key, n = rest
                    inst = fn(h) if fn is not None else h.nop()
                    if key is not None:
                        inst.then_inc(sems[key], n)

        @block.tensor
        def _(h):
            emit(h, S.q["t"])

        @block.vector
        def _(h):
            emit(h, S.q["v"])

        @block.scalar
        def _(h):
            emit(h, S.q["s"])

        @block.gpsimd
        def _(h):
            emit(h, S.q["g"])

        @block.sync
        def _(h):
            emit(h, S.q["y"])

    es.close()
    return nc


def _prep_inputs(inputs):
    f = np.ascontiguousarray
    w = {}
    w["Wq"] = f(inputs["L0_attn_Wq"])
    w["Wk"] = f(inputs["L0_attn_Wk"])
    w["Wv"] = f(inputs["L0_attn_Wv"])
    w["Wo"] = f(inputs["L0_attn_Wo"])
    w["rf"] = f(inputs["L0_attn_rf"])
    w["r0w"] = f(inputs["L0_ffn_router_w"])
    w["l0w1"] = f(inputs["L0_ffn_w1"])
    w["l0b1"] = f(inputs["L0_ffn_b1"])
    w["l0w2"] = f(inputs["L0_ffn_w2"])
    w["l0b2"] = f(inputs["L0_ffn_b2"])
    for m in ("m1", "m2"):
        w[m + "_inwT"] = f(inputs[f"L1_attn_{m}_in_w"].T)
        w[m + "_xprojT"] = f(inputs[f"L1_attn_{m}_xproj_w"].T)
        w[m + "_dtwT"] = f(inputs[f"L1_attn_{m}_dt_w"].T)
        w[m + "_outwT"] = f(inputs[f"L1_attn_{m}_out_w"].T)
        w[m + "_convw"] = f(inputs[f"L1_attn_{m}_conv_w"])
        w[m + "_convb"] = f(inputs[f"L1_attn_{m}_conv_b"])
        w[m + "_dtb"] = f(inputs[f"L1_attn_{m}_dt_b"])
        w[m + "_Alog"] = f(inputs[f"L1_attn_{m}_Alog"])
        w[m + "_D"] = f(inputs[f"L1_attn_{m}_D"])
    w["norm_g"] = f(inputs["L1_attn_norm_g"])
    w["norm_b"] = f(inputs["L1_attn_norm_b"])
    w["ffn_g"] = f(inputs["L1_attn_ffn_g"])
    w["ffn_b"] = f(inputs["L1_attn_ffn_b"])
    w["ff1T"] = f(inputs["L1_attn_ff1_w"].T)
    w["ff1b"] = f(inputs["L1_attn_ff1_b"])
    w["ff2T"] = f(inputs["L1_attn_ff2_w"].T)
    w["ff2b"] = f(inputs["L1_attn_ff2_b"])
    w["Wr"] = f(inputs["L1_ffn_Wr"])
    w["W1"] = f(inputs["L1_ffn_W1"])
    w["W2"] = f(inputs["L1_ffn_W2"])
    w["ones128"] = np.ones((128, 128), np.float32)
    rfp = np.zeros((128, 256), np.float32)
    rfp[0:64, 0:128] = inputs["L0_attn_rf"]
    rfp[64:128, 128:256] = inputs["L0_attn_rf"]
    w["rfpad"] = rfp
    w["onescol"] = np.ones((128, 1), np.float32)
    w["eye128"] = np.eye(128, dtype=np.float32)
    w["l0b1r"] = np.ascontiguousarray(
        inputs["L0_ffn_b1"].reshape(14, 16, 128).transpose(0, 2, 1))

    def cols(vec, n):
        return np.ascontiguousarray(np.asarray(vec).reshape(n, 128).T)

    w["nrmpack"] = np.concatenate([cols(inputs["L1_attn_norm_g"], 8),
                                   cols(inputs["L1_attn_norm_b"], 8),
                                   cols(inputs["L1_attn_ffn_g"], 8),
                                   cols(inputs["L1_attn_ffn_b"], 8)], axis=1)
    w["ffbpack"] = np.concatenate([cols(inputs["L1_attn_ff1_b"], 32),
                                   cols(inputs["L1_attn_ff2_b"], 8)], axis=1)
    for m in ("m1", "m2"):
        cw = np.asarray(inputs[f"L1_attn_{m}_conv_w"]).reshape(16, 128, 4)
        cwc = cw.transpose(1, 0, 2).reshape(128, 64)
        w[m + "_colpack"] = np.concatenate(
            [cwc, cols(inputs[f"L1_attn_{m}_conv_b"], 16),
             cols(inputs[f"L1_attn_{m}_dt_b"], 16),
             cols(inputs[f"L1_attn_{m}_D"], 16)], axis=1).astype(np.float32)
        al = np.asarray(inputs[f"L1_attn_{m}_Alog"]).reshape(16, 128, 16)
        w[m + "_Alogr"] = np.ascontiguousarray(
            al.transpose(1, 0, 2).reshape(128, 256)).astype(np.float32)
        dtw = np.zeros((128, 2048), np.float32)
        dtw[0:64, :] = np.asarray(inputs[f"L1_attn_{m}_dt_w"]).T
        w[m + "_dtw128"] = dtw
    w["eye16"] = np.eye(16, dtype=np.float32)
    w["cvals"] = np.tile(np.pad(np.array(_CONSTS, np.float32), (0, 16 - len(_CONSTS)))[None, :], (128, 1))
    sel = np.zeros((128, 2048), np.float32)
    for p in range(128):
        if (p % 32) < 16 and p < 96:
            r = p % 32
            sel[p, 128 * r:128 * (r + 1)] = 1.0
    w["sel128"] = sel
    return w


def kernel(**inputs):
    x = np.asarray(inputs["L0_attn_x"])
    w = _prep_inputs(inputs)
    nc = build_core_program()
    in_maps = []
    for b in range(8):
        m = dict(w)
        m["x_fm"] = np.ascontiguousarray(x[b].T)
        in_maps.append(m)
    res = run_bass_kernel_spmd(nc, in_maps, core_ids=list(range(8)))
    out = np.stack([np.ascontiguousarray(r["out_fm"].T) for r in res.results], axis=0)
    if _DEBUG:
        kernel.dbg = res.results
    return out


# version 05 (final)
# speedup vs baseline: 1.0995x; 1.0063x over previous
import math
import numpy as np
import ml_dtypes
from contextlib import ExitStack

import concourse.bass as bass
import concourse.mybir as mybir
from concourse.bass_utils import run_bass_kernel_spmd
from concourse.alu_op_type import AluOpType as OP

F32 = mybir.dt.float32
BF16 = mybir.dt.bfloat16
I32 = mybir.dt.int32
AF = mybir.ActivationFunctionType

D = 1024
T = 2048
NB = D // 128
TT = 512
NTT = T // TT
PI = math.pi
BF = ml_dtypes.bfloat16

_DEBUG = False
_CONSTS = [1e-05, PI / 2, 1.0, 0.25, -0.5, 2 * PI, -2 * PI, 0.5, 1.0 / 128.0]
RMAGIC = 12582912.0
_NPHASE = 4
_L0_STOP = ""

ENGS = ("t", "v", "s", "g", "y")


AUTO_SYNC = ("v", "s")


class Sched:

    def __init__(self):
        self.q = {e: [] for e in ENGS}
        self.cnt = {k: 0 for k in ("t", "v", "s", "g", "dg", "dy", "da")}
        self.wm = {e: dict.fromkeys(self.cnt, 0) for e in ENGS}
        self.pending_bar = {"dg": 0, "dy": 0, "da": 0}

    def raw(self, e, fn):
        if e in AUTO_SYNC:
            self.tok(e, fn)
            return None
        self.q[e].append(("op", fn, None, 0))
        return None

    def tok(self, e, fn=None, key=None):
        key = key or e
        if e in AUTO_SYNC and key == e:
            self.wait(e, (e, self.cnt[e]))
        self.cnt[key] += 1
        self.q[e].append(("op", fn, key, 1))
        return (key, self.cnt[key])

    def dma(self, e, out, in_, sem=None, **kw):
        key = sem or ("dg" if e == "g" else "dy")
        bar = self.pending_bar[key]
        if bar > self.wm[e][key]:
            self.wm[e][key] = bar
            self.q[e].append(("wait", key, bar))
        self.cnt[key] += 16

        def fn(h, out=out, in_=in_, kw=kw):
            return h.dma_start(out=out, in_=in_, **kw)

        self.q[e].append(("op", fn, key, 16))
        return (key, self.cnt[key])

    def wait(self, e, tok):
        if tok is None:
            return
        key, c = tok
        if key in ("dg", "dy", "da"):
            c = self.cnt[key]
            self.pending_bar[key] = max(self.pending_bar[key], c)
        if self.wm[e][key] >= c:
            return
        self.wm[e][key] = c
        self.q[e].append(("wait", key, c))

    def now(self, key):
        return (key, self.cnt[key])

    def sync_self(self, e):
        t = self.tok(e)
        self.wait(e, t)


def build_core_program():
    nc = bass.Bass("TRN2", target_bir_lowering=False)
    S = Sched()
    dram = {}

    def din(name, shape, dt=F32):
        dram[name] = nc.dram_tensor(name, list(shape), dt, kind="ExternalInput").ap()
        return dram[name]

    def dout(name, shape, dt=F32):
        dram[name] = nc.dram_tensor(name, list(shape), dt, kind="ExternalOutput").ap()
        return dram[name]

    x_in = din("x_fm", [D, T])
    out_fm = dout("out_fm", [D, T])

    for w in ("Wq", "Wk", "Wv", "Wo"):
        din(w, [D, D])
    din("rf", [64, 128])
    din("rfpad", [128, 256])
    din("r0w", [D, 14])
    din("l0w1", [14, D, 2048])
    din("l0b1", [14, 2048])
    din("l0b1r", [14, 128, 16])
    din("l0w2", [14, 2048, D])
    din("l0b2", [14, D])
    for m in ("m1", "m2"):
        din(m + "_inwT", [D, 4096])
        din(m + "_xprojT", [2048, 96])
        din(m + "_dtwT", [64, 2048])
        din(m + "_outwT", [2048, D])
        din(m + "_convw", [2048, 4])
        din(m + "_convb", [2048])
        din(m + "_dtb", [2048])
        din(m + "_Alog", [2048, 16])
        din(m + "_D", [2048])
    for v in ("norm_g", "norm_b", "ffn_g", "ffn_b"):
        din(v, [D])
    din("ff1T", [D, 4096])
    din("ff1b", [4096])
    din("ff2T", [4096, D])
    din("ff2b", [D])
    din("Wr", [D, 8])
    din("W1", [8, D, 4096])
    din("W2", [8, 4096, D])
    din("ones128", [128, 128])
    din("nrmpack", [128, 32])
    din("ffbpack", [128, 40])
    for m in ("m1", "m2"):
        din(m + "_colpack", [128, 112])
        din(m + "_Alogr", [128, 256])
        din(m + "_dtw128", [128, 2048])
    din("sel128", [128, 2048])
    din("cvals", [128, 16])
    din("onescol", [128, 1])
    din("eye128", [128, 128])
    din("eye16", [16, 16])
    dram["x2_spill"] = nc.dram_tensor("x2_spill", [D, T], F32, kind="Internal").ap()

    dbg_names = ("x1", "x2", "x3") if _DEBUG else ()
    for name in dbg_names:
        dout("dbg_" + name, [D, T])

    es = ExitStack()
    es.__enter__()

    def sbuf(name, shape, dt=F32):
        return es.enter_context(nc.sbuf_tensor(name, list(shape), dt))

    XA = sbuf("XA", [128, NB * T])
    XLNB = sbuf("XLNB", [128, NB * T], BF16)
    GB = sbuf("GB", [128, NB * T], BF16)
    AUX = sbuf("AUX", [128, 24576], BF16)

    ONESR = sbuf("ONESR", [128, 128])
    ONESCB = sbuf("ONESCB", [128, 1], BF16)
    ROWS = sbuf("ROWS", [1, 2 * T + 1024])
    STG = [sbuf(f"STG{i}", [128, TT]) for i in range(2)]
    STGB = [sbuf(f"STGB{i}", [128, TT], BF16) for i in range(2)]
    RFP = sbuf("RFP", [128, 256], BF16)
    CVALS = sbuf("CVALS", [128, 16])
    CVALSB = sbuf("CVALSB", [128, 16], BF16)
    SELW = sbuf("SELW", [16, T], BF16)
    EYEB = sbuf("EYEB", [16, 16], BF16)
    B1E = sbuf("B1E", [128, 16], BF16)
    XSH = AUX[:, 19968:20016]
    BCR = AUX[0:16, 20016:20528]
    DA = [AUX[:, 20528 + 256 * i:20784 + 256 * i] for i in range(2)]
    DB = AUX[:, 21040:21296]
    EYE128B = AUX[:, 21296:21424]
    FFB = AUX[:, 21424:21464]
    HH = AUX[:, 21504:22016].bitcast(F32)
    CARR = AUX[:, 22016:22528].bitcast(F32)
    NEGA = AUX[:, 22528:23040].bitcast(F32)
    MCOL = AUX[:, 23040:23264].bitcast(F32)
    NRM = AUX[:, 23264:23328].bitcast(F32)

    PS = [es.enter_context(nc.psum_tensor(f"PS{i}", [128, 512], F32)) for i in range(8)]

    sems = {k: es.enter_context(nc.semaphore("sem_" + k)) for k in S.cnt}

    def XT(i):
        return XA[:, T * i:T * (i + 1)]

    def XLN(i):
        return XLNB[:, T * i:T * (i + 1)]

    def ROW(r, sl=None):
        if sl is None:
            sl = slice(0, T)
        return ROWS[0:1, T * r + sl.start: T * r + sl.stop]

    def RT1(tt):
        return ROWS[0:1, 2 * T + TT * (tt % 2): 2 * T + TT * (tt % 2 + 1)]

    def act(out, in_, func, bias=0.0, scale=1.0, accum=None):
        def fn(h, o=out, i=in_, f=func, b=bias, sc=scale, a=accum):
            if a is None:
                return h.activation(out=o, in_=i, func=f, bias=b, scale=sc)
            return h.activation(out=o, in_=i, func=f, bias=b, scale=sc, accum_out=a)
        return fn

    def vtt(out, a, b, op):
        return lambda h, o=out, x=a, y=b, p=op: h.tensor_tensor(out=o, in0=x, in1=y, op=p)

    def vcopy(out, in_):
        return lambda h, o=out, i=in_: h.tensor_copy(out=o, in_=i)

    def vts(out, a, s1, s2=None, op0=OP.mult, op1=None):
        def fn(h, o=out, x=a, u=s1, w=s2, p0=op0, p1=op1):
            if p1 is None:
                return h.tensor_scalar(out=o, in0=x, scalar1=u, scalar2=None, op0=p0)
            return h.tensor_scalar(out=o, in0=x, scalar1=u, scalar2=w, op0=p0, op1=p1)
        return fn

    def vstt(out, a, s, b, op0, op1):
        return lambda h, o=out, x=a, u=s, y=b, p0=op0, p1=op1: h.scalar_tensor_tensor(
            out=o, in0=x, scalar=u, in1=y, op0=p0, op1=p1)

    def mm(psum_ap, lhsT, rhs, start, stop):
        return lambda h, o=psum_ap, l=lhsT, r=rhs, a=start, b=stop: h.matmul(
            o, l, r, start=a, stop=b)

    def load_w(dst, src, K=128):
        R, C = src.shape
        k = R // 128
        return S.dma("g", dst.rearrange("p (k c) -> p k c", k=k),
                     src.rearrange("(k p) c -> p k c", p=128))

    t_cval = S.dma("y", CVALS[:, :], dram["cvals"])
    t_cvalb = S.dma("g", CVALSB[:, :], dram["cvals"])
    for j, cv in enumerate(_CONSTS):
        nc.const_aps.register(F32, cv, CVALS[:, j:j + 1])
        nc.const_aps.register(BF16, cv, CVALSB[:, j:j + 1])
    for e in ENGS:
        S.wait(e, t_cval)
        S.wait(e, t_cvalb)
    t_ones = S.dma("y", ONESR[:, :], dram["ones128"])
    S.dma("g", ONESCB[:, :], dram["onescol"])
    t_rf = S.dma("g", RFP[:, :], dram["rfpad"])

    t_x = S.dma("y", XA.rearrange("p (i t) -> p i t", i=NB),
                x_in.rearrange("(i p) t -> p i t", p=128))
    XB0 = AUX[:, 0:NB * T]
    t_xb0 = S.dma("g", XB0.rearrange("p (i t) -> p i t", i=NB),
                  x_in.rearrange("(i p) t -> p i t", p=128))

    def ln_stats(xb_view, r_mu, r_inv, dep_tok, pa, pb):
        assert (r_mu, r_inv) == (0, 1)
        tq = None
        prev_mm = [None, None]
        for tt in range(NTT):
            sl = slice(TT * tt, TT * (tt + 1))
            for i in range(NB):
                S.wait("v", dep_tok)
                S.wait("v", prev_mm[i % 2])
                tk = S.tok("v", vtt(STGB[i % 2][:, :], xb_view(i)[:, sl],
                                    xb_view(i)[:, sl], OP.mult))
                S.wait("t", tk)
                S.raw("t", mm(PS[pa][0:1, :], ONESCB[:, :], xb_view(i)[:, sl],
                              i == 0, i == NB - 1))
                tq = S.tok("t", mm(PS[pb][0:1, :], ONESCB[:, :], STGB[i % 2][:, :],
                                   i == 0, i == NB - 1))
                prev_mm[i % 2] = tq
            S.wait("v", tq)
            S.raw("v", vts(ROW(0, sl), PS[pa][0:1, :], 1.0 / D))
            tv = S.tok("v", vts(ROW(1, sl), PS[pb][0:1, :], 1.0 / D))
        S.wait("v", tv)
        last = None
        for p2 in range(NTT // 2):
            c0, c1 = 2 * p2, 2 * p2 + 1
            sl0 = slice(TT * c0, TT * (c0 + 1))
            sl1 = slice(TT * c1, TT * (c1 + 1))
            if last is not None:
                S.wait("v", last)
            S.raw("v", vtt(RT1(c0), ROW(0, sl0), ROW(0, sl0), OP.mult))
            S.raw("v", vtt(RT1(c1), ROW(0, sl1), ROW(0, sl1), OP.mult))
            S.raw("v", vtt(STG[0][0:1, :], ROW(1, sl0), RT1(c0), OP.subtract))
            tb = S.tok("v", vtt(STG[1][0:1, :], ROW(1, sl1), RT1(c1), OP.subtract))
            S.wait("s", tb)
            S.raw("s", act(RT1(c0), STG[0][0:1, :], AF.Ln, bias=1e-5))
            S.raw("s", act(RT1(c1), STG[1][0:1, :], AF.Ln, bias=1e-5))
            S.raw("s", act(ROW(1, sl0), RT1(c0), AF.Exp, scale=-0.5))
            last = S.tok("s", act(ROW(1, sl1), RT1(c1), AF.Exp, scale=-0.5))
        return last

    def bcast_row(r, sl, bank, tok_dep):
        S.wait("t", tok_dep)
        S.wait("t", t_ones)
        return S.tok("t", mm(PS[bank][:, :], ONESR[0:1, :], ROW(r, sl), True, True))

    def ln_apply(xb_view, out_view, r_mu, r_inv, stat_tok, pa, pb):
        last = None
        te = None
        for tt in range(NTT):
            sl = slice(TT * tt, TT * (tt + 1))
            S.wait("t", te)
            tb1 = bcast_row(r_mu, sl, pa, stat_tok)
            tb2 = bcast_row(r_inv, sl, pb, stat_tok)
            S.wait("s", tb2)
            if last is not None:
                S.wait("s", last)
            S.raw("s", act(STG[0][:, :], PS[pa][:, :], AF.Copy))
            te = S.tok("s", act(STG[1][:, :], PS[pb][:, :], AF.Copy))
            for i in range(NB):
                S.wait("v", te)
                S.raw("v", vtt(STGB[i % 2][:, :], xb_view(i)[:, sl], STG[0][:, :], OP.subtract))
                if i > 0:
                    last = S.tok("v", vtt(out_view(i - 1)[:, sl], STGB[(i - 1) % 2][:, :],
                                          STG[1][:, :], OP.mult))
            last = S.tok("v", vtt(out_view(NB - 1)[:, sl], STGB[(NB - 1) % 2][:, :],
                                  STG[1][:, :], OP.mult))
        return last

    def fresh_xb(dst_region, dep_tok):
        S.wait("g", dep_tok)
        return S.dma("g", dst_region.rearrange("p (i t) -> p i t", i=NB),
                     XA.rearrange("p (i t) -> p i t", i=NB))

    def xb0_view(i):
        return XB0[:, T * i:T * (i + 1)]

    st1 = ln_stats(xb0_view, 0, 1, t_xb0, 0, 1)
    t_xln = ln_apply(xb0_view, XLN, 0, 1, st1, 2, 3)

    WK = AUX[:, 0:8192]
    WV = AUX[:, 8192:16384]
    WQ = AUX[:, 16384:24576]
    WO = AUX[:, 0:8192]
    S.wait("g", t_xln)
    t_wk = load_w(WK, dram["Wk"])
    t_wv = load_w(WV, dram["Wv"])
    t_wq = load_w(WQ, dram["Wq"])
    t_wo = None

    def wsl(arena, kb, ob):
        return arena[:, kb * D + 128 * ob: kb * D + 128 * (ob + 1)]

    KC = GB[:, 0:4096]
    VTC = GB[:, 4096:8192]
    KPT = GB[:, 8192:10240]
    KVX = GB[:, 14336:15488]
    QC = GB[:, 0:4096]
    QPG = GB[:, 8192:10240]
    ATTC = GB[:, 10240:14336]

    if _NPHASE >= 1:
        def trig_block(ps_bank, n_grp, out_ap_fn, part_halves=False):
            S.wait("v", S.now("s"))
            parts = []
            for half, (k_off, s_bias) in enumerate([(0.25, PI / 2), (0.0, 0.0)]):
                if part_halves:
                    src = ps_bank[64 * half:64 * (half + 1), 0:512]
                    w0 = STG[0][64 * half:64 * (half + 1), 0:512]
                    w1 = STG[1][64 * half:64 * (half + 1), 0:512]
                else:
                    src = ps_bank.rearrange("p (g f) -> p g f", f=128)[:, 0:n_grp, 64 * half:64 * (half + 1)]
                    w0 = STG[0].rearrange("p (g f) -> p g f", f=128)[:, 0:n_grp, 64 * half:64 * (half + 1)]
                    w1 = STG[1].rearrange("p (g f) -> p g f", f=128)[:, 0:n_grp, 64 * half:64 * (half + 1)]
                parts.append((src, w0, w1, out_ap_fn(half), s_bias, k_off))
            tr = None
            for src, w0, w1, oo, s_bias, k_off in parts:
                S.raw("v", vts(w0, src, 1.0 / (2 * PI), k_off, OP.mult, OP.add))
                S.raw("v", vts(w1, w0, RMAGIC, -RMAGIC, OP.add, OP.add))
                tr = S.tok("v", vstt(w0, w1, -2 * PI, src, OP.mult, OP.add))
            S.wait("s", tr)
            tok = None
            for src, w0, w1, oo, s_bias, k_off in parts:
                tok = S.tok("s", act(oo, w0, AF.Sin, bias=s_bias))
            return tok

        for tt in range(NTT):
            sl = slice(TT * tt, TT * (tt + 1))
            tke = None
            for ob in ([] if _L0_STOP in ("VnoK", "Vswap") else range(NB)):
                S.wait("t", t_wk)
                S.wait("t", t_xln)
                if tke is not None:
                    S.wait("t", tke)
                for kb in range(NB):
                    tk = S.tok("t", mm(PS[ob % 2][:, :], wsl(WK, kb, ob), XLN(kb)[:, sl],
                                       kb == 0, kb == NB - 1))
                S.wait("s", tk)
                tke = S.tok("s", act(KC[:, 512 * ob:512 * (ob + 1)], PS[ob % 2][:, :], AF.Copy))
            if _L0_STOP == "K":
                continue
            for sub in range(4):
                ssl = slice(TT * tt + 128 * sub, TT * tt + 128 * (sub + 1))
                S.wait("t", t_wv)
                for vh in range(2):
                    for kb in range(NB):
                        lhs_vt = wsl(WV, kb, 0) if _L0_STOP == "Vswap" else XLN(kb)[:, ssl]
                        tv = S.tok("t", mm(PS[2 + vh][:, :], lhs_vt,
                                           WV[:, kb * D + 512 * vh: kb * D + 512 * (vh + 1)],
                                           kb == 0, kb == NB - 1))
                S.wait("s", tv)
                if _L0_STOP == "Vnoevac":
                    tve = S.tok("s")
                else:
                    S.raw("s", act(VTC[:, 1024 * sub:1024 * sub + 512], PS[2][:, :], AF.Copy))
                    tve = S.tok("s", act(VTC[:, 1024 * sub + 512:1024 * (sub + 1)], PS[3][:, :], AF.Copy))
                if _L0_STOP in ("V", "Vnoevac", "VnoK", "Vswap"):
                    continue
                S.wait("t", tke)
                S.wait("t", t_rf)
                first_cs = (tt == 0 and sub == 0)
                last_cs = (tt == NTT - 1 and sub == 3)
                for g4 in range(4):
                    for j in range(4):
                        h = 4 * g4 + j
                        ob, half = h // 2, h % 2
                        lhs = KC[:, 512 * ob + 128 * sub: 512 * ob + 128 * (sub + 1)]
                        rf_t = RFP[:, 128 * half:128 * (half + 1)]
                        tp = S.tok("t", mm(PS[4][:, 128 * j:128 * (j + 1)], lhs, rf_t, True, True))
                    S.wait("v", tp)
                    tsin = trig_block(PS[4], 4,
                                      lambda half, g4=g4: KPT.rearrange("p (g f) -> p g f", f=128)
                                      [:, 4 * g4:4 * g4 + 4, 64 * half:64 * (half + 1)])
                    if _L0_STOP == "P":
                        continue
                    S.wait("t", tsin)
                    S.wait("t", tve)
                    for j in range(4):
                        h = 4 * g4 + j
                        bank, off = 5 + h // 7, 65 * (h % 7)
                        first = first_cs and (h % 7 == 0)
                        last = last_cs and (h in (6, 13, 15))
                        lhs = KPT[:, 128 * h:128 * (h + 1)]
                        S.raw("t", mm(PS[bank][:, off:off + 64], lhs,
                                      VTC[:, 1024 * sub + 64 * h: 1024 * sub + 64 * (h + 1)],
                                      first, False))
                        tkv = S.tok("t", mm(PS[bank][:, off + 64:off + 65], lhs,
                                            ONESCB[:, :], False, last))
        S.wait("g", S.now("t"))
        t_wo = load_w(WO, dram["Wo"])
        tkvx = None
        if _L0_STOP in ("", "B"):
            S.wait("s", tkv)
            for b, (lo, n) in enumerate([(0, 7), (7, 7), (14, 2)]):
                tkvx = S.tok("s", act(KVX[:, 65 * lo:65 * (lo + n)], PS[5 + b][:, 0:65 * n],
                                      AF.Copy, scale=1.0 / 128.0))

        if _L0_STOP == "B":
            globals()["_B_ONLY_TOK"] = tkvx
        if _L0_STOP == "":
            tx1 = None
            for tt in range(NTT):
                sl = slice(TT * tt, TT * (tt + 1))
                tqe = None
                for ob in range(NB):
                    S.wait("t", t_wq)
                    S.wait("t", tkvx)
                    S.wait("t", tx1)
                    if tqe is not None:
                        S.wait("t", tqe)
                    for kb in range(NB):
                        tk = S.tok("t", mm(PS[ob % 2][:, :], wsl(WQ, kb, ob), XLN(kb)[:, sl],
                                           kb == 0, kb == NB - 1))
                    S.wait("s", tk)
                    tqe = S.tok("s", act(QC[:, 512 * ob:512 * (ob + 1)], PS[ob % 2][:, :], AF.Copy))
                tae = None
                tx1_hist = []
                for g4 in range(4):
                    S.wait("t", tqe)
                    for j in range(4):
                        h = 4 * g4 + j
                        ob, half = h // 2, h % 2
                        rhs = QC[:, 512 * ob:512 * (ob + 1)]
                        rf_t = RFP[:, 128 * half:128 * (half + 1)]
                        tp = S.tok("t", mm(PS[2 + j][:, :], rf_t, rhs, True, True))
                    S.wait("v", tp)
                    tqp = None
                    for j in range(4):
                        tqp = trig_block(PS[2 + j], 1,
                                         lambda half, j=j: QPG[64 * half:64 * (half + 1),
                                                              512 * j:512 * (j + 1)],
                                         part_halves=True)
                    for j in range(4):
                        h = 4 * g4 + j
                        S.wait("t", tqp)
                        bank = 6 + (j % 2)
                        if tae is not None:
                            S.wait("t", tae)
                        tq = S.tok("t", mm(PS[bank][0:65, :], KVX[:, 65 * h:65 * (h + 1)],
                                           QPG[:, 512 * j:512 * (j + 1)], True, True))
                        S.wait("v", tq)
                        S.raw("v", vts(STG[0][64:65, :], PS[bank][64:65, :], 1e-6, None, OP.max))
                        S.sync_self("v")
                        tz = S.tok("v", lambda h_, b=bank: h_.reciprocal(
                            out=STG[1][64:65, :], in_=STG[0][64:65, :]))
                        S.wait("t", tz)
                        tzb = S.tok("t", mm(PS[j % 2][0:64, :], ONESR[64:65, 0:64],
                                            STG[1][64:65, :], True, True))
                        S.wait("s", tq)
                        S.wait("s", tz)
                        tse = S.tok("s", act(STGB[0][0:64, :], PS[bank][0:64, :], AF.Copy))
                        S.wait("v", tse)
                        S.wait("v", tzb)
                        ob, half = h // 2, h % 2
                        tae = S.tok("v", vtt(ATTC[64 * half:64 * (half + 1), 512 * ob:512 * (ob + 1)],
                                             STGB[0][0:64, :], PS[j % 2][0:64, :], OP.mult))
                for ob in range(NB):
                    S.wait("t", tae)
                    S.wait("t", t_wo)
                    if ob >= 2:
                        S.wait("t", tx1_hist[ob - 2])
                    for kb in range(NB):
                        tk = S.tok("t", mm(PS[2 + ob % 2][:, :], wsl(WO, kb, ob),
                                           ATTC[:, 512 * kb:512 * (kb + 1)], kb == 0, kb == NB - 1))
                    S.wait("v", tk)
                    S.wait("v", t_x)
                    tx1 = S.tok("v", vtt(XT(ob)[:, sl], XT(ob)[:, sl], PS[2 + ob % 2][:, :], OP.add))
                    tx1_hist.append(tx1)

            t_last = tx1
        if _L0_STOP != "":
            t_last = t_xln
    if _NPHASE == 0:
        t_last = t_xln

    if _NPHASE >= 2 and _L0_STOP == "":
        t_in = t_last

        t_eyeb = S.dma("g", EYEB[:, :], dram["eye16"])
        XB2 = GB[:, 0:NB * T]

        def xb2_view(i):
            return XB2[:, T * i:T * (i + 1)]

        S.wait("g", t_in)
        t_xb2 = S.dma("g", XB2.rearrange("p (i t) -> p i t", i=NB),
                      XA.rearrange("p (i t) -> p i t", i=NB))
        st2 = ln_stats(xb2_view, 0, 1, t_xb2, 0, 1)
        t_xln2 = ln_apply(xb2_view, XLN, 0, 1, st2, 2, 3)

        HR = GB[:, 8192:10240].bitcast(F32)
        R0WS = HR[:, 0:112]
        EYE = HR[:, 112:240]
        LTM = HR[:, 240:304]
        LTZ = HR[:, 304:368]
        RCOL = HR[:, 368:376]
        REXP = HR[:, 376:392]
        RSBN = HR[:, 392:406]
        GST = HR[:, 406:918]
        RRS = HR[0:1, 918:934]
        B2SV = GB[0:16, 0:1024]
        S.wait("y", t_xln2)
        t_r0 = S.dma("y", R0WS.rearrange("p (k c) -> p k c", k=NB),
                     dram["r0w"].rearrange("(k p) c -> p k c", p=128))
        t_eye = S.dma("y", EYE, dram["eye128"])
        S.wait("g", t_xln2)
        t_b2 = S.dma("g", B2SV[0:14, :], dram["l0b2"])
        t_b1all = None

        S.wait("t", t_r0)
        S.wait("t", S.now("v"))
        for kb in range(NB):
            trs = S.tok("t", mm(PS[5][0:14, 0:1], R0WS[:, 14 * kb:14 * (kb + 1)],
                                ONESR[:, 0:1], kb == 0, kb == NB - 1))
        S.wait("t", t_eye)
        S.wait("v", trs)
        S.raw("v", vcopy(GST[0:14, 0:1], PS[5][0:14, 0:1]))
        tev = S.tok("v", vcopy(GST[0:14, 1:15], EYE[0:14, 0:14]))
        S.wait("t", tev)
        trr = S.tok("t", mm(PS[6][0:1, 0:14], GST[0:14, 0:1], GST[0:14, 1:15], True, True))
        S.wait("v", trr)
        tn = S.tok("v", vts(RRS[0:1, 0:14], PS[6][0:1, 0:14], -1.0))
        S.wait("t", tn)
        trsb = S.tok("t", mm(PS[6][:, 0:14], ONESR[0:1, :], RRS[0:1, 0:14], True, True))
        S.wait("v", trsb)
        tn2 = S.tok("v", vcopy(RSBN, PS[6][:, 0:14]))

        t_selw = None
        memset_tok = S.tok("v", lambda h: h.memset(SELW[:, :], 0.0))
        for tt in range(NTT):
            sl = slice(TT * tt, TT * (tt + 1))
            S.wait("t", S.now("v"))
            for kb in range(NB):
                tg = S.tok("t", mm(PS[5][0:14, :], R0WS[:, 14 * kb:14 * (kb + 1)],
                                   XT(kb)[:, sl], kb == 0, kb == NB - 1))
            S.wait("v", tg)
            tgs = S.tok("v", vcopy(GST[0:14, :], PS[5][0:14, :]))
            S.wait("t", tgs)
            S.wait("t", st2)
            for tc in range(4):
                csl = slice(128 * tc, 128 * (tc + 1))
                S.raw("t", mm(PS[7][:, 16 * tc:16 * tc + 14],
                              GST[0:14, csl], EYE[0:14, 0:14], tc == 0, False))
                S.raw("t", mm(PS[7][:, 16 * tc + 14:16 * tc + 15],
                              ROW(0, slice(sl.start + csl.start, sl.start + csl.stop)),
                              EYE[0:1, 0:1], False, False))
                ttr = S.tok("t", mm(PS[7][:, 16 * tc + 15:16 * tc + 16],
                                    ROW(1, slice(sl.start + csl.start, sl.start + csl.stop)),
                                    EYE[0:1, 0:1], False, tc == 3))
            S.wait("v", ttr)
            S.wait("v", tn2)
            tswt = None
            for tc in range(4):
                o = 16 * tc
                S.wait("v", tswt)
                S.raw("v", vstt(LTM[:, 0:14], RSBN, PS[7][:, o + 14:o + 15],
                                PS[7][:, o:o + 14], OP.mult, OP.add))
                S.raw("v", vts(LTZ[:, 0:14], LTM[:, 0:14], PS[7][:, o + 15:o + 16], None, OP.mult))
                S.raw("v", lambda h, z=LTZ: h.tensor_reduce(
                    out=RCOL[:, 0:1], in_=z[:, 0:14], axis=mybir.AxisListType.X, op=OP.max))
                S.raw("v", vts(RCOL[:, 1:2], RCOL[:, 0:1], -1.0))
                tmask = S.tok("v", vts(LTM[:, 0:14], LTZ[:, 0:14], RCOL[:, 0:1], None, OP.is_ge))
                S.wait("s", tmask)
                texp = S.tok("s", act(REXP[:, 0:14], LTZ[:, 0:14], AF.Exp, bias=RCOL[:, 1:2]))
                S.wait("v", texp)
                S.raw("v", lambda h: h.tensor_reduce(
                    out=RCOL[:, 2:3], in_=REXP[:, 0:14], axis=mybir.AxisListType.X, op=OP.add))
                S.raw("v", lambda h: h.reciprocal(out=RCOL[:, 3:4], in_=RCOL[:, 2:3]))
                tswm = S.tok("v", vts(LTZ[:, 0:14], LTM[:, 0:14], RCOL[:, 3:4], None, OP.mult))
                S.wait("t", tswm)
                tswt = S.tok("t", mm(PS[4][0:14, 128 * tc:128 * (tc + 1)],
                                     LTZ[:, 0:14], EYE[:, :], tc == 0, tc == 3))
            S.wait("v", tswt)
            S.wait("v", memset_tok)
            t_selw = S.tok("v", vcopy(SELW[0:14, sl], PS[4][0:14, :]))

        S.wait("g", t_xln2)
        t_onesrow = S.dma("g", STGB[1][0:1, 0:128], dram["ones128"][0:1, :])

        tb2a_hist = [None, None]
        for tt in range(NTT):
            sl = slice(TT * tt, TT * (tt + 1))
            for ob in range(NB):
                S.wait("t", t_selw)
                S.wait("t", t_b2)
                S.wait("t", tb2a_hist[ob % 2])
                tb2m = S.tok("t", mm(PS[ob % 2][:, :], B2SV[0:14, 128 * ob:128 * (ob + 1)],
                                     SELW[0:14, sl], True, True))
                S.wait("v", tb2m)
                tb2a = S.tok("v", vtt(XT(ob)[:, sl], XT(ob)[:, sl], PS[ob % 2][:, :], OP.add))
                tb2a_hist[ob % 2] = tb2a
        S.wait("t", tb2a)

        W1S = AUX[:, 0:16384]
        W2A = AUX[:, 16384:24576]
        W2B = GB[:, 0:8192]
        HIDR = GB[:, 8192:16384]

        def w2sl(fb, ob):
            base = W2A if fb < 8 else W2B
            f = fb if fb < 8 else fb - 8
            return base[:, 1024 * f + 128 * ob: 1024 * f + 128 * (ob + 1)]

        t_w1 = t_w2 = None
        last_w1_use = last_w2_use = None
        hid_evac = [None, None]
        txa_hist = [None, None]
        t_mult = None
        tb2a_tok = tb2a
        for e in range(14):
            S.wait("g", last_w1_use)
            t_w1 = S.dma("g", W1S.rearrange("p (k c) -> p k c", k=NB),
                         dram["l0w1"][e].rearrange("(k p) c -> p k c", p=128))
            t_b1e = S.dma("g", B1E[:, :], dram["l0b1r"][e])
            S.wait("g", last_w2_use)
            if e == 0:
                S.wait("g", tb2a_tok)
                S.wait("g", t_selw)
            t_w2 = S.dma("g", W2A.rearrange("p (k c) -> p k c", k=8),
                         dram["l0w2"][e][0:1024].rearrange("(k p) c -> p k c", p=128))
            t_w2b = S.dma("g", W2B.rearrange("p (k c) -> p k c", k=8),
                          dram["l0w2"][e][1024:2048].rearrange("(k p) c -> p k c", p=128))
            for tt in range(NTT):
                sl = slice(TT * tt, TT * (tt + 1))
                S.wait("t", t_selw)
                S.wait("t", t_eyeb)
                trx = S.tok("t", mm(PS[5][0:1, :], EYEB[0:14, e:e + 1], SELW[0:14, sl],
                                    True, True))
                S.wait("s", trx)
                tsw = S.tok("s", act(STGB[0][0:1, :], PS[5][0:1, :], AF.Copy))
                S.wait("t", tsw)
                S.wait("t", t_onesrow)
                S.wait("t", t_mult)
                tswb = S.tok("t", mm(PS[4][:, :], STGB[1][0:1, 0:128],
                                     STGB[0][0:1, :], True, True))
                for fb in range(16):
                    S.wait("t", t_w1)
                    S.wait("t", hid_evac[fb % 2])
                    for kb in range(NB):
                        th = S.tok("t", mm(PS[fb % 2][:, :],
                                           W1S[:, 2048 * kb + 128 * fb: 2048 * kb + 128 * (fb + 1)],
                                           XLN(kb)[:, sl], kb == 0, kb == NB - 1))
                    S.wait("s", th)
                    S.wait("s", t_b1e)
                    S.wait("s", t_selw)
                    hid_evac[fb % 2] = S.tok("s", act(HIDR[:, 512 * fb:512 * (fb + 1)],
                                                      PS[fb % 2][:, :], AF.Gelu,
                                                      bias=B1E[:, fb:fb + 1]))
                for ob in range(NB):
                    S.wait("t", t_w2)
                    S.wait("t", t_w2b)
                    S.wait("t", hid_evac[1])
                    S.wait("t", txa_hist[ob % 2])
                    for fb in range(16):
                        tw = S.tok("t", mm(PS[2 + ob % 2][:, :], w2sl(fb, ob),
                                           HIDR[:, 512 * fb:512 * (fb + 1)],
                                           fb == 0, fb == 15))
                    S.wait("s", tw)
                    S.wait("s", t_mult)
                    tev2 = S.tok("s", act(STG[0][:, :], PS[2 + ob % 2][:, :], AF.Copy))
                    S.wait("v", tev2)
                    S.wait("v", tswb)
                    t_mult = S.tok("v", vtt(STG[1][:, :], STG[0][:, :], PS[4][:, :], OP.mult))
                    txa = S.tok("v", vtt(XT(ob)[:, sl], XT(ob)[:, sl], STG[1][:, :], OP.add))
                    txa_hist[ob % 2] = txa
                last_w1_use = S.now("t")
            last_w2_use = S.now("t")
        S.wait("t", txa)
        t_last = txa

    if _NPHASE >= 3 and _L0_STOP == "":
        t_in = t_last
        CH = 256
        NCH = T // CH

        XB3 = GB[:, 0:NB * T]

        def xb3_view(i):
            return XB3[:, T * i:T * (i + 1)]

        S.wait("g", t_in)
        t_xb3 = S.dma("g", XB3.rearrange("p (i t) -> p i t", i=NB),
                      XA.rearrange("p (i t) -> p i t", i=NB))
        st3 = ln_stats(xb3_view, 0, 1, t_xb3, 0, 1)
        t_xln3 = ln_apply(xb3_view, XLN, 0, 1, st3, 2, 3)

        S.wait("y", t_xln3)
        t_spill = S.dma("y", dram["x2_spill"].rearrange("(i p) t -> p i t", p=128),
                        XA.rearrange("p (i t) -> p i t", i=NB))
        XAB = XA[:, :].bitcast(BF16)

        st_in = ln_stats(XLN, 0, 1, t_xln3, 0, 1)

        t_ey128b = S.dma("g", EYE128B[:, :], dram["eye128"])
        t_bcsel = S.dma("g", SELW[:, :], dram["sel128"][0:16, :])
        t_nrm = S.dma("g", NRM[:, :], dram["nrmpack"])
        mz = S.tok("v", lambda h: h.memset(STGB[1][:, :], 0.0))

        XSR = GB[:, 0:4096]
        XSIL = GB[:, 4096:8192]
        SZR = GB[:, 8192:12288]
        DTH = GB[:, 12288:16384]
        OUT_W = AUX[:, 0:16384]
        XPJ = AUX[:, 16384:17920]
        DTW = AUX[:, 17920:19968]

        def xsr_t(db):
            return XSR[:, 256 * db:256 * (db + 1)]

        def xsil_t(db):
            return XSIL[:, 256 * db:256 * (db + 1)]

        def sz_t(db):
            return SZR[:, 256 * db:256 * (db + 1)]

        def dt_t(db):
            return DTH[:, 256 * db:256 * (db + 1)]

        def g1s_t(kb):
            return DTH[:, 256 * kb:256 * (kb + 1)]

        gsp = nc.dram_tensor("g1_spill", [D, T], BF16, kind="Internal").ap()

        def load_mamba_weights(mi, gate_tok):
            S.wait("g", gate_tok)
            S.wait("g", S.now("v"))
            S.wait("g", S.now("s"))
            S.dma("g", XAB.rearrange("p (k c) -> p k c", k=NB),
                  dram[mi + "_inwT"].rearrange("(k p) c -> p k c", p=128))
            S.dma("g", OUT_W.rearrange("p (k c) -> p k c", k=16),
                  dram[mi + "_outwT"].rearrange("(k p) c -> p k c", p=128))
            S.dma("g", XPJ.rearrange("p (k c) -> p k c", k=16),
                  dram[mi + "_xprojT"].rearrange("(k p) c -> p k c", p=128))
            S.dma("g", DTW[:, :], dram[mi + "_dtw128"])
            S.dma("g", MCOL[:, :], dram[mi + "_colpack"])
            S.wait("y", gate_tok)
            S.wait("y", S.now("v"))
            S.wait("y", S.now("s"))
            t_al = S.dma("y", STG[0][:, 0:256], dram[mi + "_Alogr"])
            S.wait("s", t_al)
            tn1 = S.tok("s", act(STG[1][:, 0:256], STG[0][:, 0:256], AF.Exp))
            S.wait("v", tn1)
            t_nega = S.tok("v", vts(NEGA[:, :], STG[1][:, 0:256], -1.0))
            return S.now("dg"), t_nega

        def emit_mamba(rev, t_w, t_nega):
            t_y_done = None
            t_sp_prev = None
            t_xi1 = None
            mzc = S.tok("v", lambda h: h.memset(CARR[:, :], 0.0))
            mzh = S.tok("v", lambda h: h.memset(XSH[:, :], 0.0))
            for c in range(NCH):
                isl = slice(CH * c, CH * (c + 1))
                osl = slice(T - CH * (c + 1), T - CH * c)
                rsl = osl if rev else isl

                def xsrc(kb):
                    a = XLN(kb)[:, rsl]
                    return a[:, ::-1] if rev else a

                S.wait("t", st_in)
                S.wait("t", t_y_done)
                S.raw("t", mm(PS[3][:, 0:256], ONESR[0:1, :], ROW(0, rsl), True, False))
                tbc = S.tok("t", mm(PS[3][:, 256:512], ONESR[0:1, :], ROW(1, rsl), False, True))
                mu_b = PS[3][:, 0:256][:, ::-1] if rev else PS[3][:, 0:256]
                iv_b = PS[3][:, 256:512][:, ::-1] if rev else PS[3][:, 256:512]
                th = None
                for kb in range(NB):
                    S.wait("v", tbc)
                    S.wait("v", t_xln3)
                    S.wait("v", t_sp_prev)
                    S.wait("v", t_nrm)
                    S.raw("v", vtt(STGB[0][:, 0:256], xsrc(kb), mu_b, OP.subtract))
                    S.raw("v", vtt(STGB[0][:, 0:256], STGB[0][:, 0:256], iv_b, OP.mult))
                    th = S.tok("v", vts(DTH[:, 256 * kb:256 * (kb + 1)], STGB[0][:, 0:256],
                                        NRM[:, kb:kb + 1], NRM[:, 8 + kb:9 + kb],
                                        OP.mult, OP.add))
                tev = {}
                for eb in range(32):
                    S.wait("t", th)
                    S.wait("t", t_w)
                    if eb >= 2:
                        S.wait("t", tev[eb - 2])
                    for kb in range(NB):
                        tm = S.tok("t", mm(PS[eb % 2][:, 0:256],
                                           XAB[:, 4096 * kb + 128 * eb:4096 * kb + 128 * (eb + 1)],
                                           DTH[:, 256 * kb:256 * (kb + 1)],
                                           kb == 0, kb == NB - 1))
                    S.wait("s", tm)
                    if eb < 16:
                        tev[eb] = S.tok("s", act(xsr_t(eb), PS[eb % 2][:, 0:256], AF.Copy))
                    else:
                        tev[eb] = S.tok("s", act(sz_t(eb - 16), PS[eb % 2][:, 0:256], AF.Silu))
                t_conv = None
                for db in range(16):
                    S.wait("v", tev[db])
                    S.wait("v", mzh)
                    S.wait("v", t_conv)
                    S.raw("v", vts(STG[0][:, 0:256], xsr_t(db), MCOL[:, 4 * db + 3:4 * db + 4]))
                    a, b = STG[0], STG[1]
                    for k in (2, 1, 0):
                        off = 3 - k
                        S.raw("v", vstt(b[:, off:256], xsr_t(db)[:, 0:256 - off],
                                        MCOL[:, 4 * db + k:4 * db + k + 1],
                                        a[:, off:256], OP.mult, OP.add))
                        S.raw("v", vstt(b[:, 0:off], XSH[:, 3 * db + 3 - off:3 * db + 3],
                                        MCOL[:, 4 * db + k:4 * db + k + 1],
                                        a[:, 0:off], OP.mult, OP.add))
                        a, b = b, a
                    tcv = S.tok("v", vcopy(XSH[:, 3 * db:3 * db + 3], xsr_t(db)[:, 253:256]))
                    S.wait("s", tcv)
                    t_conv = S.tok("s", act(xsil_t(db), a[:, 0:256], AF.Silu,
                                            bias=MCOL[:, 64 + db:65 + db]))
                S.wait("t", t_conv)
                for kb in range(16):
                    tp = S.tok("t", mm(PS[2][0:96, 0:256], XPJ[:, 96 * kb:96 * (kb + 1)],
                                       xsil_t(kb), kb == 0, kb == 15))
                S.wait("s", tp)
                S.wait("s", mz)
                tdbl = S.tok("s", act(STGB[1][0:96, 0:256], PS[2][0:96, 0:256], AF.Copy))
                t_dtev = {}
                for mb in range(16):
                    S.wait("t", tdbl)
                    if mb >= 2:
                        S.wait("t", t_dtev[mb - 2])
                    tdm = S.tok("t", mm(PS[mb % 2][:, 0:256], DTW[:, 128 * mb:128 * (mb + 1)],
                                        STGB[1][:, 0:256], True, True))
                    S.wait("s", tdm)
                    if mb >= 1:
                        S.wait("s", t_dtev[mb - 1])
                    S.raw("s", act(STG[0][:, 0:256], PS[mb % 2][:, 0:256], AF.Abs,
                                   bias=MCOL[:, 80 + mb:81 + mb]))
                    S.raw("s", act(STG[1][:, 0:256], STG[0][:, 0:256], AF.Exp, scale=-1.0))
                    S.raw("s", act(STG[0][:, 0:256], STG[1][:, 0:256], AF.Ln, bias=1.0))
                    tr1 = S.tok("s", act(STG[1][:, 0:256], PS[mb % 2][:, 0:256], AF.Relu,
                                         bias=MCOL[:, 80 + mb:81 + mb]))
                    S.wait("v", tr1)
                    t_dtev[mb] = S.tok("v", vtt(dt_t(mb), STG[0][:, 0:256],
                                                STG[1][:, 0:256], OP.add))
                t_dt_last = t_dtev[15]
                S.wait("t", t_dt_last)
                S.wait("t", t_ey128b)
                S.raw("t", mm(PS[2][0:16, 0:256], EYE128B[:, 64:80], STGB[1][:, 0:256],
                              True, False))
                trx = S.tok("t", mm(PS[2][0:16, 256:512], EYE128B[:, 80:96],
                                    STGB[1][:, 0:256], False, True))
                S.wait("s", trx)
                tbcr = S.tok("s", act(BCR[0:16, :], PS[2][0:16, :], AF.Copy))
                t_da = [None, None]
                t_scan = [None, None]
                t_hc_done = None
                for sg in range(2):
                    S.wait("t", tbcr)
                    S.wait("t", t_bcsel)
                    S.wait("t", t_hc_done)
                    tsb = None
                    for j in range(8):
                        s_ = 8 * sg + j
                        bank, half = j // 2, j % 2
                        S.raw("t", mm(PS[bank][:, 256 * half:256 * (half + 1)],
                                      SELW[0:16, 128 * s_:128 * (s_ + 1)],
                                      BCR[0:16, 0:256], half == 0, half == 1))
                        tsb = S.tok("t", mm(PS[4 + bank][:, 256 * half:256 * (half + 1)],
                                            SELW[0:16, 128 * s_:128 * (s_ + 1)],
                                            BCR[0:16, 256:512], half == 0, half == 1))
                    for db in range(16):
                        S.wait("v", tsb)
                        S.wait("v", t_dt_last)
                        S.wait("v", t_nega)
                        S.raw("v", vtt(STGB[0][:, 0:256], dt_t(db), xsil_t(db), OP.mult))
                        for j in range(8):
                            s_ = 8 * sg + j
                            bank, half = j // 2, j % 2
                            bb = PS[bank][:, 256 * half:256 * (half + 1)]
                            cb = PS[4 + bank][:, 256 * half:256 * (half + 1)]
                            par = j % 2
                            S.wait("s", t_scan[par])
                            t_da[par] = S.tok("s", act(DA[par][:, 0:256], dt_t(db), AF.Exp,
                                                       scale=NEGA[:, 16 * db + s_:16 * db + s_ + 1]))
                            S.wait("v", t_da[par])
                            S.raw("v", vtt(DB[:, 0:256], STGB[0][:, 0:256], bb, OP.mult))
                            S.wait("v", mzc)
                            S.raw("v", lambda h, p=par, ix=16 * db + s_: h.tensor_tensor_scan(
                                out=HH[:, 0:256], data0=DA[p][:, 0:256], data1=DB[:, 0:256],
                                initial=CARR[:, ix:ix + 1], op0=OP.mult, op1=OP.add))
                            S.raw("v", vcopy(CARR[:, 16 * db + s_:16 * db + s_ + 1],
                                             HH[:, 255:256]))
                            t_scan[par] = S.tok("v", vtt(STGB[0][:, 256:512], HH[:, 0:256],
                                                         cb, OP.mult))
                            if j == 0:
                                S.raw("v", vcopy(STG[1][:, 0:256].bitcast(BF16)[:, 0:256],
                                                 STGB[0][:, 256:512]))
                            else:
                                S.raw("v", vtt(STG[1][:, 0:256].bitcast(BF16)[:, 0:256],
                                               STG[1][:, 0:256].bitcast(BF16)[:, 0:256],
                                               STGB[0][:, 256:512], OP.add))
                        if sg == 0:
                            t_hc_done = S.tok("v", vcopy(xsr_t(db), STG[1][:, 0:256].bitcast(BF16)[:, 0:256]))
                        else:
                            t_hc_done = S.tok("v", vtt(xsr_t(db), xsr_t(db),
                                                       STG[1][:, 0:256].bitcast(BF16)[:, 0:256], OP.add))
                ty2 = None
                for db in range(16):
                    S.wait("v", t_hc_done)
                    S.raw("v", vstt(STG[0][:, 0:256], xsil_t(db),
                                    MCOL[:, 96 + db:97 + db], xsr_t(db), OP.mult, OP.add))
                    ty2 = S.tok("v", vtt(xsr_t(db), STG[0][:, 0:256], sz_t(db), OP.mult))
                if rev:
                    S.wait("g", ty2)
                    tg1l = S.dma("g", DTH[:, 0:2048].rearrange("p (k t) -> p k t", k=NB),
                                 gsp.rearrange("(k p) t -> p k t", p=128)[:, :, osl])
                t_go = None
                for mb2 in range(NB):
                    S.wait("t", ty2)
                    if t_go is not None:
                        S.wait("t", t_go)
                    for db in range(16):
                        tom = S.tok("t", mm(PS[mb2 % 2][:, 0:256],
                                            OUT_W[:, 1024 * db + 128 * mb2:1024 * db + 128 * (mb2 + 1)],
                                            xsr_t(db), db == 0, db == 15))
                    S.wait("v", tom)
                    if not rev:
                        S.wait("v", t_xln3)
                        t_go = S.tok("v", vtt(g1s_t(mb2), XLN(mb2)[:, isl],
                                              PS[mb2 % 2][:, 0:256], OP.add))
                    else:
                        S.wait("v", tg1l)
                        tga = S.tok("v", vtt(STG[0][:, 0:256], g1s_t(mb2),
                                             PS[mb2 % 2][:, 0:256][:, ::-1], OP.add))
                        S.wait("s", tga)
                        tge = S.tok("s", act(STG[1][:, 0:256], STG[0][:, 0:256], AF.Gelu))
                        S.wait("v", tge)
                        t_go = S.tok("v", vtt(XLN(mb2)[:, osl], XLN(mb2)[:, osl],
                                              STG[1][:, 0:256], OP.add))
                        t_xi1 = t_go
                if not rev:
                    S.wait("g", t_go)
                    t_sp_prev = S.dma("g", gsp.rearrange("(k p) t -> p k t", p=128)[:, :, isl],
                                      DTH[:, 0:2048].rearrange("p (k t) -> p k t", k=NB))
                t_y_done = t_go
            return t_xi1, S.now("t")

        tw1, tn1 = load_mamba_weights("m1", t_spill)
        _xx, t_m1t = emit_mamba(False, tw1, tn1)
        tw2, tn2_ = load_mamba_weights("m2", t_m1t)
        t_xi1, t_m2t = emit_mamba(True, tw2, tn2_)

        st4 = ln_stats(XLN, 0, 1, t_xi1, 0, 1)
        XFF = GB[:, 0:NB * T]

        def xff_t(kb):
            return XFF[:, 2048 * kb:2048 * (kb + 1)]

        t_xff = None
        for tt in range(NTT):
            sl = slice(TT * tt, TT * (tt + 1))
            S.wait("t", S.now("v"))
            tb1 = bcast_row(0, sl, 2, st4)
            tb2 = bcast_row(1, sl, 3, st4)
            for kb in range(NB):
                S.wait("v", tb2)
                S.raw("v", vtt(STGB[0][:, :], XLN(kb)[:, sl], PS[2][:, :], OP.subtract))
                S.raw("v", vtt(STGB[0][:, :], STGB[0][:, :], PS[3][:, :], OP.mult))
                t_xff = S.tok("v", vts(xff_t(kb)[:, sl], STGB[0][:, :],
                                       NRM[:, 16 + kb:17 + kb], NRM[:, 24 + kb:25 + kb],
                                       OP.mult, OP.add))
        FF1Q = AUX[:, 0:8192]
        FF2Q = AUX[:, 8192:16384]
        HIDQ = XAB[:, 0:2048]
        ACC2 = XAB[:, 8192:24576]

        def acc2_t(ob):
            return ACC2[:, 2048 * ob:2048 * (ob + 1)]

        t_ff1b = S.dma("g", FFB[:, :], dram["ffbpack"])
        t_q = None
        t_acc = None
        for q in range(4):
            S.wait("g", t_m2t)
            if q > 0:
                S.wait("g", t_q)
            t_w1q = S.dma("g", FF1Q.rearrange("p (k c) -> p k c", k=NB),
                          dram["ff1T"][:, 1024 * q:1024 * (q + 1)].rearrange(
                              "(k p) c -> p k c", p=128))
            t_w2q = S.dma("g", FF2Q.rearrange("p (k c) -> p k c", k=NB),
                          dram["ff2T"][1024 * q:1024 * (q + 1), :].rearrange(
                              "(k p) c -> p k c", p=128))
            for tq in range(8):
                qsl = slice(256 * tq, 256 * (tq + 1))
                thev = {}
                for hb in range(8):
                    S.wait("t", t_w1q)
                    S.wait("t", t_xff)
                    S.wait("t", t_acc)
                    if hb >= 2:
                        S.wait("t", thev[hb - 2])
                    for kb in range(NB):
                        tm = S.tok("t", mm(PS[hb % 2][:, 0:256],
                                           FF1Q[:, 1024 * kb + 128 * hb:1024 * kb + 128 * (hb + 1)],
                                           xff_t(kb)[:, qsl], kb == 0, kb == NB - 1))
                    S.wait("s", tm)
                    S.wait("s", t_ff1b)
                    thev[hb] = S.tok("s", act(HIDQ[:, 256 * hb:256 * (hb + 1)],
                                              PS[hb % 2][:, 0:256], AF.Gelu,
                                              bias=FFB[:, 8 * q + hb:8 * q + hb + 1]))
                for ob in range(NB):
                    S.wait("t", t_w2q)
                    S.wait("t", thev[7])
                    if ob >= 2:
                        S.wait("t", t_acc)
                    for ht in range(8):
                        tm2 = S.tok("t", mm(PS[2 + ob % 2][:, 0:256],
                                            FF2Q[:, 1024 * ht + 128 * ob:1024 * ht + 128 * (ob + 1)],
                                            HIDQ[:, 256 * ht:256 * (ht + 1)],
                                            ht == 0, ht == 7))
                    if q == 0:
                        S.wait("s", tm2)
                        t_acc = S.tok("s", act(acc2_t(ob)[:, qsl], PS[2 + ob % 2][:, 0:256],
                                               AF.Copy))
                    else:
                        S.wait("v", tm2)
                        t_acc = S.tok("v", vtt(acc2_t(ob)[:, qsl], acc2_t(ob)[:, qsl],
                                               PS[2 + ob % 2][:, 0:256], OP.add))
            t_q = S.now("t")
        t_xi2 = None
        for ob in range(NB):
            S.wait("v", t_acc)
            t_xi2 = S.tok("v", vstt(XLN(ob)[:, :], acc2_t(ob), NRM[:, 28]
                          if False else FFB[:, 32 + ob:33 + ob], XLN(ob)[:, :],
                          OP.add, OP.add))
        S.wait("y", t_q)
        S.wait("y", t_xi2)
        t_rel = S.dma("y", XA.rearrange("p (i t) -> p i t", i=NB),
                      dram["x2_spill"].rearrange("(i p) t -> p i t", p=128))
        t_x3 = None
        for ob in range(NB):
            S.wait("v", t_rel)
            S.wait("v", t_xi2)
            t_x3 = S.tok("v", vtt(XT(ob), XT(ob), XLN(ob)[:, :], OP.add))
        t_last = t_x3

    if _NPHASE >= 4 and _L0_STOP == "":
        t_in = t_last

        XB4 = GB[:, 0:NB * T]

        def xb4_view(i):
            return XB4[:, T * i:T * (i + 1)]

        S.wait("g", t_in)
        t_xb4 = S.dma("g", XB4.rearrange("p (i t) -> p i t", i=NB),
                      XA.rearrange("p (i t) -> p i t", i=NB))
        st4b = ln_stats(xb4_view, 0, 1, t_xb4, 0, 1)
        t_xln4 = ln_apply(xb4_view, XLN, 0, 1, st4b, 2, 3)

        HR2 = GB[:, 8192:10240].bitcast(F32)
        WRS = HR2[:, 0:64]
        EYE2 = HR2[:, 64:192]
        LTM2 = HR2[:, 192:208]
        LTZ2 = HR2[:, 208:224]
        RC2 = HR2[:, 224:232]
        RE2 = HR2[:, 232:248]
        RSBN2 = HR2[:, 248:256]
        GST2 = HR2[:, 256:768]
        RRS2 = HR2[0:1, 768:784]
        S.wait("y", t_xln4)
        t_wr = S.dma("y", WRS.rearrange("p (k c) -> p k c", k=NB),
                     dram["Wr"].rearrange("(k p) c -> p k c", p=128))
        t_eye2 = S.dma("y", EYE2, dram["eye128"])
        S.wait("g", t_xln4)
        t_or = S.dma("g", STGB[1][0:1, 0:128], dram["ones128"][0:1, :])

        S.wait("t", t_wr)
        S.wait("t", S.now("v"))
        for kb in range(NB):
            trs = S.tok("t", mm(PS[5][0:8, 0:1], WRS[:, 8 * kb:8 * (kb + 1)],
                                ONESR[:, 0:1], kb == 0, kb == NB - 1))
        S.wait("t", t_eye2)
        S.wait("v", trs)
        S.raw("v", vcopy(GST2[0:8, 0:1], PS[5][0:8, 0:1]))
        tev4 = S.tok("v", vcopy(GST2[0:8, 1:9], EYE2[0:8, 0:8]))
        S.wait("t", tev4)
        trr = S.tok("t", mm(PS[6][0:1, 0:8], GST2[0:8, 0:1], GST2[0:8, 1:9], True, True))
        S.wait("v", trr)
        tn = S.tok("v", vts(RRS2[0:1, 0:8], PS[6][0:1, 0:8], -1.0))
        S.wait("t", tn)
        trsb = S.tok("t", mm(PS[6][:, 0:8], ONESR[0:1, :], RRS2[0:1, 0:8], True, True))
        S.wait("v", trsb)
        tn2b = S.tok("v", vcopy(RSBN2, PS[6][:, 0:8]))

        t_selw4 = None
        memset4 = S.tok("v", lambda h: h.memset(SELW[:, :], 0.0))
        for tt in range(NTT):
            sl = slice(TT * tt, TT * (tt + 1))
            S.wait("t", S.now("v"))
            for kb in range(NB):
                tg = S.tok("t", mm(PS[5][0:8, :], WRS[:, 8 * kb:8 * (kb + 1)],
                                   XT(kb)[:, sl], kb == 0, kb == NB - 1))
            S.wait("v", tg)
            tgs = S.tok("v", vcopy(GST2[0:8, :], PS[5][0:8, :]))
            S.wait("t", tgs)
            S.wait("t", st4b)
            for tc in range(4):
                csl = slice(128 * tc, 128 * (tc + 1))
                S.raw("t", mm(PS[7][:, 16 * tc:16 * tc + 8],
                              GST2[0:8, csl], EYE2[0:8, 0:8], tc == 0, False))
                S.raw("t", mm(PS[7][:, 16 * tc + 14:16 * tc + 15],
                              ROW(0, slice(sl.start + csl.start, sl.start + csl.stop)),
                              EYE2[0:1, 0:1], False, False))
                ttr = S.tok("t", mm(PS[7][:, 16 * tc + 15:16 * tc + 16],
                                    ROW(1, slice(sl.start + csl.start, sl.start + csl.stop)),
                                    EYE2[0:1, 0:1], False, tc == 3))
            S.wait("v", ttr)
            S.wait("v", tn2b)
            tswt = None
            for tc in range(4):
                o = 16 * tc
                S.wait("v", tswt)
                S.raw("v", vstt(LTM2[:, 0:8], RSBN2, PS[7][:, o + 14:o + 15],
                                PS[7][:, o:o + 8], OP.mult, OP.add))
                S.raw("v", vts(LTZ2[:, 0:8], LTM2[:, 0:8], PS[7][:, o + 15:o + 16], None, OP.mult))
                S.raw("v", lambda h: h.tensor_reduce(
                    out=RC2[:, 0:1], in_=LTZ2[:, 0:8], axis=mybir.AxisListType.X, op=OP.max))
                S.raw("v", vts(RC2[:, 1:2], RC2[:, 0:1], -1.0))
                S.raw("v", vts(LTM2[:, 0:8], LTZ2[:, 0:8], RC2[:, 0:1], None, OP.is_ge))
                S.raw("v", vts(LTM2[:, 8:16], LTM2[:, 0:8], -1e30, None, OP.mult))
                S.raw("v", vtt(LTM2[:, 8:16], LTZ2[:, 0:8], LTM2[:, 8:16], OP.add))
                S.raw("v", lambda h: h.tensor_reduce(
                    out=RC2[:, 4:5], in_=LTM2[:, 8:16], axis=mybir.AxisListType.X, op=OP.max))
                tmask = S.tok("v", vts(LTM2[:, 0:8], LTZ2[:, 0:8], RC2[:, 4:5], None, OP.is_ge))
                S.wait("s", tmask)
                texp = S.tok("s", act(RE2[:, 0:8], LTZ2[:, 0:8], AF.Exp, bias=RC2[:, 1:2]))
                S.wait("v", texp)
                S.raw("v", lambda h: h.tensor_reduce(
                    out=RC2[:, 2:3], in_=RE2[:, 0:8], axis=mybir.AxisListType.X, op=OP.add))
                S.raw("v", lambda h: h.reciprocal(out=RC2[:, 3:4], in_=RC2[:, 2:3]))
                S.raw("v", vts(RE2[:, 0:8], RE2[:, 0:8], RC2[:, 3:4], None, OP.mult))
                tswm = S.tok("v", vtt(LTZ2[:, 0:8], LTM2[:, 0:8], RE2[:, 0:8], OP.mult))
                S.wait("t", tswm)
                tswt = S.tok("t", mm(PS[4][0:8, 128 * tc:128 * (tc + 1)],
                                     LTZ2[:, 0:8], EYE2[:, :], tc == 0, tc == 3))
            S.wait("v", tswt)
            S.wait("v", memset4)
            t_selw4 = S.tok("v", vcopy(SELW[0:8, sl], PS[4][0:8, :]))

        CAP = float(globals().get('_CAP_OVERRIDE', 5120.0))
        cc_in = nc.dram_tensor("cc_in", [8, T], F32, kind="Internal").ap()
        cc_out = nc.dram_tensor("cc_out", [8, 8, T], F32, kind="Internal").ap()
        S.wait("g", t_selw4)
        t_ccw = S.dma("g", cc_in, SELW[0:8, :])
        S.wait("g", t_ccw)
        t_cc = S.tok("g", lambda h: h.collective_compute(
            "AllGather", OP.bypass, replica_groups=[list(range(8))],
            ins=[cc_in.opt()], outs=[cc_out.opt()]))
        GATES = GB[:, 0:2048].bitcast(F32)
        CBIS = GB[:, 2048:2176].bitcast(F32)
        LO = CBIS[:, 0:8]
        CAND = CBIS[:, 8:16]
        CNT = CBIS[:, 16:24]
        MSKC = GB[:, 2176:2304].bitcast(F32)
        S.wait("g", t_cc)
        for e in range(8):
            for r in range(8):
                t_g8 = S.dma("g", GATES[16 * r:16 * (r + 1), 128 * e:128 * (e + 1)],
                             cc_out[r, e, :].rearrange("(a f) -> a f", f=128))
        tz = S.tok("v", lambda h: h.memset(CBIS[:, 0:24], 0.0))
        t_bis = None
        for it in range(1, 51):
            step = 2.0 ** (-it)
            S.wait("v", t_g8)
            S.wait("v", tz)
            S.wait("v", t_bis)
            S.raw("v", vts(CAND, LO, step, None, OP.add))
            for e in range(8):
                S.raw("v", vts(MSKC[:, 0:64] if False else STG[0][:, 0:128],
                               GATES[:, 128 * e:128 * (e + 1)], CAND[:, e:e + 1],
                               None, OP.is_ge))
                S.raw("v", lambda h, e=e: h.tensor_reduce(
                    out=CNT[:, e:e + 1], in_=STG[0][:, 0:128],
                    axis=mybir.AxisListType.X, op=OP.add))
            tcn = S.tok("v")
            S.wait("t", tcn)
            tcm = S.tok("t", mm(PS[5][0:1, 0:8], ONESR[:, 0:1], CNT, True, True))
            S.wait("v", tcm)
            tsl = S.tok("v", vts(STG[0][0:1, 0:8], PS[5][0:1, 0:8], CAP, None, OP.is_ge))
            S.wait("t", tsl)
            tsb5 = S.tok("t", mm(PS[5][:, 8:16], ONESR[0:1, :], STG[0][0:1, 0:8],
                                 True, True))
            S.wait("v", tsb5)
            t_bis = S.tok("v", vstt(LO, PS[5][:, 8:16], step, LO, OP.mult, OP.add))
        S.wait("t", t_bis)
        S.wait("t", t_eye2)
        ttau = S.tok("t", mm(PS[5][0:8, 16:17], LO[0:1, 0:8], EYE2[0:1, 0:1], True, True))
        S.wait("v", ttau)
        MSK8 = GB[0:8, 4096:6144]
        S.raw("v", vts(MSK8, SELW[0:8, :], PS[5][0:8, 16:17], None, OP.is_ge))
        t_capm = S.tok("v", vtt(SELW[0:8, :], SELW[0:8, :], MSK8, OP.mult))
        t_selw4 = t_capm

        S.wait("y", t_in)
        t_init = S.dma("y", out_fm.rearrange("(i p) t -> p i t", p=128),
                       XA.rearrange("p (i t) -> p i t", i=NB))
        XAB4 = XA[:, :].bitcast(BF16)

        def w2t(j):
            if j < 24:
                return AUX[:, 1024 * j:1024 * (j + 1)]
            return GB[:, 8192 + 1024 * (j - 24):8192 + 1024 * (j - 23)]

        HID4 = GB[:, 0:8192]
        t_w1 = t_w2 = None
        last_w1_use = last_w2_use = None
        t_acc_tok = None
        last_da = None
        tgt_bank = {}
        tse4 = None
        for e in range(8):
            S.wait("g", t_init)
            S.wait("g", t_selw4)
            S.wait("g", S.now("t"))
            S.wait("g", last_w1_use)
            t_w1 = S.dma("g", XAB4.rearrange("p (k c) -> p k c", k=NB),
                         dram["W1"][e].rearrange("(k p) c -> p k c", p=128))
            S.wait("g", last_w2_use)
            S.wait("g", t_selw4 if e == 0 else None)
            t_w2 = S.dma("g", AUX[:, 0:24576].rearrange("p (k c) -> p k c", k=24),
                         dram["W2"][e][0:3072].rearrange("(k p) c -> p k c", p=128))
            t_w2b = S.dma("g", GB[:, 8192:16384].rearrange("p (k c) -> p k c", k=8),
                          dram["W2"][e][3072:4096].rearrange("(k p) c -> p k c", p=128))
            for tq in range(8):
                qsl = slice(256 * tq, 256 * (tq + 1))
                S.wait("t", t_selw4)
                S.wait("t", t_eyeb)
                S.wait("t", tse4)
                trx4 = S.tok("t", mm(PS[6][0:1, 0:256], EYEB[0:8, e:e + 1],
                                     SELW[0:8, qsl], True, True))
                S.wait("s", trx4)
                tsw4 = S.tok("s", act(STGB[0][0:1, 0:256], PS[6][0:1, 0:256], AF.Copy))
                S.wait("t", tsw4)
                S.wait("t", t_or)
                tswb4 = S.tok("t", mm(PS[6][:, 0:256], STGB[1][0:1, 0:128],
                                      STGB[0][0:1, 0:256], True, True))
                S.wait("s", tswb4)
                S.wait("s", t_acc_tok)
                tse4 = S.tok("s", act(STGB[1][:, 256:512], PS[6][:, 0:256], AF.Copy))
                thv = {}
                for hb in range(32):
                    S.wait("t", t_w1)
                    if hb >= 2:
                        S.wait("t", thv[hb - 2])
                    for kb in range(NB):
                        tm = S.tok("t", mm(PS[4 + hb % 2][:, 0:256],
                                           XAB4[:, 4096 * kb + 128 * hb:4096 * kb + 128 * (hb + 1)],
                                           XLN(kb)[:, qsl], kb == 0, kb == NB - 1))
                    S.wait("s", tm)
                    thv[hb] = S.tok("s", act(HID4[:, 256 * hb:256 * (hb + 1)],
                                             PS[4 + hb % 2][:, 0:256], AF.Relu))
                for ob in range(NB):
                    S.wait("t", t_w2)
                    S.wait("t", t_w2b)
                    S.wait("t", thv[31])
                    bank, half = ob // 2, ob % 2
                    S.wait("t", t_acc_tok if half == 1 else tgt_bank.get(bank))
                    for j in range(32):
                        tm2 = S.tok("t", mm(PS[bank][:, 256 * half:256 * (half + 1)],
                                            w2t(j)[:, 128 * ob:128 * (ob + 1)],
                                            HID4[:, 256 * j:256 * (j + 1)],
                                            j == 0, j == 31))
                    S.wait("v", tm2)
                    S.wait("v", tse4)
                    S.wait("v", last_da)
                    tgt = S.tok("v", vtt(STG[ob % 2][:, 0:256],
                                         PS[bank][:, 256 * half:256 * (half + 1)],
                                         STGB[1][:, 256:512], OP.mult))
                    S.wait("g", tgt)
                    last_da = S.dma("g", out_fm[128 * ob:128 * (ob + 1), qsl],
                                    STG[ob % 2][:, 0:256], sem="da", accum_op=OP.add)
                    t_acc_tok = tgt
                    if half == 1:
                        tgt_bank[bank] = tgt
                last_w1_use = S.now("t")
            last_w2_use = S.now("t")
        S.wait("y", S.now("dg"))
        t_last = None

    if _DEBUG:
        S.wait("y", t_last)
        t_dbg = S.dma("y", dram["dbg_x1"].rearrange("(i p) t -> p i t", p=128),
                      XA.rearrange("p (i t) -> p i t", i=NB))
        S.wait("y", t_dbg)

    if _NPHASE >= 4 and _L0_STOP == "":
        S.wait("g", S.now("da"))
        S.wait("g", S.now("dg"))
        S.wait("y", S.now("da"))
    else:
        S.wait("y", t_last)
        t_out = S.dma("y", out_fm.rearrange("(i p) t -> p i t", p=128),
                      XA.rearrange("p (i t) -> p i t", i=NB))
        S.wait("y", t_out)
        S.wait("g", S.now("dg"))

    with nc.Block() as block:
        def emit(h, lst):
            for kind, *rest in lst:
                if kind == "wait":
                    key, c = rest
                    h.wait_ge(sems[key], c)
                else:
                    fn, key, n = rest
                    inst = fn(h) if fn is not None else h.nop()
                    if key is not None:
                        inst.then_inc(sems[key], n)

        @block.tensor
        def _(h):
            emit(h, S.q["t"])

        @block.vector
        def _(h):
            emit(h, S.q["v"])

        @block.scalar
        def _(h):
            emit(h, S.q["s"])

        @block.gpsimd
        def _(h):
            emit(h, S.q["g"])

        @block.sync
        def _(h):
            emit(h, S.q["y"])

    es.close()
    return nc


def _prep_inputs(inputs):
    f = np.ascontiguousarray
    w = {}
    w["Wq"] = f(inputs["L0_attn_Wq"])
    w["Wk"] = f(inputs["L0_attn_Wk"])
    w["Wv"] = f(inputs["L0_attn_Wv"])
    w["Wo"] = f(inputs["L0_attn_Wo"])
    w["rf"] = f(inputs["L0_attn_rf"])
    w["r0w"] = f(inputs["L0_ffn_router_w"])
    w["l0w1"] = f(inputs["L0_ffn_w1"])
    w["l0b1"] = f(inputs["L0_ffn_b1"])
    w["l0w2"] = f(inputs["L0_ffn_w2"])
    w["l0b2"] = f(inputs["L0_ffn_b2"])
    for m in ("m1", "m2"):
        w[m + "_inwT"] = f(inputs[f"L1_attn_{m}_in_w"].T)
        w[m + "_xprojT"] = f(inputs[f"L1_attn_{m}_xproj_w"].T)
        w[m + "_dtwT"] = f(inputs[f"L1_attn_{m}_dt_w"].T)
        w[m + "_outwT"] = f(inputs[f"L1_attn_{m}_out_w"].T)
        w[m + "_convw"] = f(inputs[f"L1_attn_{m}_conv_w"])
        w[m + "_convb"] = f(inputs[f"L1_attn_{m}_conv_b"])
        w[m + "_dtb"] = f(inputs[f"L1_attn_{m}_dt_b"])
        w[m + "_Alog"] = f(inputs[f"L1_attn_{m}_Alog"])
        w[m + "_D"] = f(inputs[f"L1_attn_{m}_D"])
    w["norm_g"] = f(inputs["L1_attn_norm_g"])
    w["norm_b"] = f(inputs["L1_attn_norm_b"])
    w["ffn_g"] = f(inputs["L1_attn_ffn_g"])
    w["ffn_b"] = f(inputs["L1_attn_ffn_b"])
    w["ff1T"] = f(inputs["L1_attn_ff1_w"].T)
    w["ff1b"] = f(inputs["L1_attn_ff1_b"])
    w["ff2T"] = f(inputs["L1_attn_ff2_w"].T)
    w["ff2b"] = f(inputs["L1_attn_ff2_b"])
    w["Wr"] = f(inputs["L1_ffn_Wr"])
    w["W1"] = f(inputs["L1_ffn_W1"])
    w["W2"] = f(inputs["L1_ffn_W2"])
    w["ones128"] = np.ones((128, 128), np.float32)
    rfp = np.zeros((128, 256), np.float32)
    rfp[0:64, 0:128] = inputs["L0_attn_rf"]
    rfp[64:128, 128:256] = inputs["L0_attn_rf"]
    w["rfpad"] = rfp
    w["onescol"] = np.ones((128, 1), np.float32)
    w["eye128"] = np.eye(128, dtype=np.float32)
    w["l0b1r"] = np.ascontiguousarray(
        inputs["L0_ffn_b1"].reshape(14, 16, 128).transpose(0, 2, 1))

    def cols(vec, n):
        return np.ascontiguousarray(np.asarray(vec).reshape(n, 128).T)

    w["nrmpack"] = np.concatenate([cols(inputs["L1_attn_norm_g"], 8),
                                   cols(inputs["L1_attn_norm_b"], 8),
                                   cols(inputs["L1_attn_ffn_g"], 8),
                                   cols(inputs["L1_attn_ffn_b"], 8)], axis=1)
    w["ffbpack"] = np.concatenate([cols(inputs["L1_attn_ff1_b"], 32),
                                   cols(inputs["L1_attn_ff2_b"], 8)], axis=1)
    for m in ("m1", "m2"):
        cw = np.asarray(inputs[f"L1_attn_{m}_conv_w"]).reshape(16, 128, 4)
        cwc = cw.transpose(1, 0, 2).reshape(128, 64)
        w[m + "_colpack"] = np.concatenate(
            [cwc, cols(inputs[f"L1_attn_{m}_conv_b"], 16),
             cols(inputs[f"L1_attn_{m}_dt_b"], 16),
             cols(inputs[f"L1_attn_{m}_D"], 16)], axis=1).astype(np.float32)
        al = np.asarray(inputs[f"L1_attn_{m}_Alog"]).reshape(16, 128, 16)
        w[m + "_Alogr"] = np.ascontiguousarray(
            al.transpose(1, 0, 2).reshape(128, 256)).astype(np.float32)
        dtw = np.zeros((128, 2048), np.float32)
        dtw[0:64, :] = np.asarray(inputs[f"L1_attn_{m}_dt_w"]).T
        w[m + "_dtw128"] = dtw
    w["eye16"] = np.eye(16, dtype=np.float32)
    w["cvals"] = np.tile(np.pad(np.array(_CONSTS, np.float32), (0, 16 - len(_CONSTS)))[None, :], (128, 1))
    sel = np.zeros((128, 2048), np.float32)
    for p in range(128):
        if (p % 32) < 16 and p < 96:
            r = p % 32
            sel[p, 128 * r:128 * (r + 1)] = 1.0
    w["sel128"] = sel
    return w


def kernel(**inputs):
    x = np.asarray(inputs["L0_attn_x"])
    w = _prep_inputs(inputs)
    nc = build_core_program()
    in_maps = []
    for b in range(8):
        m = dict(w)
        m["x_fm"] = np.ascontiguousarray(x[b].T)
        in_maps.append(m)
    res = run_bass_kernel_spmd(nc, in_maps, core_ids=list(range(8)))
    out = np.stack([np.ascontiguousarray(r["out_fm"].T) for r in res.results], axis=0)
    if _DEBUG:
        kernel.dbg = res.results
    return out
